# version 06
# baseline (speedup 1.0000x reference)
import math
import numpy as np
import ml_dtypes
import concourse.bass as bass
import concourse.mybir as mybir
from concourse.bass_utils import run_bass_kernel_spmd

F32 = mybir.dt.float32
BF16 = mybir.dt.bfloat16
AT = mybir.AluOpType
AF = mybir.ActivationFunctionType

T, D, NT = 2048, 1024, 16
H, DH, KF = 16, 64, 32
KK = 100
NEG = -1.0e9
C23 = float(1.5 * 2.0 ** 23)
C52 = float(1.5 * 2.0 ** 52)
ZR = 92.0
NEWTON = 4
SQ2PI = float(math.sqrt(math.pi / 2.0))

ENGS = ("tensor", "vector", "scalar", "gpsimd", "sync")



def _riemann_zeros(K):
    n = np.arange(1, K + 1, dtype=np.float64)
    t = 10.0 + 6.0 * n
    for _ in range(60):
        f = t / (2 * np.pi) * np.log(t / (2 * np.pi * np.e)) - (n - 0.875)
        fp = np.log(t / (2 * np.pi)) / (2 * np.pi)
        t = t - f / fp
    return t


def _norm_ppf(p):
    a = [-3.969683028665376e+01, 2.209460984245205e+02, -2.759285104469687e+02,
         1.383577518672690e+02, -3.066479806614716e+01, 2.506628277459239e+00]
    b = [-5.447609879822406e+01, 1.615858368580409e+02, -1.556989798598866e+02,
         6.680131188771972e+01, -1.328068155288572e+01]
    cc = [-7.784894002430293e-03, -3.223964580411365e-01, -2.400758277161838e+00,
          -2.549732539343734e+00, 4.374664141464968e+00, 2.938163982698783e+00]
    dd = [7.784695709041462e-03, 3.224671290700398e-01, 2.445134137142996e+00,
          3.754408661907416e+00]
    p = np.asarray(p, dtype=np.float64)
    out = np.empty_like(p)
    pl = 0.02425
    lo = p < pl
    hi = p > 1 - pl
    mid = ~(lo | hi)
    q = np.sqrt(-2 * np.log(np.where(lo, p, 0.5)))
    out[lo] = ((((((cc[0] * q + cc[1]) * q + cc[2]) * q + cc[3]) * q + cc[4]) * q + cc[5]) /
               ((((dd[0] * q + dd[1]) * q + dd[2]) * q + dd[3]) * q + 1))[lo]
    q = np.sqrt(-2 * np.log(np.where(hi, 1 - p, 0.5)))
    out[hi] = -(((((((cc[0] * q + cc[1]) * q + cc[2]) * q + cc[3]) * q + cc[4]) * q + cc[5]) /
                ((((dd[0] * q + dd[1]) * q + dd[2]) * q + dd[3]) * q + 1)))[hi]
    q = p - 0.5
    r = q * q
    out[mid] = (((((((a[0] * r + a[1]) * r + a[2]) * r + a[3]) * r + a[4]) * r + a[5]) * q) /
                (((((b[0] * r + b[1]) * r + b[2]) * r + b[3]) * r + b[4]) * r + 1))[mid]
    return out


def host_constants():
    F = np.float32
    g = _riemann_zeros(KF)
    g = g / g[0]
    denom = 0.25 + g * g
    w_re = (0.5 / denom).astype(F)
    w_im = (-g / denom).astype(F)
    pos = np.arange(T, dtype=F)
    k_idx = np.arange(KF, dtype=F)
    inv_freq = 10000.0 ** (-k_idx / KF)
    phase = pos[:, None] * inv_freq[None, :]
    cos_p = np.cos(phase).astype(F)
    sin_p = np.sin(phase).astype(F)
    log1p = np.log1p(pos)
    log_decay = (0.5 * (log1p[None, :] - log1p[:, None])).astype(F)

    c = {}
    c["c_ident"] = np.eye(128, dtype=F)
    cos2 = np.concatenate([cos_p, cos_p], axis=1)
    sin2 = np.concatenate([sin_p, sin_p], axis=1)
    c["c_cos"] = np.ascontiguousarray(
        cos2.reshape(NT, 128, 64).transpose(1, 0, 2).reshape(128, NT * 64))
    c["c_sin"] = np.ascontiguousarray(
        sin2.reshape(NT, 128, 64).transpose(1, 0, 2).reshape(128, NT * 64))
    wr2 = np.concatenate([w_re / 8, w_re / 8]).astype(F)
    wi2 = np.concatenate([w_im / 8, w_im / 8]).astype(F)
    c["c_wr8"] = np.broadcast_to(wr2, (128, 64)).copy()
    c["c_wi8"] = np.broadcast_to(wi2, (128, 64)).copy()
    c["c_logdecay"] = log_decay
    lens = np.arange(1, T + 1, dtype=np.float64)
    frac = np.clip(ZR / np.maximum(lens, KK + 1.0), 1e-9, 1 - 1e-9)
    za = _norm_ppf(1 - frac)
    dens = np.maximum(np.exp(-0.5 * za * za) / math.sqrt(2 * math.pi) * lens, 1.0)
    zt = np.zeros((T, 8), dtype=F)
    zt[:, 0] = za
    zt[:, 1] = 1.0 / dens
    zt[:, 2] = (lens <= KK).astype(F)
    zt[:, 3] = (lens <= 160).astype(F)
    c["c_zst"] = np.ascontiguousarray(
        zt.reshape(NT, 128, 8).transpose(1, 0, 2).reshape(128, NT * 8))
    c["c_iota16"] = np.broadcast_to(np.arange(1, 17, dtype=F), (128, 16)).copy()
    ii = np.arange(128)
    c["c_tri"] = np.where(ii[:, None] >= ii[None, :], 0.0, NEG).astype(F)
    return c


def host_weights(W):
    out = {}
    def perm_qk(w):
        w = w.reshape(D, H // 2, 2, 2, KF)
        return None
    def perm(w):
        w4 = w.reshape(D, H // 2, 2, KF, 2)
        w4 = w4.transpose(0, 1, 2, 4, 3)
        return np.ascontiguousarray(w4.reshape(D, D))
    out["Wq"] = perm(W["L0_attn_a_Wq"])
    out["Wk"] = perm(W["L0_attn_a_Wk"])
    out["Wv"] = np.ascontiguousarray(W["L0_attn_a_Wv"])
    out["Wo"] = np.ascontiguousarray(W["L0_attn_a_Wo"])
    out["w_qkv"] = np.ascontiguousarray(W["L0_attn_b_w_qkv"])
    out["w_fc"] = np.ascontiguousarray(W["L0_attn_b_w_fc"])
    out["b_fc"] = np.ascontiguousarray(
        np.repeat(W["L0_attn_b_b_fc"].reshape(1, D), 128, axis=0))
    out["w_conv"] = np.ascontiguousarray(
        np.repeat(W["L0_attn_b_w_conv"][:, 0, :, 0], 64, axis=0))
    out["wg_t"] = np.ascontiguousarray(W["L0_ffn_w_gate"].T)
    out["wu_t"] = np.ascontiguousarray(W["L0_ffn_w_up"].T)
    out["wd_t"] = np.ascontiguousarray(W["L0_ffn_w_down"].T)
    return out



class Sem:
    __slots__ = ("h", "n")

    def __init__(self, nc, name):
        self.h = nc.alloc_semaphore(name)
        self.n = 0


class Ctx:
    def __init__(self, stage="full"):
        self.stage = stage
        self.nc = bass.Bass("TRN2", target_bir_lowering=False,
                            detect_race_conditions=False)
        self.q = {e: [] for e in ENGS}
        self.sm = {e: Sem(self.nc, "sm_" + e) for e in ENGS}
        self.qd = {"sync": Sem(self.nc, "q_sync"),
                   "gpsimd": Sem(self.nc, "q_gp"),
                   "scalar": Sem(self.nc, "q_act")}

    def op(self, eng, fn, waits=(), mark=False, small=False, fresh=False,
           nohz=False):
        if eng == "vector_s":
            eng, small = "vector", True
        for s, v in waits:
            assert v <= s.n, f"wait-on-future {v}>{s.n} eng={eng}"
        inc = None
        if mark:
            self.sm[eng].n += 1
            inc = (self.sm[eng], 1)
        self.q[eng].append((fn, list(waits), inc, small, fresh, nohz))
        return self.sm[eng].n

    def dma(self, eng, dst, src, waits=()):
        for s, v in waits:
            assert v <= s.n
        qs = self.qd[eng]
        qs.n += 16
        self.q[eng].append(
            (lambda e, d=dst, s_=src: e.dma_start(d, s_), list(waits),
             (qs, 16), False, False, True))
        return qs.n

    def wv(self, eng):
        return self.sm[eng].n

    def materialize(self):
        nc = self.nc
        hz = {e: Sem(nc, "hz_" + e) for e in ("vector", "gpsimd")}
        with nc.Block() as blk:
            for ename in ENGS:
                stream = self.q[ename]

                def run(e, stream=stream, ename=ename):
                    last = {}
                    hzs = hz.get(ename)
                    own = self.sm[ename]
                    hzc = 0
                    mkc = 0
                    lsync = None
                    n = len(stream)
                    for i, (fn, waits, inc, small, fresh, nohz) in enumerate(stream):
                        if small and hzs is not None and lsync is not None:
                            s, v = lsync
                            if last.get(id(s), -1) < v:
                                last[id(s)] = v
                                e.wait_ge(s.h, v)
                        for s, v in waits:
                            if not fresh and last.get(id(s), -1) >= v:
                                continue
                            last[id(s)] = max(last.get(id(s), -1), v)
                            e.wait_ge(s.h, v)
                        ins = fn(e)
                        if inc is not None:
                            ins.then_inc(inc[0].h, inc[1])
                            if inc[0] is own:
                                mkc += inc[1]
                                lsync = (inc[0], mkc)
                        elif hzs is not None and not nohz:
                            hzc += 1
                            ins.then_inc(hzs.h, 1)
                            lsync = (hzs, hzc)

                getattr(blk, ename)(run)
        return nc


def sbt(nc, name, shape, dtype, off):
    return nc.alloc_sbuf_tensor_at(name, shape, dtype, offset=off)



class Mem:
    def __init__(self, c):
        nc = c.nc
        d = {}
        d["x"] = nc.dram_tensor("x", [T, D], F32, kind="ExternalInput")
        for n, shp in [("Wq", [D, D]), ("Wk", [D, D]), ("Wv", [D, D]),
                       ("Wo", [D, D]), ("w_qkv", [D, 3 * D]), ("w_fc", [D, D]),
                       ("b_fc", [128, D]), ("wg_t", [D, 4096]), ("wu_t", [D, 4096]),
                       ("wd_t", [4096, D]), ("w_conv", [D, 13]),
                       ("c_ident", [128, 128]), ("c_cos", [128, NT * 64]),
                       ("c_sin", [128, NT * 64]), ("c_wr8", [128, 64]),
                       ("c_wi8", [128, 64]), ("c_logdecay", [T, T]),
                       ("c_zst", [128, NT * 8]), ("c_iota16", [128, 16]),
                       ("c_tri", [128, 128])]:
            d[n] = nc.dram_tensor(n, shp, F32, kind="ExternalInput")
        d["out"] = nc.dram_tensor("out", [T, D], F32, kind="ExternalOutput")
        d["dbg"] = nc.dram_tensor("dbg", [128, 8192], F32, kind="ExternalOutput")
        d["s_xlnT"] = nc.dram_tensor("s_xlnT", [D, T], F32)
        d["s_att"] = nc.dram_tensor("s_att", [D, T], F32)
        d["s_qg"] = nc.dram_tensor("s_qg", [D, 4096], BF16)
        d["s_qu"] = nc.dram_tensor("s_qu", [D, 4096], BF16)
        d["s_qd"] = nc.dram_tensor("s_qd", [4096, D], BF16)
        d["s_qkvb"] = nc.dram_tensor("s_qkvb", [3 * D, T], F32)
        d["s_outb"] = nc.dram_tensor("s_outb", [D, T], F32)
        d["s_hg"] = nc.dram_tensor("s_hg", [T, 4096], F32)
        d["s_xq"] = nc.dram_tensor("s_xq", [D, T], BF16)
        d["s_qhT"] = nc.dram_tensor("s_qhT", [4096, T], BF16)
        d["s_gmax"] = nc.dram_tensor("s_gmax", [1, 2], F32)
        d["s_gmax2"] = nc.dram_tensor("s_gmax2", [1, 2], F32)
        self.d = {k: v.ap() for k, v in d.items()}

        arena = nc.alloc_sbuf_tensor("arena_reserve", [128, 0x2F400], mybir.dt.uint8)
        base = None
        for al in nc.m.functions[0].allocations:
            if isinstance(al, mybir.MemoryLocationSet):
                for ml in al.memorylocations:
                    if ml.name.startswith("arena_reserve"):
                        base = ml.addr
        assert base is not None and base % 32 == 0, base
        self.base = base

        def sbt(nc_, name, shape, dtype, off, _b=base):
            return nc_.alloc_sbuf_tensor_at(name, shape, dtype, offset=_b + off)
        self.ident = sbt(nc, "ident", [128, 128], F32, 0x00000)
        self.cosT = sbt(nc, "cosT", [128, NT * 64], F32, 0x00200)
        self.sinT = sbt(nc, "sinT", [128, NT * 64], F32, 0x01200)
        self.wr8 = sbt(nc, "wr8", [128, 64], F32, 0x02200)
        self.wi8 = sbt(nc, "wi8", [128, 64], F32, 0x02300)
        self.zst = sbt(nc, "zst", [128, NT * 8], F32, 0x02400)
        self.iota16 = sbt(nc, "iota16", [128, 16], F32, 0x02600)
        self.tri = sbt(nc, "tri", [128, 128], F32, 0x03600)
        self.pic = sbt(nc, "pic", [128, 4], F32, 0x02640)
        self.cc = sbt(nc, "cc", [128, 108], F32, 0x02660)
        self.bfc = sbt(nc, "bfc", [128, D], F32, 0x02820)

        A = 0x03820
        self.A = A
        self.pQ = [sbt(nc, f"pQ{i}", [128, T], F32, A + i * 0x6000) for i in range(4)]
        self.pK = [sbt(nc, f"pK{i}", [128, T], F32, A + i * 0x6000 + 0x2000) for i in range(4)]
        self.pV = [sbt(nc, f"pV{i}", [128, T], F32, A + i * 0x6000 + 0x4000) for i in range(4)]
        self.X = sbt(nc, "X", [128, NT * 1024], F32, A)
        self.A2 = sbt(nc, "A2", [128, 8192], F32, A + 0x10000)

        WB = A + 0x18000
        self.WB = WB
        self.dcy = sbt(nc, "dcy", [128, T], F32, WB)
        self.st = [sbt(nc, "st0", [128, T], F32, WB + 0x2000),
                   sbt(nc, "st1", [128, T], F32, WB + 0x4000)]
        self.Eb = sbt(nc, "Eb", [128, T], F32, WB + 0x6000)
        self.Pb = sbt(nc, "Pb", [128, T], F32, WB + 0x8000)
        self.pT = sbt(nc, "pT", [128, T], F32, WB + 0xA000)
        self.msc = sbt(nc, "msc", [128, T], F32, WB + 0xC000)
        self.msc2 = sbt(nc, "msc2", [128, T], F32, WB + 0xE000)
        self.W64 = sbt(nc, "W64", [128, NT * 1024], F32, WB)
        self.xlnT = sbt(nc, "xlnT", [128, NT * 1024], F32, WB)

        TL = WB + 0x10000
        self.rtmp = sbt(nc, "rtmp", [128, 256], F32, TL)
        self.avst = sbt(nc, "avst", [128, 256], F32, TL + 0x400)
        self.wproj = sbt(nc, "wproj", [128, 3072], F32, TL + 0x800)
        self.wproj2 = self.wproj

        self.wq_sb = sbt(nc, "wq_sb", [128, 32768], BF16, WB)
        self.wqbf = sbt(nc, "wqbf", [128, 4096], BF16, WB + 0xC000)
        A2b = A + 0x10000
        self.f_h = sbt(nc, "f_h", [128, 4096], F32, A2b)
        self.qhT_bf = sbt(nc, "qhT_bf", [128, 4096], BF16, A2b + 0x4000)
        self.xqT_bf = sbt(nc, "xqT_bf", [128, 1024], BF16, A2b + 0x6000)
        self.f_stage = sbt(nc, "f_stage", [128, 1024], F32, A2b + 0x6800)

        self.bq = sbt(nc, "bq", [64, T], F32, WB)
        self.bk = sbt(nc, "bk", [64, T], F32, WB + 0x2000)
        self.bv = sbt(nc, "bv", [64, T], F32, WB + 0x4000)
        self.a1T = sbt(nc, "a1T", [128, 4096], F32, WB + 0x6000)
        self.a3T = sbt(nc, "a3T", [128, 4096], F32, WB + 0xA000)
        self.bs = sbt(nc, "bs", [128, T], F32, WB + 0xE000)
        self.b_a2 = sbt(nc, "b_a2", [128, 512], F32, A2b + 0x4000)
        self.b_z = sbt(nc, "b_z", [128, 512], F32, A2b + 0x4800)
        self.b_az = sbt(nc, "b_az", [128, 512], F32, A2b + 0x5000)
        self.b_t = sbt(nc, "b_t", [128, 512], F32, A2b + 0x5800)
        self.b_aT = sbt(nc, "b_aT", [128, 512], F32, A2b + 0x6000)
        self.b_azT = sbt(nc, "b_azT", [128, 512], F32, A2b + 0x6800)
        self.b_zT = sbt(nc, "b_zT", [128, 512], F32, A2b + 0x7000)
        self.bql = sbt(nc, "bql", [64, 256], F32, TL + 0x800)
        self.bkl = sbt(nc, "bkl", [64, 256], F32, TL + 0xC00)
        self.wcv = sbt(nc, "wcv", [64, 16], F32, TL + 0x1000)
        self.bY = sbt(nc, "bY", [128, 128], F32, TL + 0x1040)
        self.bZ2 = sbt(nc, "bZ2", [128, 128], F32, TL + 0x1240)

        self.psA = nc.alloc_psum_tensor("psA", [128, 2048], F32)
        self.psB = nc.alloc_psum_tensor("psB", [128, 2048], F32)


CC = dict(t=0, c=2, lo=4, hi=6, chi=8, tperf=10, hasperf=12, mu=14, sig=16,
          g=18, gn=20, pf=22, negt=24, sk=26, Z=28, craw=30, r=32,
          u1=34, u2=36, u3=38, lnS=40, lnQ=42, lnmu=44, lnvar=46, lnr=48,
          lnt1=50, lnt2=52, t16a=56, t16b=72, qv=88)


def ccs(m, name, k=2):
    base = CC[name]
    return m.cc[:, base:base + k]



def vsel(c, m, out, g, a, b):
    k = out.shape[1]
    u3 = ccs(m, "u3")[:, 0:k]
    c.op("vector_s", lambda e: e.tensor_tensor(u3, a, b, AT.subtract))
    c.op("vector_s", lambda e: e.tensor_tensor(u3, g, u3, AT.mult))
    c.op("vector_s", lambda e: e.tensor_tensor(out, b, u3, AT.add))


def emit_rstd(c, vcol, out, t1, t2):
    c.op("vector_s", lambda e: e.tensor_scalar(t2, vcol, 1e-5, None, AT.add))
    c.op("vector_s", lambda e: e.tensor_scalar(out, t2, 1.0, 0.5, AT.add, AT.mult))
    for _ in range(NEWTON):
        c.op("vector_s", lambda e: e.reciprocal(t1, out))
        c.op("vector_s", lambda e: e.tensor_tensor(t1, t2, t1, AT.mult))
        c.op("vector_s", lambda e: e.tensor_tensor(out, out, t1, AT.add))
        c.op("vector_s", lambda e: e.tensor_scalar(out, out, 0.5, None, AT.mult))
    return c.op("vector_s", lambda e: e.reciprocal(out, out), mark=True)



def emit_ln_transpose(c, m, src_getter):
    prev_spill = 0
    prev_ve = 0
    for t in range(NT):
        stage = m.msc[:, 0:1024]
        raw = m.msc2[:, 0:1024]
        ldw = src_getter(t, raw, [(c.qd["scalar"], prev_spill)] if prev_spill else [])
        c.op("vector", lambda e, r=raw, o=stage: e.scalar_tensor_tensor(
            o, r, 1.0, r, AT.bypass, AT.bypass, accum_out=ccs(m, "lnS", 1)),
            waits=ldw)
        c.op("vector", lambda e, r=raw, o=stage: e.scalar_tensor_tensor(
            o, r, 1.0, r, AT.bypass, AT.mult, accum_out=ccs(m, "lnQ", 1)))
        c.op("vector_s", lambda e: e.engine_nop())
        c.op("vector_s", lambda e: e.tensor_scalar(ccs(m, "lnmu", 1), ccs(m, "lnS", 1), 1.0 / D, None, AT.mult))
        c.op("vector_s", lambda e: e.tensor_scalar(ccs(m, "lnvar", 1), ccs(m, "lnQ", 1), 1.0 / D, None, AT.mult))
        c.op("vector_s", lambda e: e.tensor_tensor(ccs(m, "lnt1", 1), ccs(m, "lnmu", 1), ccs(m, "lnmu", 1), AT.mult))
        c.op("vector_s", lambda e: e.tensor_tensor(ccs(m, "lnvar", 1), ccs(m, "lnvar", 1), ccs(m, "lnt1", 1), AT.subtract))
        rm = emit_rstd(c, ccs(m, "lnvar", 1), ccs(m, "lnr", 1), ccs(m, "lnt1", 1), ccs(m, "lnt2", 1))
        vn = c.op("vector", lambda e, r=raw, o=stage: e.tensor_scalar(
            o, r, ccs(m, "lnmu", 1), ccs(m, "lnr", 1), AT.subtract, AT.mult),
            waits=[(c.sm["vector"], rm)], mark=True)
        tm = 0
        for b in range(8):
            tw = []
            if b == 0:
                tw = [(c.sm["vector"], vn)]
                if prev_ve:
                    tw.append((c.sm["scalar"], prev_ve))
            tm = c.op("tensor", lambda e, b=b, s=stage: e.transpose(
                m.psA[:, b * 128:(b + 1) * 128], s[:, b * 128:(b + 1) * 128], m.ident[:]),
                waits=tw, mark=(b == 7))
        c.op("scalar", lambda e, r=raw: e.activation(
            r[:, 0:512], m.psA[:, 0:512], AF.Copy, bias=0.0, scale=1.0),
            waits=[(c.sm["tensor"], tm)])
        ve = c.op("scalar", lambda e, r=raw: e.activation(
            r[:, 512:1024], m.psA[:, 512:1024], AF.Copy, bias=0.0, scale=1.0),
            mark=True)
        prev_ve = ve
        dst = m.d["s_xlnT"].rearrange("(cc p) t2 -> p cc t2", p=128)[:, :, t * 128:(t + 1) * 128]
        src = raw.rearrange("p (cc w) -> p cc w", w=128)
        prev_spill = c.dma("scalar", dst, src)



def emit_attn_a(c, m):
    d = m.d

    def src(t, raw, extra):
        w = c.dma("sync", raw, d["x"][t * 128:(t + 1) * 128, :], waits=extra)
        return [(c.qd["sync"], w)]
    emit_ln_transpose(c, m, src)
    ln_done_q = c.qd["scalar"].n

    for g in range(2):
        wx = c.dma("sync", m.xlnT[:].rearrange("p (cc t2) -> p cc t2", t2=T),
                   d["s_xlnT"].rearrange("(cc p) t2 -> p cc t2", p=128),
                   waits=[(c.qd["scalar"], c.qd["scalar"].n),
                          (c.sm["scalar"], c.wv("scalar")),
                          (c.sm["vector"], c.wv("vector")),
                          (c.sm["gpsimd"], c.wv("gpsimd")),
                          (c.sm["tensor"], c.wv("tensor"))])
        xw = [(c.qd["sync"], wx)]
        c.op("gpsimd", lambda e: e.engine_nop(),
             waits=[(c.qd["scalar"], c.qd["scalar"].n)], mark=True)
        for i in range(4):
            emit_project_pair(c, m, g * 4 + i, i, xw)
            if c.stage == "proj":
                return
        for qt in range(NT):
            emit_attn_a_qt(c, m, g, qt)
            if (c.stage == f"qt{qt}" or c.stage.startswith(f"probe{qt}s")
                    or c.stage.startswith(f"fin{qt}s")):
                return
    emit_attn_a_out(c, m)


def emit_project_pair(c, m, p, slot, xw):
    d = m.d
    wp = m.wproj
    ws = []
    for wi, wn in enumerate(("Wq", "Wk", "Wv")):
        dst = wp[:, wi * 1024:(wi + 1) * 1024].rearrange("q (k w) -> q k w", w=128)
        src = d[wn][:, p * 128:(p + 1) * 128].rearrange("(k q) w -> q k w", q=128)
        ws.append(c.dma("sync", dst, src,
                        waits=list(xw) + [(c.sm["tensor"], c.wv("tensor"))]))
    wld = [(c.qd["sync"], ws[-1])]
    outs = (m.pQ[slot], m.pK[slot], m.pV[slot])
    prev_ev = 0
    for t in range(NT):
        mmlast = 0
        for wi in range(3):
            for kc in range(8):
                tw = []
                if wi == 0 and kc == 0:
                    if t == 0:
                        tw += wld
                    if prev_ev:
                        tw.append((c.sm["scalar"], prev_ev))
                mmlast = c.op("tensor", lambda e, wi=wi, kc=kc, t=t: e.matmul(
                    m.psA[:, wi * 128:(wi + 1) * 128],
                    m.xlnT[:, kc * 2048 + t * 128: kc * 2048 + (t + 1) * 128],
                    wp[:, wi * 1024 + kc * 128: wi * 1024 + (kc + 1) * 128],
                    start=(kc == 0), stop=(kc == 7)),
                    waits=tw,
                    mark=(wi == 2 and kc == 7))
        ev = 0
        for wi in range(3):
            ev = c.op("scalar", lambda e, wi=wi, t=t, o=outs[wi]: e.activation(
                o[:, t * 128:(t + 1) * 128], m.psA[:, wi * 128:(wi + 1) * 128],
                AF.Copy, bias=0.0, scale=1.0),
                waits=[(c.sm["tensor"], mmlast)] if wi == 0 else (),
                mark=(wi == 2))
        prev_ev = ev
        qtile = m.pQ[slot][:, t * 128:(t + 1) * 128]
        ktile = m.pK[slot][:, t * 128:(t + 1) * 128]
        def hv(ap, lohalf):
            v3 = ap.rearrange("p (h c) -> p h c", h=2)
            return v3[:, :, 0:32] if lohalf else v3[:, :, 32:64]
        q_re = hv(qtile, True); q_im = hv(qtile, False)
        k_re = hv(ktile, True); k_im = hv(ktile, False)
        cosb = m.cosT[:, t * 64:(t + 1) * 64].rearrange("p (h c) -> p h c", h=2)
        sinb = m.sinT[:, t * 64:(t + 1) * 64].rearrange("p (h c) -> p h c", h=2)
        wr = m.wr8[:].rearrange("p (h c) -> p h c", h=2)
        wi = m.wi8[:].rearrange("p (h c) -> p h c", h=2)
        def h3(ap):
            return ap.rearrange("p (h c) -> p h c", h=2)
        s0 = h3(m.rtmp[:, 0:64]); s1 = h3(m.rtmp[:, 64:128])
        s2 = h3(m.rtmp[:, 128:192]); s3 = h3(m.rtmp[:, 192:256])
        a0 = h3(m.avst[:, 0:64]); a1 = h3(m.avst[:, 64:128])
        a2 = h3(m.avst[:, 128:192]); a3 = h3(m.avst[:, 192:256])
        aw = [(c.sm["scalar"], ev)]
        def vop(o, i0, i1, alu, w=()):
            return c.op("vector_s", lambda e, o=o, i0=i0, i1=i1, alu=alu: e.tensor_tensor(o, i0, i1, alu), waits=w)
        def gop(o, i0, i1, alu, w=(), mark=False):
            return c.op("gpsimd", lambda e, o=o, i0=i0, i1=i1, alu=alu: e.tensor_tensor(o, i0, i1, alu), waits=w, mark=mark, small=True)
        vop(s0, cosb, q_re, AT.mult, aw)
        vop(s2, sinb, q_im, AT.mult)
        vop(s0, s0, s2, AT.add)
        vop(s1, cosb, q_im, AT.mult)
        vop(s2, sinb, q_re, AT.mult)
        vop(s1, s1, s2, AT.subtract)
        vop(s2, wr, s0, AT.mult)
        vop(s3, wi, s1, AT.mult)
        vop(q_re, s2, s3, AT.subtract)
        vop(s2, wr, s1, AT.mult)
        vop(s3, wi, s0, AT.mult)
        vq = c.op("vector_s", lambda e, o=q_im, i0=s2, i1=s3: e.tensor_tensor(o, i0, i1, AT.add), mark=True)
        gop(a0, cosb, k_re, AT.mult, aw)
        gop(a1, sinb, k_im, AT.mult)
        gop(a2, cosb, k_im, AT.mult)
        gop(a3, sinb, k_re, AT.mult)
        gop(k_re, a0, a1, AT.add)
        vk = gop(k_im, a2, a3, AT.subtract, mark=True)
        for which, (buf, wmk, weng) in enumerate(((m.pQ[slot], vq, "vector"), (m.pK[slot], vk, "gpsimd"))):
            pslot = m.psB[:, 0:128] if which == 0 else m.psB[:, 1536:1664]
            tt = c.op("tensor", lambda e, b=buf, t=t, ps=pslot: e.transpose(
                ps, b[:, t * 128:(t + 1) * 128], m.ident[:]),
                waits=[(c.sm[weng], wmk), (c.sm["scalar"], c.wv("scalar"))],
                mark=True)
            c.op("scalar", lambda e, b=buf, t=t, ps=pslot: e.activation(
                b[:, t * 128:(t + 1) * 128], ps, AF.Copy, bias=0.0, scale=1.0),
                waits=[(c.sm["tensor"], tt)], mark=True)


PROBE_ENG = ["vector"] * 8
PROBE_DENS = 4


def _emit_probes(c, m, W, za, idens, losel):
    t_ = ccs(m, "t")
    cc_ = ccs(m, "c")
    lo = ccs(m, "lo")
    hi = ccs(m, "hi")
    chi = ccs(m, "chi")
    tp = ccs(m, "tperf")
    hp = ccs(m, "hasperf")
    mu = ccs(m, "mu")
    sg = ccs(m, "sig")
    gg = ccs(m, "g")
    gn = ccs(m, "gn")
    pf = ccs(m, "pf")
    nt_ = ccs(m, "negt")
    u1 = ccs(m, "u1")
    u2 = ccs(m, "u2")
    gw = [(c.sm["scalar"], c.wv("scalar")), (c.sm["vector"], c.wv("vector")),
          (c.sm["gpsimd"], c.wv("gpsimd"))]
    c.op("vector_s", lambda e: e.tensor_scalar(mu, mu, 1.0 / W, None, AT.mult), waits=gw)
    c.op("vector_s", lambda e: e.tensor_scalar(sg, sg, SQ2PI / W, None, AT.mult))
    c.op("vector_s", lambda e: e.tensor_scalar(u1, sg, -2.0, None, AT.mult))
    c.op("vector_s", lambda e: e.tensor_tensor(lo, mu, u1, AT.add))
    lsm = c.op("vector_s", lambda e: e.tensor_scalar(u1, lo, 300.0, -1.0, AT.add, AT.mult), mark=True)
    c.op("vector_s", lambda e: e.tensor_scalar(u1, u1, losel, None, AT.mult),
         waits=[(c.sm["vector"], lsm)])
    c.op("vector_s", lambda e: e.tensor_tensor(lo, lo, u1, AT.add))
    c.op("vector_s", lambda e: e.tensor_scalar(u1, sg, 12.0, None, AT.mult))
    c.op("vector_s", lambda e: e.tensor_tensor(hi, mu, u1, AT.add))
    c.op("vector_s", lambda e: e.memset(chi, -1e4))
    c.op("vector_s", lambda e: e.memset(tp, 0.0))
    hpm = c.op("vector_s", lambda e: e.memset(hp, 0.0), mark=True)
    c.op("vector_s", lambda e: e.tensor_scalar(u1, sg, za, None, AT.mult),
         waits=[(c.sm["vector"], hpm)])
    c.op("vector_s", lambda e: e.tensor_tensor(t_, mu, u1, AT.add))
    c.op("vector_s", lambda e: e.tensor_tensor(t_, t_, lo, AT.max))
    c.op("vector_s", lambda e: e.tensor_tensor(t_, t_, hi, AT.min))
    vp = c.op("vector_s", lambda e: e.tensor_scalar(nt_, t_, -1.0, None, AT.mult), mark=True)

    for j in range(len(PROBE_ENG)):
        eng = PROBE_ENG[j]
        if eng == "scalar":
            cl = 0
            for h01 in range(2):
                cl = c.op("scalar", lambda e, h=h01: e.activation(
                    m.msc[:, 0:W], m.st[h][:, 0:W], AF.Sign,
                    bias=m.cc[:, CC["negt"] + h:CC["negt"] + h + 1], scale=1.0,
                    accum_out=m.cc[:, CC["craw"] + h:CC["craw"] + h + 1]),
                    waits=[(c.sm["vector"], vp)], mark=(h01 == 1))
            vp2 = c.op("vector_s", lambda e: e.tensor_scalar(
                cc_, ccs(m, "craw"), float(W), 0.5, AT.add, AT.mult),
                waits=[(c.sm["scalar"], cl)], mark=True)
        else:
            dump = m.msc if eng != "vector" else m.msc2
            cl = 0
            for h01 in range(2):
                cl = c.op(eng, lambda e, h=h01, dp=dump: e.tensor_scalar(
                    dp[:, 0:W], m.st[h][:, 0:W],
                    m.cc[:, CC["t"] + h:CC["t"] + h + 1], None,
                    AT.is_ge, AT.add,
                    accum_out=m.cc[:, CC["c"] + h:CC["c"] + h + 1]),
                    waits=[(c.sm["vector"], vp)], mark=(h01 == 1))
            vp2 = c.op("vector_s", lambda e: e.engine_nop(),
                       waits=[(c.sm[eng], cl)], mark=True)
        c.op("vector_s", lambda e: e.tensor_scalar(gg, cc_, KK - 0.5, None, AT.is_ge))
        c.op("vector_s", lambda e: e.tensor_scalar(gn, gg, -1.0, 1.0, AT.mult, AT.add))
        vsel(c, m, lo, gg, t_, lo)
        vsel(c, m, hi, gn, t_, hi)
        vsel(c, m, chi, gn, cc_, chi)
        c.op("vector_s", lambda e: e.tensor_scalar(pf, cc_, float(KK), None, AT.is_equal))
        vsel(c, m, tp, pf, t_, tp)
        c.op("vector_s", lambda e: e.tensor_tensor(hp, hp, pf, AT.max))
        if j < len(PROBE_ENG) - 1:
            if j < PROBE_DENS - 1:
                c.op("vector_s", lambda e: e.tensor_scalar(u1, cc_, ZR, None, AT.subtract))
                idm = c.op("vector_s", lambda e: e.tensor_tensor(u1, u1, sg, AT.mult), mark=True)
                c.op("vector_s", lambda e: e.tensor_scalar(u1, u1, idens, None, AT.mult),
                     waits=[(c.sm["vector"], idm)])
                c.op("vector_s", lambda e: e.tensor_tensor(t_, t_, u1, AT.add))
                c.op("vector_s", lambda e: e.tensor_tensor(t_, t_, lo, AT.max))
                c.op("vector_s", lambda e: e.tensor_tensor(t_, t_, hi, AT.min))
            else:
                c.op("vector_s", lambda e: e.tensor_tensor(u1, hi, lo, AT.subtract))
                c.op("vector_s", lambda e: e.tensor_scalar(u1, u1, 0.62, None, AT.mult))
                c.op("vector_s", lambda e: e.tensor_tensor(t_, lo, u1, AT.add))
                c.op("vector_s", lambda e: e.tensor_scalar(u2, sg, 12.0, None, AT.mult))
                c.op("vector_s", lambda e: e.tensor_tensor(u2, mu, u2, AT.add))
                c.op("vector_s", lambda e: e.tensor_tensor(t_, t_, u2, AT.min))
            vp = c.op("vector_s", lambda e: e.tensor_scalar(nt_, t_, -1.0, None, AT.mult), mark=True)
        else:
            vp = c.op("vector_s", lambda e: e.engine_nop(), mark=True)



def emit_attn_a_qt(c, m, g, qt):
    d = m.d
    W = 128 * (qt + 1)
    zoff = qt * 8
    za = m.zst[:, zoff + 0:zoff + 1]
    idens = m.zst[:, zoff + 1:zoff + 2]
    ka = m.zst[:, zoff + 2:zoff + 3]
    losel = m.zst[:, zoff + 3:zoff + 4]
    wdc = c.dma("sync", m.msc2[:, 0:W], d["c_logdecay"][qt * 128:(qt + 1) * 128, 0:W],
                waits=[(c.sm["scalar"], c.wv("scalar")),
                       (c.sm["vector"], c.wv("vector")),
                       (c.sm["gpsimd"], c.wv("gpsimd")),
                       (c.sm["tensor"], c.wv("tensor"))])
    dce = c.op("scalar", lambda e: e.activation(m.dcy[:, 0:W], m.msc2[:, 0:W],
                                                AF.Exp, bias=0.0, scale=1.0),
               waits=[(c.qd["sync"], wdc),
                      (c.sm["gpsimd"], c.wv("gpsimd"))], mark=True)

    for slot in range(4):
        p = g * 4 + slot
        for h01 in range(2):
            lhs = m.pQ[slot][64 * h01:64 * (h01 + 1), qt * 128:(qt + 1) * 128]
            stt_ = m.st[h01]
            mml = 0
            for j0 in range(0, W, 512):
                j1 = min(j0 + 512, W)
                mml = c.op("tensor", lambda e, j0=j0, j1=j1, lhs=lhs, s=slot, h=h01: e.matmul(
                    m.psA[:, j0:j1], lhs, m.pK[s][64 * h:64 * (h + 1), j0:j1],
                    start=True, stop=True),
                    waits=[(c.sm["scalar"], c.wv("scalar"))] if j0 == 0 else (),
                    mark=(j1 == W))
            rawb = m.Eb if h01 == 0 else m.Pb
            evl = 0
            for j0 in range(0, W, 512):
                j1 = min(j0 + 512, W)
                evl = c.op("scalar", lambda e, j0=j0, j1=j1, rb=rawb: e.activation(
                    rb[:, j0:j1], m.psA[:, j0:j1], AF.Copy, bias=0.0, scale=1.0),
                    waits=[(c.sm["tensor"], mml)] if j0 == 0 else (),
                    mark=(j1 == W))
            c.op("vector", lambda e, o=stt_, rb=rawb, h=h01: e.scalar_tensor_tensor(
                o[:, 0:W], rb[:, 0:W], 1.0, m.dcy[:, 0:W], AT.mult, AT.mult,
                accum_out=m.cc[:, CC["mu"] + h:CC["mu"] + h + 1]),
                waits=[(c.sm["scalar"], evl), (c.sm["scalar"], dce)])
            vab = c.op("vector", lambda e, o=stt_, h=h01: e.reduce_sum(
                m.cc[:, CC["sig"] + h:CC["sig"] + h + 1], o[:, 0:W],
                axis=mybir.AxisListType.X, apply_absolute_value=True), mark=True)
            gpm = c.op("gpsimd", lambda e, o=stt_: e.tensor_tensor(
                o[:, qt * 128:(qt + 1) * 128], o[:, qt * 128:(qt + 1) * 128],
                m.tri[:], AT.add),
                waits=[(c.sm["vector"], vab)], mark=True)

        _emit_probes(c, m, W, za, idens, losel)

        if c.stage == f"probe{qt}s{slot}":
            return
        for h01 in range(2):
            _emit_finisher_av(c, m, W, slot, h01, ka)
        if c.stage == f"fin{qt}s{slot}":
            return
        dst = m.d["s_att"][p * 128:(p + 1) * 128, qt * 128:(qt + 1) * 128] \
            .rearrange("(h r) w -> r h w", h=2)
        c.last_av_dma = c.dma("scalar", dst,
                              m.avst[0:64, 0:256].rearrange("r (h w) -> r h w", h=2))


def _emit_finisher_av(c, m, W, slot, h01, ka):
    stt_ = m.st[h01]
    ebuf = m.Eb if h01 == 0 else m.Pb
    hicol = m.cc[:, CC["hi"] + h01:CC["hi"] + h01 + 1]
    skcol = m.cc[:, CC["sk"] + h01:CC["sk"] + h01 + 1]
    zcol = m.cc[:, CC["Z"] + h01:CC["Z"] + h01 + 1]
    rcol = m.cc[:, CC["r"] + h01:CC["r"] + h01 + 1]
    hpcol = m.cc[:, CC["hasperf"] + h01:CC["hasperf"] + h01 + 1]
    tpcol = m.cc[:, CC["tperf"] + h01:CC["tperf"] + h01 + 1]
    t16 = m.cc[:, CC["t16a"]:CC["t16a"] + 16]
    gpp = c.op("gpsimd", lambda e: e.tensor_scalar(
        m.msc[:, 0:W], stt_[:, 0:W], hicol, NEG, AT.is_ge, AT.mult),
        waits=[(c.sm["vector"], c.wv("vector"))], mark=True, small=True)
    gpp = c.op("gpsimd", lambda e: e.tensor_tensor(
        m.msc[:, 0:W], stt_[:, 0:W], m.msc[:, 0:W], AT.add), mark=True,
        small=True)
    c.op("vector_s", lambda e: e.max(out=t16[:, 0:8], in_=m.msc[:, 0:W]),
         waits=[(c.sm["gpsimd"], gpp)])
    c.op("vector_s", lambda e: e.match_replace(
        out=m.msc2[:, 0:W], in_to_replace=t16[:, 0:8],
        in_values=m.msc[:, 0:W], imm_value=NEG))
    c.op("vector_s", lambda e: e.max(out=t16[:, 8:16], in_=m.msc2[:, 0:W]))
    rnp = c.op("vector_s", lambda e: e.tensor_scalar(
        ccs(m, "r"), ccs(m, "chi"), float(KK), -1.0, AT.subtract, AT.mult), mark=True)
    c.op("vector_s", lambda e: e.scalar_tensor_tensor(
        m.msc2[:, 0:16], m.iota16[:], rcol, t16,
        AT.is_equal, AT.mult, accum_out=skcol),
        waits=[(c.sm["vector"], rnp)])
    vsel(c, m, skcol, hpcol, tpcol, skcol)
    c.op("vector_s", lambda e: e.memset(ccs(m, "u1", 1), -7e8))
    vsel(c, m, skcol, ka, ccs(m, "u1", 1), skcol)
    c.op("vector_s", lambda e: e.engine_nop(), mark=True)
    vfin = c.op("vector_s", lambda e: e.engine_nop(), mark=True)
    c.op("gpsimd", lambda e: e.tensor_scalar(
        m.msc2[:, 0:W], stt_[:, 0:W], skcol, NEG, AT.is_lt, AT.mult),
        waits=[(c.sm["vector"], vfin)], small=True)
    gpk = c.op("gpsimd", lambda e: e.tensor_tensor(
        stt_[:, 0:W], stt_[:, 0:W], m.msc2[:, 0:W], AT.add), mark=True,
        small=True)
    aex = c.op("scalar", lambda e: e.activation(
        ebuf[:, 0:W], stt_[:, 0:W], AF.Exp, bias=m.pic[:, 3:4], scale=1.0),
        waits=[(c.sm["gpsimd"], gpk)], mark=True)
    c.op("vector", lambda e: e.tensor_scalar(
        m.msc[:, 0:W], ebuf[:, 0:W], 0.0, None, AT.add, AT.add,
        accum_out=zcol),
        waits=[(c.sm["scalar"], aex)], mark=True)
    c.op("vector_s", lambda e: e.engine_nop())
    c.op("vector_s", lambda e: e.reciprocal(zcol, zcol), mark=True)
    vsc = c.wv("vector")
    asc = c.op("scalar", lambda e: e.activation(
        stt_[:, 0:W], ebuf[:, 0:W], AF.Copy, bias=0.0, scale=zcol),
        waits=[(c.sm["vector"], vsc)], mark=True)
    nblk = W // 128
    avps = m.psB[0:64, 512:640] if h01 == 0 else m.psB[0:64, 1024:1152]
    for b0 in range(0, nblk, 4):
        bN = min(b0 + 4, nblk)
        ps = m.psB[:, 0:512] if (b0 // 4) % 2 == 0 else m.psB[:, 1536:2048]
        tl = 0
        for b in range(b0, bN):
            tl = c.op("tensor", lambda e, b=b, b0=b0, ps=ps: e.transpose(
                ps[:, (b - b0) * 128:(b - b0 + 1) * 128],
                stt_[:, b * 128:(b + 1) * 128], m.ident[:]),
                waits=[(c.sm["scalar"], asc)] if b == b0 else (),
                mark=(b == bN - 1))
        el = c.op("scalar", lambda e, b0=b0, bN=bN, ps=ps: e.activation(
            m.pT[:, b0 * 128:bN * 128], ps[:, 0:(bN - b0) * 128],
            AF.Copy, bias=0.0, scale=1.0),
            waits=[(c.sm["tensor"], tl)], mark=True)
        for b in range(b0, bN):
            c.op("tensor", lambda e, b=b, ps=ps: e.matmul(
                avps, m.pV[slot][:, b * 128 + 64 * h01:b * 128 + 64 * (h01 + 1)],
                m.pT[:, b * 128:(b + 1) * 128],
                start=(b == 0), stop=(b == nblk - 1)),
                waits=[(c.sm["scalar"], el)], mark=(b == bN - 1))
    avm = c.wv("tensor")
    avw = [(c.sm["tensor"], avm)]
    if h01 == 0 and getattr(c, "last_av_dma", 0):
        avw.append((c.qd["scalar"], c.last_av_dma))
    c.op("scalar", lambda e: e.activation(
        m.avst[0:64, h01 * 128:(h01 + 1) * 128], avps, AF.Copy,
        bias=0.0, scale=1.0),
        waits=avw, mark=True)


def emit_attn_a_out(c, m):
    d = m.d
    wo = m.W64
    wl = c.dma("sync", wo[:, 0:8192].rearrange("q (k w) -> q k w", w=1024),
               d["Wo"].rearrange("(k q) w -> q k w", q=128),
               waits=[(c.sm["scalar"], c.wv("scalar")),
                      (c.sm["vector"], c.wv("vector")),
                      (c.sm["gpsimd"], c.wv("gpsimd")),
                      (c.sm["tensor"], c.wv("tensor")),
                      (c.qd["scalar"], c.qd["scalar"].n)])
    prev_vr = 0
    for t in range(NT):
        ps = m.psA[:, 0:1024] if t % 2 == 0 else m.psA[:, 1024:2048]
        wa = c.dma("sync", m.wproj[:, 0:1024].rearrange("q (k w) -> q k w", w=128),
                   d["s_att"].rearrange("(k q) t2 -> q k t2", q=128)[:, :, t * 128:(t + 1) * 128],
                   waits=[(c.qd["sync"], wl)] if t == 0
                   else [(c.sm["tensor"], c.wv("tensor"))])
        wxr = c.dma("sync", m.msc[:, 1024:2048],
                    d["x"][t * 128:(t + 1) * 128, :],
                    waits=[(c.sm["vector"], prev_vr)] if prev_vr else ())
        mml = 0
        for half in range(2):
            for kc in range(8):
                tw = []
                if half == 0 and kc == 0:
                    tw = [(c.qd["sync"], wa)]
                    if prev_vr:
                        tw.append((c.sm["vector"], prev_vr))
                mml = c.op("tensor", lambda e, kc=kc, half=half, ps=ps: e.matmul(
                    ps[:, half * 512:(half + 1) * 512],
                    m.wproj[:, kc * 128:(kc + 1) * 128],
                    wo[:, kc * 1024 + half * 512: kc * 1024 + (half + 1) * 512],
                    start=(kc == 0), stop=(kc == 7)),
                    waits=tw, mark=(half == 1 and kc == 7))
        vr = c.op("vector", lambda e, t=t, ps=ps: e.tensor_tensor(
            m.X[:, t * 1024:(t + 1) * 1024], m.msc[:, 1024:2048], ps, AT.add),
            waits=[(c.sm["tensor"], mml), (c.qd["sync"], wxr)], mark=True)
        prev_vr = vr
        if c.stage == "attn_a":
            c.dma("scalar", d["out"][t * 128:(t + 1) * 128, :],
                  m.X[:, t * 1024:(t + 1) * 1024],
                  waits=[(c.sm["vector"], vr)])
    if c.stage == "attn_a":
        fin = c.qd["scalar"].n
        c.op("sync", lambda e: e.nop(), waits=[(c.qd["scalar"], fin)])



WGI, WUI, WDI = 104, 105, 106
INV127 = 1.0 / 127.0


def _bar(c, dma=True):
    w = [(c.sm[e], c.wv(e)) for e in ("vector", "scalar", "gpsimd", "tensor")]
    if dma:
        w += [(c.qd[q], c.qd[q].n) for q in ("sync", "scalar", "gpsimd")]
    return w


def emit_wquant(c, m):
    d = m.d
    onesr = m.rtmp[0:1, 128:256]
    onesc = m.rtmp[:, 130:131]
    c.op("vector_s", lambda e: e.memset(m.rtmp[0:1, 128:256], 1.0), waits=_bar(c))
    c.op("vector_s", lambda e: e.memset(m.rtmp[:, 130:131], 1.0), mark=True)
    sA = m.W64[:, 0:4096]
    sB = m.W64[:, 4096:8192]
    sC = m.W64[:, 8192:12288]
    for wn, qn, rows, invcol in (("wg_t", "s_qg", 1024, WGI),
                                 ("wu_t", "s_qu", 1024, WUI),
                                 ("wd_t", "s_qd", 4096, WDI)):
        nt = rows // 128
        cols = 4096 if rows == 1024 else 1024
        prev = c.op("vector_s", lambda e: e.memset(m.cc[:, 0:2], 0.0), mark=True)
        for i in range(nt):
            wld = c.dma("sync", sA[:, 0:cols], d[wn][i * 128:(i + 1) * 128, :],
                        waits=[(c.sm["vector"], prev)])
            c.op("vector", lambda e, cols=cols: e.reduce_sum(
                m.cc[:, 1:2], sA[:, 0:cols], axis=mybir.AxisListType.X,
                apply_absolute_value=True),
                waits=[(c.qd["sync"], wld)])
            prev = c.op("vector_s", lambda e: e.tensor_tensor(
                m.cc[:, 0:1], m.cc[:, 0:1], m.cc[:, 1:2], AT.add), mark=True)
        mm1 = c.op("tensor", lambda e: e.matmul(
            m.psA[0:1, 0:1], m.cc[:, 0:1], onesc, start=True, stop=True),
            waits=[(c.sm["vector"], prev)], mark=True)
        c.op("vector_s", lambda e: e.tensor_scalar(
            m.cc[0:1, 2:3], m.psA[0:1, 0:1], float(2.0 ** -22), 1e-5,
            AT.mult, AT.max),
            waits=[(c.sm["tensor"], mm1)])
        c.op("vector_s", lambda e: e.reciprocal(m.cc[0:1, 3:4], m.cc[0:1, 2:3]))
        for _ in range(2):
            c.op("vector_s", lambda e: e.tensor_tensor(
                m.cc[0:1, 4:5], m.cc[0:1, 2:3], m.cc[0:1, 3:4], AT.mult))
            c.op("vector_s", lambda e: e.tensor_scalar(
                m.cc[0:1, 4:5], m.cc[0:1, 4:5], -1.0, 2.0, AT.mult, AT.add))
            c.op("vector_s", lambda e: e.tensor_tensor(
                m.cc[0:1, 3:4], m.cc[0:1, 3:4], m.cc[0:1, 4:5], AT.mult))
        vb = c.op("vector_s", lambda e: e.engine_nop(), mark=True)
        bc = c.op("tensor", lambda e: e.matmul(
            m.psA[:, 4:6], onesr, m.cc[0:1, 2:4], start=True, stop=True),
            waits=[(c.sm["vector"], vb)], mark=True)
        c.op("vector_s", lambda e, ic=invcol: e.tensor_scalar(
            m.cc[:, ic:ic + 1], m.psA[:, 4:5], 1.0, None, AT.mult),
            waits=[(c.sm["tensor"], bc)])
        bv = c.op("vector_s", lambda e: e.tensor_scalar(
            m.cc[:, 7:8], m.psA[:, 5:6], 1.0, None, AT.mult), mark=True)
        wq_dma = 0
        t1m = bv
        for i in range(nt):
            wld = c.dma("sync", sA[:, 0:cols], d[wn][i * 128:(i + 1) * 128, :],
                        waits=[(c.sm["vector"], t1m)])
            t1m = c.op("vector", lambda e, cols=cols: e.tensor_scalar(
                sB[:, 0:cols], sA[:, 0:cols], m.cc[:, 7:8], None, AT.mult),
                waits=[(c.qd["sync"], wld), (c.sm["vector"], bv)], mark=True)
            c.op("vector", lambda e, cols=cols: e.tensor_scalar(
                sC[:, 0:cols], sB[:, 0:cols], 0.5, None, AT.is_ge))
            c.op("vector", lambda e, cols=cols: e.tensor_scalar(
                sB[:, 0:cols], sB[:, 0:cols], -1.0, 0.5, AT.mult, AT.is_ge))
            c.op("vector", lambda e, cols=cols: e.tensor_tensor(
                sC[:, 0:cols], sC[:, 0:cols], sB[:, 0:cols], AT.subtract))
            cast = c.op("vector", lambda e, cols=cols: e.tensor_scalar(
                m.wqbf[:, 0:cols], sC[:, 0:cols], 1.0, None, AT.mult),
                waits=[(c.qd["scalar"], wq_dma)] if wq_dma else (), mark=True)
            wq_dma = c.dma("scalar", d[qn][i * 128:(i + 1) * 128, :],
                           m.wqbf[:, 0:cols], waits=[(c.sm["vector"], cast)])
    c.op("sync", lambda e: e.nop(), waits=[(c.qd["scalar"], c.qd["scalar"].n)])


def _ffn_scale_col(c, m, src_col, inv_col, out_col, waits=()):
    c.op("vector_s", lambda e: e.tensor_scalar(
        out_col, src_col, INV127, None, AT.mult), waits=waits)
    return c.op("vector_s", lambda e, ic=inv_col: e.tensor_tensor(
        out_col, out_col, m.cc[:, ic:ic + 1], AT.mult), mark=True)


def emit_ffn_f0(c, m):
    d = m.d
    stage = m.f_stage[:]
    pi = m.pic[:, 0:1]
    prev_dma = 0
    prev_ev = 0
    for t in range(NT):
        raw = m.X[:, t * 1024:(t + 1) * 1024]
        w0 = _bar(c) if t == 0 else [(c.sm["vector"], c.wv("vector"))]
        c.op("vector", lambda e, r=raw, o=stage: e.scalar_tensor_tensor(
            o, r, 1.0, r, AT.bypass, AT.bypass, accum_out=ccs(m, "lnS", 1)),
            waits=w0)
        c.op("vector", lambda e, r=raw, o=stage: e.scalar_tensor_tensor(
            o, r, 1.0, r, AT.bypass, AT.mult, accum_out=ccs(m, "lnQ", 1)))
        c.op("vector_s", lambda e: e.tensor_scalar(ccs(m, "lnmu", 1), ccs(m, "lnS", 1), 1.0 / D, None, AT.mult))
        c.op("vector_s", lambda e: e.tensor_scalar(ccs(m, "lnvar", 1), ccs(m, "lnQ", 1), 1.0 / D, None, AT.mult))
        c.op("vector_s", lambda e: e.tensor_tensor(ccs(m, "lnt1", 1), ccs(m, "lnmu", 1), ccs(m, "lnmu", 1), AT.mult))
        c.op("vector_s", lambda e: e.tensor_tensor(ccs(m, "lnvar", 1), ccs(m, "lnvar", 1), ccs(m, "lnt1", 1), AT.subtract))
        rm = emit_rstd(c, ccs(m, "lnvar", 1), ccs(m, "lnr", 1), ccs(m, "lnt1", 1), ccs(m, "lnt2", 1))
        vn = c.op("vector", lambda e, r=raw, o=stage: e.tensor_scalar(
            o, r, ccs(m, "lnmu", 1), ccs(m, "lnr", 1), AT.subtract, AT.mult),
            waits=[(c.sm["vector"], rm)], mark=True)
        c.op("vector_s", lambda e: e.reduce_max(
            m.cc[:, 0:1], stage, axis=mybir.AxisListType.X,
            apply_absolute_value=True))
        c.op("vector_s", lambda e: e.tensor_scalar(
            m.cc[:, 0:1], m.cc[:, 0:1], 1e-5, None, AT.max))
        c.op("vector_s", lambda e, t=t: e.tensor_scalar(
            m.rtmp[:, t:t + 1], m.cc[:, 0:1], 1.0, None, AT.mult))
        c.op("vector_s", lambda e: e.reciprocal(m.cc[:, 1:2], m.cc[:, 0:1]))
        for _ in range(2):
            c.op("vector_s", lambda e: e.tensor_tensor(
                m.cc[:, 2:3], m.cc[:, 0:1], m.cc[:, 1:2], AT.mult))
            c.op("vector_s", lambda e: e.tensor_scalar(
                m.cc[:, 2:3], m.cc[:, 2:3], -1.0, 2.0, AT.mult, AT.add))
            c.op("vector_s", lambda e: e.tensor_tensor(
                m.cc[:, 1:2], m.cc[:, 1:2], m.cc[:, 2:3], AT.mult))
        scm = c.op("vector_s", lambda e: e.tensor_scalar(
            m.cc[:, 3:4], m.cc[:, 1:2], 127.0, None, AT.mult), mark=True)
        s0 = m.f_h[:, 0:1024]
        s1 = m.f_h[:, 1024:2048]
        s2 = m.f_h[:, 2048:3072]
        s3 = m.f_h[:, 3072:4096]
        c.op("vector", lambda e: e.tensor_scalar(
            s0, stage, m.cc[:, 3:4], None, AT.mult),
            waits=[(c.sm["vector"], scm)], fresh=True)
        c.op("vector", lambda e: e.tensor_scalar(s1, s0, C52, C52, AT.add, AT.subtract))
        c.op("vector", lambda e: e.scalar_tensor_tensor(
            s2, s0, pi, s1, AT.mult, AT.subtract))
        c.op("vector", lambda e: e.tensor_scalar(s3, s0, C23, C23, AT.add, AT.subtract))
        vq = c.op("vector", lambda e: e.tensor_tensor(s3, s3, s2, AT.subtract), mark=True)
        tl = 0
        for b in range(8):
            tw = [(c.sm["vector"], vq)] if b == 0 else []
            if b == 0 and prev_ev:
                tw.append((c.sm["scalar"], prev_ev))
            tl = c.op("tensor", lambda e, b=b: e.transpose(
                m.psB[:, b * 128:(b + 1) * 128], s3[:, b * 128:(b + 1) * 128],
                m.ident[:]), waits=tw, mark=(b == 7))
        ev = c.op("scalar", lambda e: e.activation(
            m.xqT_bf[:], m.psB[:, 0:1024], AF.Copy, bias=0.0, scale=1.0),
            waits=[(c.sm["tensor"], tl)] + ([(c.qd["scalar"], prev_dma)] if prev_dma else []),
            mark=True)
        prev_ev = ev
        prev_dma = c.dma("scalar",
                         d["s_xq"].rearrange("(kc p) tk -> p kc tk", p=128)[:, :, t * 128:(t + 1) * 128],
                         m.xqT_bf[:].rearrange("p (kc tk) -> p kc tk", tk=128),
                         waits=[(c.sm["scalar"], ev)])
    c.op("sync", lambda e: e.nop(), waits=[(c.qd["scalar"], c.qd["scalar"].n)])


def _ffn_load_resident(c, m, qn, nchunk, cols):
    return c.dma("sync", m.wq_sb[:, 0:nchunk * cols].rearrange(
        "p (kc h) -> p kc h", h=cols),
        m.d[qn].rearrange("(kc p) h -> p kc h", p=128),
        waits=_bar(c))


def emit_ffn_f1(c, m):
    d = m.d
    wl = _ffn_load_resident(c, m, "s_qg", 8, 4096)
    prev_hg = 0
    prev_ev = 0
    prev_vg = 0
    for t in range(NT):
        xl = c.dma("sync", m.xqT_bf[:].rearrange("p (kc tk) -> p kc tk", tk=128),
                   d["s_xq"].rearrange("(kc p) tk -> p kc tk", p=128)[:, :, t * 128:(t + 1) * 128],
                   waits=[(c.qd["sync"], wl)] if t == 0 else [(c.sm["tensor"], c.wv("tensor"))])
        mml = 0
        for j in range(8):
            ps = m.psA[:, (j % 4) * 512:(j % 4 + 1) * 512] if j < 4 \
                else m.psB[:, (j - 4) * 512:(j - 3) * 512]
            for kc in range(8):
                tw = []
                if j == 0 and kc == 0:
                    tw = [(c.qd["sync"], xl)]
                    if prev_ev:
                        tw.append((c.sm["scalar"], prev_ev))
                    if prev_vg:
                        tw.append((c.sm["vector"], prev_vg))
                mml = c.op("tensor", lambda e, j=j, kc=kc, ps=ps: e.matmul(
                    ps, m.xqT_bf[:, kc * 128:(kc + 1) * 128],
                    m.wq_sb[:, kc * 4096 + j * 512:kc * 4096 + (j + 1) * 512],
                    start=(kc == 0), stop=(kc == 7)),
                    waits=tw, mark=(j == 7 and kc == 7))
        sccol = m.cc[:, 4 + (t % 2):5 + (t % 2)]
        nsccol = m.cc[:, 6 + (t % 2):7 + (t % 2)]
        scm = _ffn_scale_col(c, m, m.rtmp[:, t:t + 1], WGI, sccol,
                             waits=[(c.sm["scalar"], prev_ev)] if prev_ev else ())
        scm = c.op("vector_s", lambda e, sc=sccol, ns=nsccol: e.tensor_scalar(
            ns, sc, -1.0, None, AT.mult), mark=True)
        ev = 0
        for j in range(8):
            ps = m.psA[:, (j % 4) * 512:(j % 4 + 1) * 512] if j < 4 \
                else m.psB[:, (j - 4) * 512:(j - 3) * 512]
            tw = [(c.sm["tensor"], mml), (c.sm["vector"], scm)]
            if j == 0 and prev_hg:
                tw.append((c.qd["scalar"], prev_hg))
            ev = c.op("scalar", lambda e, j=j, ps=ps, ns=nsccol: e.activation(
                m.f_h[:, j * 512:(j + 1) * 512], ps, AF.Exp,
                bias=0.0, scale=ns),
                waits=tw, mark=(j == 7), fresh=True)
        vg = 0
        for half, pst in ((0, m.psA), (1, m.psB)):
            hh = m.f_h[:, half * 2048:(half + 1) * 2048]
            c.op("vector", lambda e, hh=hh: e.tensor_scalar(
                hh, hh, 1.0, None, AT.add),
                waits=[(c.sm["scalar"], ev)] if half == 0 else ())
            c.op("vector", lambda e, hh=hh: e.reciprocal(hh, hh))
            vg = c.op("vector", lambda e, hh=hh, pst=pst, sc=sccol: e.scalar_tensor_tensor(
                hh, pst[:], sc, hh, AT.mult, AT.mult),
                mark=(half == 1), fresh=True,
                waits=[(c.sm["vector"], scm)])
        prev_ev = c.wv("scalar")
        prev_vg = vg
        prev_hg = c.dma("scalar", d["s_hg"][t * 128:(t + 1) * 128, :],
                        m.f_h[:], waits=[(c.sm["vector"], vg)])
    c.op("sync", lambda e: e.nop(), waits=[(c.qd["scalar"], c.qd["scalar"].n)])


def emit_ffn_f2(c, m):
    d = m.d
    pi = m.pic[:, 0:1]
    wl = _ffn_load_resident(c, m, "s_qu", 8, 4096)
    prev_qh = 0
    prev_ev = 0
    prev_vh = 0
    for t in range(NT):
        xl = c.dma("sync", m.xqT_bf[:].rearrange("p (kc tk) -> p kc tk", tk=128),
                   d["s_xq"].rearrange("(kc p) tk -> p kc tk", p=128)[:, :, t * 128:(t + 1) * 128],
                   waits=[(c.qd["sync"], wl)] if t == 0 else [(c.sm["tensor"], c.wv("tensor"))])
        hg = c.dma("sync", m.f_h[:], d["s_hg"][t * 128:(t + 1) * 128, :],
                   waits=[(c.sm["scalar"], prev_ev)] if prev_ev else [])
        mml = 0
        for j in range(8):
            ps = m.psA[:, (j % 4) * 512:(j % 4 + 1) * 512] if j < 4 \
                else m.psB[:, (j - 4) * 512:(j - 3) * 512]
            for kc in range(8):
                tw = []
                if j == 0 and kc == 0:
                    tw = [(c.qd["sync"], xl), (c.sm["scalar"], c.wv("scalar")),
                          (c.sm["vector"], c.wv("vector"))]
                mml = c.op("tensor", lambda e, j=j, kc=kc, ps=ps: e.matmul(
                    ps, m.xqT_bf[:, kc * 128:(kc + 1) * 128],
                    m.wq_sb[:, kc * 4096 + j * 512:kc * 4096 + (j + 1) * 512],
                    start=(kc == 0), stop=(kc == 7)),
                    waits=tw, mark=(j == 7 and kc == 7))
        sccol = m.cc[:, 4 + (t % 2):5 + (t % 2)]
        scm = _ffn_scale_col(c, m, m.rtmp[:, t:t + 1], WUI, sccol)
        c.op("vector", lambda e, sc=sccol: e.scalar_tensor_tensor(
            m.f_h[:, 0:2048], m.psA[:], sc, m.f_h[:, 0:2048],
            AT.mult, AT.mult),
            waits=[(c.sm["tensor"], mml), (c.sm["vector"], scm), (c.qd["sync"], hg)],
            fresh=True)
        vh = c.op("vector", lambda e, sc=sccol: e.scalar_tensor_tensor(
            m.f_h[:, 2048:4096], m.psB[:], sc, m.f_h[:, 2048:4096],
            AT.mult, AT.mult),
            waits=[(c.sm["tensor"], mml), (c.sm["vector"], scm)],
            mark=True, fresh=True)
        c.op("vector_s", lambda e: e.reduce_max(
            m.cc[:, 0:1], m.f_h[:], axis=mybir.AxisListType.X,
            apply_absolute_value=True))
        c.op("vector_s", lambda e: e.tensor_scalar(
            m.cc[:, 0:1], m.cc[:, 0:1], 1e-5, None, AT.max))
        c.op("vector_s", lambda e, t=t: e.tensor_scalar(
            m.rtmp[:, 16 + t:17 + t], m.cc[:, 0:1], 1.0, None, AT.mult))
        c.op("vector_s", lambda e: e.reciprocal(m.cc[:, 1:2], m.cc[:, 0:1]))
        for _ in range(2):
            c.op("vector_s", lambda e: e.tensor_tensor(
                m.cc[:, 2:3], m.cc[:, 0:1], m.cc[:, 1:2], AT.mult))
            c.op("vector_s", lambda e: e.tensor_scalar(
                m.cc[:, 2:3], m.cc[:, 2:3], -1.0, 2.0, AT.mult, AT.add))
            c.op("vector_s", lambda e: e.tensor_tensor(
                m.cc[:, 1:2], m.cc[:, 1:2], m.cc[:, 2:3], AT.mult))
        s2col = m.cc[:, 6 + (t % 2):7 + (t % 2)]
        scm2 = c.op("vector_s", lambda e, sc=s2col: e.tensor_scalar(
            sc, m.cc[:, 1:2], 127.0, None, AT.mult), mark=True)
        for half, pst in ((0, m.psA), (1, m.psB)):
            hh = m.f_h[:, half * 2048:(half + 1) * 2048]
            c.op("vector", lambda e, pst=pst, hh=hh, sc=s2col: e.tensor_scalar(
                pst[:], hh, sc, None, AT.mult),
                waits=[(c.sm["vector"], scm2)], fresh=True)
            c.op("vector", lambda e, pst=pst, hh=hh: e.tensor_scalar(
                hh, pst[:], C52, C52, AT.add, AT.subtract))
            c.op("vector", lambda e, pst=pst, hh=hh: e.scalar_tensor_tensor(
                hh, pst[:], pi, hh, AT.mult, AT.subtract))
            c.op("vector", lambda e, pst=pst: e.tensor_scalar(
                pst[:], pst[:], C23, C23, AT.add, AT.subtract))
            c.op("vector", lambda e, pst=pst, hh=hh: e.tensor_tensor(
                hh, pst[:], hh, AT.subtract), mark=(half == 1))
        vq = c.wv("vector")
        ev = 0
        for b0 in range(0, 32, 8):
            tl = 0
            for b in range(b0, b0 + 8):
                tw = []
                if b == b0:
                    tw = [(c.sm["vector"], vq)] if b0 == 0 else [(c.sm["scalar"], ev)]
                tl = c.op("tensor", lambda e, b=b, b0=b0: e.transpose(
                    m.psA[:, (b - b0) * 128:(b - b0 + 1) * 128],
                    m.f_h[:, b * 128:(b + 1) * 128], m.ident[:]),
                    waits=tw, mark=(b == b0 + 7))
            ev = c.op("scalar", lambda e, b0=b0: e.activation(
                m.qhT_bf[:, b0 * 128:(b0 + 8) * 128], m.psA[:, 0:1024],
                AF.Copy, bias=0.0, scale=1.0),
                waits=[(c.sm["tensor"], tl)] + ([(c.qd["scalar"], prev_qh)] if (b0 == 0 and prev_qh) else []),
                mark=True)
        prev_vh = vq
        prev_ev = ev
        prev_qh = c.dma("scalar",
                        d["s_qhT"].rearrange("(hc p) tk -> p hc tk", p=128)[:, :, t * 128:(t + 1) * 128],
                        m.qhT_bf[:].rearrange("p (hc tk) -> p hc tk", tk=128),
                        waits=[(c.sm["scalar"], ev)])
    c.op("sync", lambda e: e.nop(), waits=[(c.qd["scalar"], c.qd["scalar"].n)])


def emit_ffn_f3(c, m, final=False):
    d = m.d
    wl = _ffn_load_resident(c, m, "s_qd", 32, 1024)
    prev_vr = 0
    prev_out = 0
    for t in range(NT):
        ql = c.dma("sync", m.qhT_bf[:].rearrange("p (hc tk) -> p hc tk", tk=128),
                   d["s_qhT"].rearrange("(hc p) tk -> p hc tk", p=128)[:, :, t * 128:(t + 1) * 128],
                   waits=[(c.qd["sync"], wl)] if t == 0 else [(c.sm["tensor"], c.wv("tensor"))])
        mml = 0
        for half in range(2):
            for ch in range(32):
                tw = []
                if half == 0 and ch == 0:
                    tw = [(c.qd["sync"], ql), (c.sm["scalar"], c.wv("scalar"))]
                mml = c.op("tensor", lambda e, ch=ch, half=half: e.matmul(
                    m.psA[:, half * 512:(half + 1) * 512],
                    m.qhT_bf[:, ch * 128:(ch + 1) * 128],
                    m.wq_sb[:, ch * 1024 + half * 512:ch * 1024 + (half + 1) * 512],
                    start=(ch == 0), stop=(ch == 31)),
                    waits=tw, mark=(half == 1 and ch == 31))
        sccol = m.cc[:, 4 + (t % 2):5 + (t % 2)]
        scm = _ffn_scale_col(c, m, m.rtmp[:, 16 + t:17 + t], WDI, sccol,
                             waits=[(c.sm["scalar"], c.wv("scalar"))])
        ev = c.op("scalar", lambda e, sc=sccol: e.activation(
            m.f_h[:, 0:1024], m.psA[:, 0:1024], AF.Copy,
            bias=0.0, scale=sc),
            waits=[(c.sm["tensor"], mml), (c.sm["vector"], scm)] +
                  ([(c.sm["vector"], prev_vr)] if prev_vr else []),
            mark=True, fresh=True)
        vr = c.op("vector", lambda e, t=t: e.tensor_tensor(
            m.X[:, t * 1024:(t + 1) * 1024], m.X[:, t * 1024:(t + 1) * 1024],
            m.f_h[:, 0:1024], AT.add),
            waits=[(c.sm["scalar"], ev)] + ([(c.qd["scalar"], prev_out)] if prev_out else []),
            mark=True)
        prev_vr = vr
        if final:
            prev_out = c.dma("scalar", d["out"][t * 128:(t + 1) * 128, :],
                             m.X[:, t * 1024:(t + 1) * 1024],
                             waits=[(c.sm["vector"], vr)])
    if final:
        c.op("sync", lambda e: e.nop(), waits=[(c.qd["scalar"], c.qd["scalar"].n)])


def emit_ffn(c, m, final=False):
    emit_ffn_f0(c, m)
    emit_ffn_f1(c, m)
    emit_ffn_f2(c, m)
    emit_ffn_f3(c, m, final=final)



def _softmax_psum(c, m, ps, outap, W, colbase, wpre, gate):
    cm = m.cc[:, colbase:colbase + 1]
    cn = m.cc[:, colbase + 1:colbase + 2]
    cz = m.cc[:, colbase + 2:colbase + 3]
    cr = m.cc[:, colbase + 3:colbase + 4]
    ct = m.cc[:, colbase + 4:colbase + 5]
    c.op("vector_s", lambda e: e.reduce_max(
        cm, ps, axis=mybir.AxisListType.X), waits=list(wpre) + list(gate))
    ngm = c.op("vector_s", lambda e: e.tensor_scalar(
        cn, cm, -1.0, None, AT.mult), mark=True)
    aex = c.op("scalar", lambda e: e.activation(
        outap, ps, AF.Exp, bias=cn, scale=1.0),
        waits=[(c.sm["vector"], ngm)], mark=True, fresh=True)
    c.op("vector", lambda e: e.tensor_scalar(
        ps, outap, 0.0, None, AT.add, AT.add, accum_out=cz),
        waits=[(c.sm["scalar"], aex)])
    c.op("vector_s", lambda e: e.reciprocal(cr, cz))
    for _ in range(2):
        c.op("vector_s", lambda e: e.tensor_tensor(ct, cz, cr, AT.mult))
        c.op("vector_s", lambda e: e.tensor_scalar(ct, ct, -1.0, 2.0, AT.mult, AT.add))
        c.op("vector_s", lambda e: e.tensor_tensor(cr, cr, ct, AT.mult))
    zm = c.op("vector_s", lambda e: e.engine_nop(), mark=True)
    vm = c.op("vector", lambda e: e.tensor_scalar(
        outap, outap, cr, None, AT.mult),
        waits=[(c.sm["vector"], zm)], mark=True, fresh=True)
    return vm


def emit_attn_b_ln_qkv(c, m):
    d = m.d
    stage = m.f_stage[:]
    prev_ev = 0
    prev_tl = 0
    for t in range(NT):
        raw = m.X[:, t * 1024:(t + 1) * 1024]
        w0 = _bar(c) if t == 0 else [(c.sm["vector"], c.wv("vector")),
                                     (c.sm["tensor"], prev_tl)]
        c.op("vector", lambda e, r=raw, o=stage: e.scalar_tensor_tensor(
            o, r, 1.0, r, AT.bypass, AT.bypass, accum_out=ccs(m, "lnS", 1)),
            waits=w0)
        c.op("vector", lambda e, r=raw, o=stage: e.scalar_tensor_tensor(
            o, r, 1.0, r, AT.bypass, AT.mult, accum_out=ccs(m, "lnQ", 1)))
        c.op("vector_s", lambda e: e.tensor_scalar(ccs(m, "lnmu", 1), ccs(m, "lnS", 1), 1.0 / D, None, AT.mult))
        c.op("vector_s", lambda e: e.tensor_scalar(ccs(m, "lnvar", 1), ccs(m, "lnQ", 1), 1.0 / D, None, AT.mult))
        c.op("vector_s", lambda e: e.tensor_tensor(ccs(m, "lnt1", 1), ccs(m, "lnmu", 1), ccs(m, "lnmu", 1), AT.mult))
        c.op("vector_s", lambda e: e.tensor_tensor(ccs(m, "lnvar", 1), ccs(m, "lnvar", 1), ccs(m, "lnt1", 1), AT.subtract))
        rm = emit_rstd(c, ccs(m, "lnvar", 1), ccs(m, "lnr", 1), ccs(m, "lnt1", 1), ccs(m, "lnt2", 1))
        vn = c.op("vector", lambda e, r=raw, o=stage: e.tensor_scalar(
            o, r, ccs(m, "lnmu", 1), ccs(m, "lnr", 1), AT.subtract, AT.mult),
            waits=[(c.sm["vector"], rm)], mark=True, fresh=True)
        tl = 0
        for b in range(8):
            tw = [(c.sm["vector"], vn)] if b == 0 else []
            if b == 0 and prev_ev:
                tw.append((c.sm["scalar"], prev_ev))
            tl = c.op("tensor", lambda e, b=b: e.transpose(
                m.psA[:, b * 128:(b + 1) * 128], stage[:, b * 128:(b + 1) * 128],
                m.ident[:]), waits=tw, mark=(b == 7))
        prev_tl = tl
        prev_ev = c.op("scalar", lambda e, t=t: e.activation(
            m.xlnT[:].rearrange("p (kc tk) -> p kc tk", tk=T)[:, :, t * 128:(t + 1) * 128],
            m.psA[:, 0:1024].rearrange("p (kc w) -> p kc w", w=128),
            AF.Copy, bias=0.0, scale=1.0),
            waits=[(c.sm["tensor"], tl)], mark=True)
    wbuf = (m.f_h[:, 0:1024], m.f_h[:, 1024:2048])
    evb = m.f_h[:, 2048:4096]
    prev_q = 0
    prev_ev2 = 0
    for ob in range(24):
        wb = wbuf[ob % 2]
        wld = c.dma("sync", wb.rearrange("p (k w) -> p k w", w=128),
                    d["w_qkv"][:, ob * 128:(ob + 1) * 128].rearrange("(k p) w -> p k w", p=128),
                    waits=([(c.sm["scalar"], prev_ev)] if ob == 0 else [])
                    + ([(c.sm["tensor"], c.wv("tensor"))] if ob >= 2 else []))
        mml = 0
        for tk in range(4):
            for kc in range(8):
                tw = []
                if tk == 0 and kc == 0:
                    tw = [(c.qd["sync"], wld)]
                    if prev_ev2:
                        tw.append((c.sm["scalar"], prev_ev2))
                mml = c.op("tensor", lambda e, tk=tk, kc=kc, wb=wb: e.matmul(
                    m.psA[:, tk * 512:(tk + 1) * 512],
                    wb[:, kc * 128:(kc + 1) * 128],
                    m.xlnT[:, kc * 2048 + tk * 512:kc * 2048 + (tk + 1) * 512],
                    start=(kc == 0), stop=(kc == 7)),
                    waits=tw, mark=(tk == 3 and kc == 7))
        ev2 = c.op("scalar", lambda e, ob=ob: e.activation(
            evb, m.psA[:], AF.Copy, bias=0.0, scale=0.125 if ob < 8 else 1.0),
            waits=[(c.sm["tensor"], mml)] + ([(c.qd["scalar"], prev_q)] if prev_q else []),
            mark=True)
        prev_ev2 = ev2
        prev_q = c.dma("scalar", d["s_qkvb"][ob * 128:(ob + 1) * 128, :], evb,
                       waits=[(c.sm["scalar"], ev2)])
    c.op("sync", lambda e: e.nop(), waits=[(c.qd["scalar"], c.qd["scalar"].n)])


def _diag_add(c, m, buf, r, val):
    sub = buf[:, r * 256 + r * 128:r * 256 + (r + 1) * 128]
    c.op("vector_s", lambda e, sub=sub, val=val: e.scalar_tensor_tensor(
        sub, m.ident[:], float(val), sub, AT.mult, AT.add), mark=True)


def _mp_prod(c, m, lhsTbuf, rhsbuf, outbuf, neg_into=None, diag=None, scale=None):
    ev = 0
    for ib in range(2):
        mm = 0
        for kb in range(2):
            mm = c.op("tensor", lambda e, ib=ib, kb=kb: e.matmul(
                m.psB[:, ib * 512:ib * 512 + 256],
                lhsTbuf[:, (kb * 2 + ib) * 128:(kb * 2 + ib + 1) * 128],
                rhsbuf[:, kb * 256:(kb + 1) * 256],
                start=(kb == 0), stop=(kb == 1)),
                waits=[(c.sm["scalar"], c.wv("scalar")),
                       (c.sm["vector"], c.wv("vector"))] if (ib == 0 and kb == 0) else (),
                mark=(kb == 1))
        ev = c.op("scalar", lambda e, ib=ib, sc=scale: e.activation(
            outbuf[:, ib * 256:(ib + 1) * 256], m.psB[:, ib * 512:ib * 512 + 256],
            AF.Copy, bias=0.0, scale=1.0 if sc is None else sc),
            waits=[(c.sm["tensor"], mm)], mark=True)
    if neg_into is not None:
        c.op("vector_s", lambda e: e.tensor_scalar(
            neg_into[:, 0:512], outbuf[:, 0:512], -1.0, None, AT.mult),
            waits=[(c.sm["scalar"], ev)])
        for r in range(2):
            _diag_add(c, m, neg_into, r, diag)
    return ev


def _transpose_blocks(c, m, src, dst, wpre):
    for ib in range(2):
        for kb in range(2):
            tt = c.op("tensor", lambda e, ib=ib, kb=kb: e.transpose(
                m.psB[:, 1792:1920], src[:, ib * 256 + kb * 128:ib * 256 + (kb + 1) * 128],
                m.ident[:]),
                waits=list(wpre) if (ib == 0 and kb == 0) else [(c.sm["scalar"], c.wv("scalar"))],
                mark=True)
            c.op("scalar", lambda e, ib=ib, kb=kb: e.activation(
                dst[:, (kb * 2 + ib) * 128:(kb * 2 + ib + 1) * 128],
                m.psB[:, 1792:1920], AF.Copy, bias=0.0, scale=1.0),
                waits=[(c.sm["tensor"], tt)], mark=True)


def emit_attn_b_premax(c, m, ncores=8):
    d = m.d
    bar0 = [(c.sm[e], c.wv(e)) for e in ("vector", "scalar", "gpsimd", "tensor")]
    bar0.append((c.qd["scalar"], c.qd["scalar"].n))
    c.op("vector_s", lambda e: e.memset(m.cc[:, 23:25], 0.0), waits=bar0)
    c.op("vector_s", lambda e: e.memset(m.msc[:, 256:257], 1.0), mark=True)
    for h in range(16):
        wq = c.dma("sync", m.bq[:], d["s_qkvb"][h * 64:(h + 1) * 64, :],
                   waits=[(c.sm[e], c.wv(e)) for e in ("vector", "scalar", "tensor")])
        wk = c.dma("sync", m.bk[:], d["s_qkvb"][1024 + h * 64:1024 + (h + 1) * 64, :])
        plm = 0
        for src_, dst, wdma in ((m.bq, m.bql, wq), (m.bk, m.bkl, wk)):
            v3 = src_[:].rearrange("p (mm w) -> p mm w", w=8)
            d3 = dst[:].rearrange("p (mm w) -> p mm w", w=1)
            c.op("vector_s", lambda e, v3=v3, d3=d3: e.tensor_tensor(
                d3, v3[:, :, 0:1], v3[:, :, 1:2], AT.add),
                waits=[(c.qd["sync"], wdma)])
            for i in range(2, 8):
                c.op("vector_s", lambda e, v3=v3, d3=d3, i=i: e.tensor_tensor(
                    d3, d3, v3[:, :, i:i + 1], AT.add))
            plm = c.op("vector_s", lambda e, dst=dst: e.tensor_scalar(
                dst[:], dst[:], 0.125, None, AT.mult), mark=True)
        for lmb in range(2):
            mm = c.op("tensor", lambda e, lmb=lmb: e.matmul(
                m.psA[:, 0:256], m.bql[:, lmb * 128:(lmb + 1) * 128], m.bkl[:],
                start=True, stop=True),
                waits=[(c.sm["vector"], plm) if lmb == 0 else
                       (c.sm["vector"], c.wv("vector")),
                       (c.sm["scalar"], c.wv("scalar"))], mark=True)
            _softmax_psum(c, m, m.psA[:, 0:256], m.b_a2[:, lmb * 256:(lmb + 1) * 256],
                          256, 8, [(c.sm["tensor"], mm)],
                          [(c.sm["scalar"], c.wv("scalar"))])
        c.op("vector_s", lambda e: e.reduce_sum(
            m.cc[:, 16:17], m.b_a2[:, 0:256], axis=mybir.AxisListType.X))
        c.op("vector_s", lambda e: e.reduce_sum(
            m.cc[:, 17:18], m.b_a2[:, 256:512], axis=mybir.AxisListType.X))
        c.op("vector_s", lambda e: e.tensor_tensor(
            m.cc[:, 16:17], m.cc[:, 16:17], m.cc[:, 17:18], AT.max))
        c.op("vector_s", lambda e: e.tensor_tensor(
            m.cc[:, 23:24], m.cc[:, 23:24], m.cc[:, 16:17], AT.max), mark=True)
        mm = c.op("tensor", lambda e: e.matmul(
            m.psB[0:1, 1792:2048], m.msc[:, 256:257], m.b_a2[:, 0:256],
            start=True, stop=False),
            waits=[(c.sm["vector"], c.wv("vector"))], mark=True)
        mm = c.op("tensor", lambda e: e.matmul(
            m.psB[0:1, 1792:2048], m.msc[:, 256:257], m.b_a2[:, 256:512],
            start=False, stop=True), mark=True)
        csum = c.op("scalar", lambda e: e.activation(
            m.msc[0:1, 768:1024], m.psB[0:1, 1792:2048], AF.Copy, bias=0.0, scale=1.0),
            waits=[(c.sm["tensor"], mm)], mark=True)
        c.op("vector_s", lambda e: e.reduce_max(
            m.cc[0:1, 16:17], m.msc[0:1, 768:1024], axis=mybir.AxisListType.X),
            waits=[(c.sm["scalar"], csum)])
        c.op("vector_s", lambda e: e.tensor_tensor(
            m.cc[0:1, 24:25], m.cc[0:1, 24:25], m.cc[0:1, 16:17], AT.max),
            mark=True)
    c.op("vector_s", lambda e: e.memset(m.msc[:, 0:128], 0.0))
    cpm = c.op("vector_s", lambda e: e.tensor_scalar(
        m.msc[:, 0:1], m.cc[:, 23:24], 1.0, None, AT.mult), mark=True)
    tt = c.op("tensor", lambda e: e.transpose(
        m.psB[:, 1792:1920], m.msc[:, 0:128], m.ident[:]),
        waits=[(c.sm["vector"], cpm)], mark=True)
    cev = c.op("scalar", lambda e: e.activation(
        m.msc[:, 128:256], m.psB[:, 1792:1920], AF.Copy, bias=0.0, scale=1.0),
        waits=[(c.sm["tensor"], tt)], mark=True)
    c.op("vector_s", lambda e: e.reduce_max(
        m.cc[0:1, 23:24], m.msc[0:1, 128:256], axis=mybir.AxisListType.X),
        waits=[(c.sm["scalar"], cev)])
    c.op("vector_s", lambda e: e.tensor_scalar(
        m.rtmp[0:1, 240:241], m.cc[0:1, 23:24], 1.0, None, AT.mult))
    vmk = c.op("vector_s", lambda e: e.tensor_scalar(
        m.rtmp[0:1, 241:242], m.cc[0:1, 24:25], 1.0, None, AT.mult), mark=True)
    if ncores > 1:
        gin = c.dma("gpsimd", d["s_gmax"], m.rtmp[0:1, 240:242],
                    waits=[(c.sm["vector"], vmk)])
        ccm = c.op("gpsimd", lambda e: e.collective_compute(
            "AllReduce", AT.max, replica_groups=[list(range(8))],
            ins=[d["s_gmax"]], outs=[d["s_gmax2"]]),
            waits=[(c.qd["gpsimd"], gin)], mark=True, nohz=True)
        gl = c.dma("sync", m.rtmp[0:1, 248:250], d["s_gmax2"],
                   waits=[(c.sm["gpsimd"], ccm)])
        gm = c.op("vector_s", lambda e: e.engine_nop(),
                  waits=[(c.qd["sync"], gl)], mark=True)
    else:
        gm = c.op("vector_s", lambda e: e.tensor_scalar(
            m.rtmp[0:1, 248:250], m.rtmp[0:1, 240:242], 1.0, None, AT.mult),
            waits=[(c.sm["vector"], vmk)], mark=True)
    c.op("vector_s", lambda e: e.tensor_tensor(
        m.cc[0:1, 20:21], m.rtmp[0:1, 248:249], m.rtmp[0:1, 249:250], AT.mult))
    c.op("vector_s", lambda e: e.reciprocal(m.cc[0:1, 21:22], m.cc[0:1, 20:21]))
    for _ in range(2):
        c.op("vector_s", lambda e: e.tensor_tensor(
            m.cc[0:1, 22:23], m.cc[0:1, 20:21], m.cc[0:1, 21:22], AT.mult))
        c.op("vector_s", lambda e: e.tensor_scalar(
            m.cc[0:1, 22:23], m.cc[0:1, 22:23], -1.0, 2.0, AT.mult, AT.add))
        c.op("vector_s", lambda e: e.tensor_tensor(
            m.cc[0:1, 21:22], m.cc[0:1, 21:22], m.cc[0:1, 22:23], AT.mult))
    c.op("vector_s", lambda e: e.memset(m.msc[0:1, 512:640], 1.0), mark=True)
    mm = c.op("tensor", lambda e: e.matmul(
        m.psB[:, 1792:1793], m.msc[0:1, 512:640], m.cc[0:1, 21:22],
        start=True, stop=True),
        waits=[(c.sm["vector"], c.wv("vector"))], mark=True)
    c.op("vector_s", lambda e: e.tensor_scalar(
        m.cc[:, 107:108], m.psB[:, 1792:1793], 1.0, None, AT.mult),
        waits=[(c.sm["tensor"], mm)], mark=True)


def emit_attn_b_head(c, m, h):
    d = m.d
    CB = 8
    bar0 = [(c.sm[e], c.wv(e)) for e in ("vector", "scalar", "gpsimd", "tensor")]
    bar0.append((c.qd["scalar"], c.qd["scalar"].n))
    wq = c.dma("sync", m.bq[:], d["s_qkvb"][h * 64:(h + 1) * 64, :], waits=bar0)
    wk = c.dma("sync", m.bk[:], d["s_qkvb"][1024 + h * 64:1024 + (h + 1) * 64, :])
    wv_ = c.dma("sync", m.bv[:], d["s_qkvb"][2048 + h * 64:2048 + (h + 1) * 64, :])
    wc = c.dma("sync", m.wcv[:, 0:13], d["w_conv"][h * 64:(h + 1) * 64, :])
    plm = 0
    for src_, dst, wdma in ((m.bq, m.bql, wq), (m.bk, m.bkl, wk)):
        v3 = src_[:].rearrange("p (mm w) -> p mm w", w=8)
        d3 = dst[:].rearrange("p (mm w) -> p mm w", w=1)
        c.op("vector_s", lambda e, v3=v3, d3=d3: e.tensor_tensor(
            d3, v3[:, :, 0:1], v3[:, :, 1:2], AT.add),
            waits=[(c.qd["sync"], wdma)])
        for i in range(2, 8):
            c.op("vector_s", lambda e, v3=v3, d3=d3, i=i: e.tensor_tensor(
                d3, d3, v3[:, :, i:i + 1], AT.add))
        plm = c.op("vector_s", lambda e, dst=dst: e.tensor_scalar(
            dst[:], dst[:], 0.125, None, AT.mult), mark=True)
    prev_sc = 0
    for t in range(NT):
        mm = c.op("tensor", lambda e, t=t: e.matmul(
            m.psA[:, 0:256], m.bq[:, t * 128:(t + 1) * 128], m.bkl[:],
            start=True, stop=True),
            waits=[(c.sm["vector"], plm), (c.sm["scalar"], c.wv("scalar"))] if t == 0
            else [(c.sm["scalar"], c.wv("scalar")), (c.sm["vector"], c.wv("vector"))],
            mark=True)
        a1t = m.f_h[:, t * 256:(t + 1) * 256]
        vm = _softmax_psum(c, m, m.psA[:, 0:256], a1t, 256, CB,
                           [(c.sm["tensor"], mm)],
                           [(c.sm["scalar"], prev_sc)] if prev_sc else [])
        for lmb in range(2):
            tt = c.op("tensor", lambda e, t=t, lmb=lmb, a1t=a1t: e.transpose(
                m.psB[:, 1792:1920], a1t[:, lmb * 128:(lmb + 1) * 128], m.ident[:]),
                waits=[(c.sm["vector"], vm)] if lmb == 0 else [(c.sm["scalar"], c.wv("scalar"))],
                mark=True)
            prev_sc = c.op("scalar", lambda e, t=t, lmb=lmb: e.activation(
                m.a1T[:, lmb * 2048 + t * 128:lmb * 2048 + (t + 1) * 128],
                m.psB[:, 1792:1920], AF.Copy, bias=0.0, scale=1.0),
                waits=[(c.sm["tensor"], tt)], mark=True)
    for lmb in range(2):
        mm = c.op("tensor", lambda e, lmb=lmb: e.matmul(
            m.psA[:, 0:256], m.bql[:, lmb * 128:(lmb + 1) * 128], m.bkl[:],
            start=True, stop=True),
            waits=[(c.sm["scalar"], c.wv("scalar")),
                   (c.sm["vector"], c.wv("vector"))], mark=True)
        _softmax_psum(c, m, m.psA[:, 0:256], m.b_a2[:, lmb * 256:(lmb + 1) * 256],
                      256, CB, [(c.sm["tensor"], mm)],
                      [(c.sm["scalar"], c.wv("scalar"))])
    for lmb in range(2):
        mm = 0
        for tk in range(4):
            mm = c.op("tensor", lambda e, lmb=lmb, tk=tk: e.matmul(
                m.psA[:, tk * 512:(tk + 1) * 512],
                m.bql[:, lmb * 128:(lmb + 1) * 128],
                m.bk[:, tk * 512:(tk + 1) * 512], start=True, stop=True),
                waits=[(c.sm["vector"], c.wv("vector")),
                       (c.sm["scalar"], c.wv("scalar"))] if tk == 0 else (),
                mark=(tk == 3))
        vm = _softmax_psum(c, m, m.psA[:], m.bs[:, 0:T], T, CB,
                           [(c.sm["tensor"], mm)],
                           [(c.sm["scalar"], c.wv("scalar"))])
        for t in range(NT):
            tt = c.op("tensor", lambda e, t=t: e.transpose(
                m.psB[:, 1792:1920], m.bs[:, t * 128:(t + 1) * 128], m.ident[:]),
                waits=[(c.sm["vector"], vm)] if t == 0 else [(c.sm["scalar"], c.wv("scalar"))],
                mark=True)
            c.op("scalar", lambda e, t=t, lmb=lmb: e.activation(
                m.a3T[:, t * 256 + lmb * 128:t * 256 + (lmb + 1) * 128],
                m.psB[:, 1792:1920], AF.Copy, bias=0.0, scale=1.0),
                waits=[(c.sm["tensor"], tt)], mark=True)
    for t in range(NT):
        tt = c.op("tensor", lambda e, t=t: e.transpose(
            m.psB[:, 1792:1856], m.bv[:, t * 128:(t + 1) * 128], m.ident[0:64, 0:64]),
            waits=[(c.qd["sync"], wv_), (c.sm["scalar"], c.wv("scalar"))] if t == 0
            else [(c.sm["scalar"], c.wv("scalar"))],
            mark=True)
        c.op("scalar", lambda e, t=t: e.activation(
            m.bs[:, t * 64:(t + 1) * 64], m.psB[:, 1792:1856],
            AF.Copy, bias=0.0, scale=1.0),
            waits=[(c.sm["tensor"], tt)], mark=True)
    for lmb in range(2):
        mm = 0
        for t in range(NT):
            mm = c.op("tensor", lambda e, t=t, lmb=lmb: e.matmul(
                m.psA[:, 0:64],
                m.a3T[:, t * 256 + lmb * 128:t * 256 + (lmb + 1) * 128],
                m.bs[:, t * 64:(t + 1) * 64],
                start=(t == 0), stop=(t == NT - 1)),
                waits=[(c.sm["scalar"], c.wv("scalar"))] if t == 0 else (),
                mark=(t == NT - 1))
        c.op("scalar", lambda e, lmb=lmb: e.activation(
            m.bY[:, lmb * 64:(lmb + 1) * 64], m.psA[:, 0:64],
            AF.Copy, bias=0.0, scale=1.0),
            waits=[(c.sm["tensor"], mm)], mark=True)
    s0m = c.op("vector_s", lambda e: e.engine_nop(), mark=True)
    _transpose_blocks(c, m, m.b_a2, m.b_aT, [(c.sm["vector"], s0m)])
    zin = c.op("vector", lambda e: e.tensor_scalar(
        m.b_z[:], m.b_aT[:], m.cc[:, 107:108], None, AT.mult),
        waits=[(c.sm["scalar"], c.wv("scalar")), (c.sm["vector"], s0m)],
        mark=True, fresh=True)
    for it in range(6):
        _mp_prod(c, m, m.b_aT, m.b_z, m.b_az)
        c.op("vector_s", lambda e: e.tensor_scalar(
            m.b_t[:], m.b_az[:], -1.0, None, AT.mult),
            waits=[(c.sm["scalar"], c.wv("scalar"))])
        for r in range(2):
            _diag_add(c, m, m.b_t, r, 7.0)
        _transpose_blocks(c, m, m.b_az, m.b_azT, [(c.sm["scalar"], c.wv("scalar"))])
        _mp_prod(c, m, m.b_azT, m.b_t, m.b_t, neg_into=m.b_t, diag=15.0)
        _mp_prod(c, m, m.b_azT, m.b_t, m.b_t, neg_into=m.b_t, diag=13.0)
        _transpose_blocks(c, m, m.b_z, m.b_zT, [(c.sm["vector"], c.wv("vector"))])
        _mp_prod(c, m, m.b_zT, m.b_t, m.b_z, scale=0.25)
    _transpose_blocks(c, m, m.b_z, m.b_zT, [(c.sm["scalar"], c.wv("scalar"))])
    for ib in range(2):
        mm = 0
        for kb in range(2):
            mm = c.op("tensor", lambda e, ib=ib, kb=kb: e.matmul(
                m.psA[:, 64:128],
                m.b_zT[:, (kb * 2 + ib) * 128:(kb * 2 + ib + 1) * 128],
                m.bY[:, kb * 64:(kb + 1) * 64],
                start=(kb == 0), stop=(kb == 1)),
                waits=[(c.sm["scalar"], c.wv("scalar"))] if kb == 0 else (),
                mark=(kb == 1))
        c.op("scalar", lambda e, ib=ib: e.activation(
            m.bZ2[:, ib * 64:(ib + 1) * 64], m.psA[:, 64:128],
            AF.Copy, bias=0.0, scale=1.0),
            waits=[(c.sm["tensor"], mm)], mark=True)
    mm = 0
    for tk in range(4):
        for lmb in range(2):
            mm = c.op("tensor", lambda e, tk=tk, lmb=lmb: e.matmul(
                m.psA[0:64, tk * 512:(tk + 1) * 512],
                m.bZ2[:, lmb * 64:(lmb + 1) * 64],
                m.a1T[:, lmb * 2048 + tk * 512:lmb * 2048 + (tk + 1) * 512],
                start=(lmb == 0), stop=(lmb == 1)),
                waits=[(c.sm["scalar"], c.wv("scalar"))] if (tk == 0 and lmb == 0) else (),
                mark=(tk == 3 and lmb == 1))
    bo = m.bs[0:64, 0:T]
    ev = c.op("scalar", lambda e: e.activation(
        bo, m.psA[0:64, 0:T], AF.Copy, bias=0.0, scale=1.0),
        waits=[(c.sm["tensor"], mm)], mark=True)
    cw = c.op("vector", lambda e: e.tensor_scalar(
        m.psB[0:64, 0:T], bo, m.wcv[:, 6:7], None, AT.mult),
        waits=[(c.sm["scalar"], ev), (c.qd["sync"], wc)], mark=True, fresh=True)
    for tap in range(13):
        if tap == 6:
            continue
        dd = tap - 6
        lo = max(0, -dd)
        hi = T - max(0, dd)
        cw = c.op("vector", lambda e, tap=tap, dd=dd, lo=lo, hi=hi: e.scalar_tensor_tensor(
            m.psB[0:64, lo:hi], bo[:, lo + dd:hi + dd], m.wcv[:, tap:tap + 1],
            m.psB[0:64, lo:hi], AT.mult, AT.add),
            mark=True, fresh=True, waits=[(c.sm["vector"], cw)])
    vo = c.op("vector", lambda e: e.tensor_tensor(
        bo, bo, m.psB[0:64, 0:T], AT.add),
        waits=[(c.sm["vector"], cw)], mark=True)
    return c.dma("scalar", d["s_outb"][h * 64:(h + 1) * 64, :], bo,
                 waits=[(c.sm["vector"], vo)])


def emit_attn_b_fc(c, m):
    d = m.d
    wl = c.dma("sync", m.W64[:, 0:8192].rearrange("p (kc w) -> p kc w", w=1024),
               d["w_fc"].rearrange("(kc p) w -> p kc w", p=128),
               waits=_bar(c))
    bl = c.dma("sync", m.bfc[:], d["b_fc"])
    prev_vr = 0
    for t in range(NT):
        ol = c.dma("sync", m.bs[:, 0:1024].rearrange("p (kc w) -> p kc w", w=128),
                   d["s_outb"].rearrange("(kc p) tk -> p kc tk", p=128)[:, :, t * 128:(t + 1) * 128],
                   waits=[(c.qd["sync"], bl)] if t == 0
                   else [(c.sm["tensor"], c.wv("tensor"))])
        mml = 0
        for half in range(2):
            for kc in range(8):
                tw = []
                if half == 0 and kc == 0:
                    tw = [(c.qd["sync"], ol)]
                    if prev_vr:
                        tw.append((c.sm["scalar"], c.wv("scalar")))
                mml = c.op("tensor", lambda e, half=half, kc=kc: e.matmul(
                    m.psA[:, half * 512:(half + 1) * 512],
                    m.bs[:, kc * 128:(kc + 1) * 128],
                    m.W64[:, kc * 1024 + half * 512:kc * 1024 + (half + 1) * 512],
                    start=(kc == 0), stop=(kc == 7)),
                    waits=tw, mark=(half == 1 and kc == 7))
        ev = c.op("scalar", lambda e: e.activation(
            m.f_h[:, 0:1024], m.psA[:, 0:1024], AF.Copy, bias=0.0, scale=1.0),
            waits=[(c.sm["tensor"], mml)] + ([(c.sm["vector"], prev_vr)] if prev_vr else []),
            mark=True)
        c.op("vector", lambda e: e.tensor_tensor(
            m.f_h[:, 0:1024], m.f_h[:, 0:1024], m.bfc[:], AT.add),
            waits=[(c.sm["scalar"], ev)])
        prev_vr = c.op("vector", lambda e, t=t: e.tensor_tensor(
            m.X[:, t * 1024:(t + 1) * 1024], m.X[:, t * 1024:(t + 1) * 1024],
            m.f_h[:, 0:1024], AT.add), mark=True)


def emit_attn_b(c, m, ncores=8):
    emit_attn_b_ln_qkv(c, m)
    emit_attn_b_premax(c, m, ncores=ncores)
    for h in range(16):
        emit_attn_b_head(c, m, h)
    c.op("sync", lambda e: e.nop(), waits=[(c.qd["scalar"], c.qd["scalar"].n)])
    emit_attn_b_fc(c, m)



def emit_setup(c, m):
    d = m.d
    w = 0
    for dst, src in [(m.ident[:], d["c_ident"]), (m.cosT[:], d["c_cos"]),
                     (m.sinT[:], d["c_sin"]), (m.wr8[:], d["c_wr8"]),
                     (m.wi8[:], d["c_wi8"]), (m.zst[:], d["c_zst"]),
                     (m.iota16[:], d["c_iota16"]), (m.tri[:], d["c_tri"])]:
        w = c.dma("sync", dst, src)
    wm = 0
    for b in range(32):
        ps = m.psA if b < 16 else m.psB
        bb = b % 16
        wm = c.op("tensor", lambda e, ps=ps, bb=bb: e.transpose(
            ps[:, bb * 128:(bb + 1) * 128], m.ident[:], m.ident[:]),
            waits=[(c.qd["sync"], w)] if b == 0 else (),
            mark=(b == 31))
    c.op("scalar", lambda e: e.activation(
        m.A2[:, 0:2048], m.psA[:], AF.Copy, bias=0.0, scale=1.0),
        waits=[(c.sm["tensor"], wm)])
    c.op("scalar", lambda e: e.activation(
        m.A2[:, 0:2048], m.psB[:], AF.Copy, bias=0.0, scale=1.0), mark=True)
    c.op("vector_s", lambda e: e.memset(m.pic[:], 0.0), waits=[(c.qd["sync"], w)])
    c.op("vector_s", lambda e: e.memset(m.cc[:], 0.0))
    c.op("vector_s", lambda e: e.memset(m.pic[:, 2:3], 1.0))
    c.op("vector_s", lambda e: e.tensor_scalar(m.pic[:, 0:1], m.pic[:, 2:3], C52, C52, AT.add, AT.subtract))
    c.op("vector_s", lambda e: e.tensor_scalar(m.pic[:, 1:2], m.pic[:, 0:1], -1.0, None, AT.mult))
    c.op("vector_s", lambda e: e.memset(m.pic[:, 3:4], -4.0), mark=True)


def build(stage="full", ncores=8):
    c = Ctx(stage)
    m = Mem(c)
    emit_setup(c, m)
    d = m.d
    if stage in ("wq", "ffn_only", "attn_b_only") or stage.startswith("headb"):
        wx = 0
        for t in range(NT):
            wx = c.dma("sync", m.X[:, t * 1024:(t + 1) * 1024],
                       d["x"][t * 128:(t + 1) * 128, :],
                       waits=_bar(c, dma=False) if t == 0 else ())
        c.op("vector_s", lambda e: e.engine_nop(),
             waits=[(c.qd["sync"], wx)], mark=True)
        if stage.startswith("headb"):
            hh = int(stage[5:])
            emit_attn_b_ln_qkv(c, m)
            for h2 in range(hh + 1):
                emit_attn_b_head(c, m, h2)
            w = _bar(c)
            c.dma("sync", d["dbg"][:, 0:512], m.b_a2[:], waits=w)
            c.dma("sync", d["dbg"][:, 512:1024], m.b_z[:])
            c.dma("sync", d["dbg"][:, 1024:1152], m.bY[:])
            c.dma("sync", d["dbg"][:, 1152:1280], m.bZ2[:])
            c.dma("sync", d["dbg"][0:64, 1280:1536], m.bql[:])
            c.dma("sync", d["dbg"][0:64, 1536:1792], m.bkl[:])
            c.dma("sync", d["dbg"][:, 2048:6144], m.a1T[:])
            fin = c.dma("sync", d["dbg"][:, 6144:7168], m.f_h[:, 0:1024])
            c.op("sync", lambda e: e.nop(), waits=[(c.qd["sync"], fin)])
            return c, m
        if stage == "attn_b_only":
            emit_attn_b(c, m, ncores=ncores)
            prev = 0
            for t in range(NT):
                prev = c.dma("scalar", d["out"][t * 128:(t + 1) * 128, :],
                             m.X[:, t * 1024:(t + 1) * 1024],
                             waits=_bar(c, dma=False) if t == 0 else ())
            c.op("sync", lambda e: e.nop(), waits=[(c.qd["scalar"], prev)])
            return c, m
        emit_wquant(c, m)
        if stage == "wq":
            fin = c.dma("sync", d["dbg"][:, 4096:4204], m.cc[:],
                        waits=_bar(c))
            c.op("sync", lambda e: e.nop(), waits=[(c.qd["sync"], fin)])
        else:
            emit_ffn(c, m, final=True)
        return c, m
    emit_attn_a(c, m)
    if stage not in ("full", "ffn1"):
        if stage != "attn_a":
            w = [(c.sm[e], c.wv(e)) for e in ("vector", "scalar", "gpsimd", "tensor")]
            c.dma("sync", m.d["dbg"][:, 0:2048], m.st[0][:], waits=w)
            c.dma("sync", m.d["dbg"][:, 2048:4096], m.st[1][:])
            c.dma("sync", m.d["dbg"][:, 4096:4204], m.cc[:])
            fin = c.dma("sync", m.d["dbg"][:, 6144:8192], m.Eb[:])
            c.op("sync", lambda e: e.nop(), waits=[(c.qd["sync"], fin)])
        return c, m
    emit_wquant(c, m)
    emit_ffn(c, m, final=(stage == "ffn1"))
    if stage == "ffn1":
        return c, m
    emit_attn_b(c, m, ncores=ncores)
    if stage == "attn_b":
        prev = 0
        for t in range(NT):
            prev = c.dma("scalar", d["out"][t * 128:(t + 1) * 128, :],
                         m.X[:, t * 1024:(t + 1) * 1024],
                         waits=_bar(c, dma=False) if t == 0 else ())
        c.op("sync", lambda e: e.nop(), waits=[(c.qd["scalar"], prev)])
        return c, m
    emit_ffn(c, m, final=True)
    return c, m


def kernel(**inputs):
    consts = host_constants()
    wts = host_weights(inputs)
    c, m = build("full")
    nc = c.materialize()
    in_maps = []
    for b in range(8):
        im = {"x": np.ascontiguousarray(inputs["L0_attn_a_x"][b])}
        im.update(wts)
        im.update(consts)
        in_maps.append(im)
    res = run_bass_kernel_spmd(nc, in_maps, core_ids=list(range(8)))
    return np.stack([r["out"] for r in res.results]).astype(np.float32)


# version 07
# speedup vs baseline: 1.0567x; 1.0567x over previous
import math
import numpy as np
import ml_dtypes
import concourse.bass as bass
import concourse.mybir as mybir
from concourse.bass_utils import run_bass_kernel_spmd

F32 = mybir.dt.float32
BF16 = mybir.dt.bfloat16
AT = mybir.AluOpType
AF = mybir.ActivationFunctionType

T, D, NT = 2048, 1024, 16
H, DH, KF = 16, 64, 32
KK = 100
NEG = -1.0e9
C23 = float(1.5 * 2.0 ** 23)
C52 = float(1.5 * 2.0 ** 52)
ZR = 92.0
NEWTON = 4
SQ2PI = float(math.sqrt(math.pi / 2.0))

ENGS = ("tensor", "vector", "scalar", "gpsimd", "sync")



def _riemann_zeros(K):
    n = np.arange(1, K + 1, dtype=np.float64)
    t = 10.0 + 6.0 * n
    for _ in range(60):
        f = t / (2 * np.pi) * np.log(t / (2 * np.pi * np.e)) - (n - 0.875)
        fp = np.log(t / (2 * np.pi)) / (2 * np.pi)
        t = t - f / fp
    return t


def _norm_ppf(p):
    a = [-3.969683028665376e+01, 2.209460984245205e+02, -2.759285104469687e+02,
         1.383577518672690e+02, -3.066479806614716e+01, 2.506628277459239e+00]
    b = [-5.447609879822406e+01, 1.615858368580409e+02, -1.556989798598866e+02,
         6.680131188771972e+01, -1.328068155288572e+01]
    cc = [-7.784894002430293e-03, -3.223964580411365e-01, -2.400758277161838e+00,
          -2.549732539343734e+00, 4.374664141464968e+00, 2.938163982698783e+00]
    dd = [7.784695709041462e-03, 3.224671290700398e-01, 2.445134137142996e+00,
          3.754408661907416e+00]
    p = np.asarray(p, dtype=np.float64)
    out = np.empty_like(p)
    pl = 0.02425
    lo = p < pl
    hi = p > 1 - pl
    mid = ~(lo | hi)
    q = np.sqrt(-2 * np.log(np.where(lo, p, 0.5)))
    out[lo] = ((((((cc[0] * q + cc[1]) * q + cc[2]) * q + cc[3]) * q + cc[4]) * q + cc[5]) /
               ((((dd[0] * q + dd[1]) * q + dd[2]) * q + dd[3]) * q + 1))[lo]
    q = np.sqrt(-2 * np.log(np.where(hi, 1 - p, 0.5)))
    out[hi] = -(((((((cc[0] * q + cc[1]) * q + cc[2]) * q + cc[3]) * q + cc[4]) * q + cc[5]) /
                ((((dd[0] * q + dd[1]) * q + dd[2]) * q + dd[3]) * q + 1)))[hi]
    q = p - 0.5
    r = q * q
    out[mid] = (((((((a[0] * r + a[1]) * r + a[2]) * r + a[3]) * r + a[4]) * r + a[5]) * q) /
                (((((b[0] * r + b[1]) * r + b[2]) * r + b[3]) * r + b[4]) * r + 1))[mid]
    return out


def host_constants():
    F = np.float32
    g = _riemann_zeros(KF)
    g = g / g[0]
    denom = 0.25 + g * g
    w_re = (0.5 / denom).astype(F)
    w_im = (-g / denom).astype(F)
    pos = np.arange(T, dtype=F)
    k_idx = np.arange(KF, dtype=F)
    inv_freq = 10000.0 ** (-k_idx / KF)
    phase = pos[:, None] * inv_freq[None, :]
    cos_p = np.cos(phase).astype(F)
    sin_p = np.sin(phase).astype(F)
    log1p = np.log1p(pos)
    log_decay = (0.5 * (log1p[None, :] - log1p[:, None])).astype(F)

    c = {}
    c["c_ident"] = np.eye(128, dtype=F)
    cos2 = np.concatenate([cos_p, cos_p], axis=1)
    sin2 = np.concatenate([sin_p, sin_p], axis=1)
    c["c_cos"] = np.ascontiguousarray(
        cos2.reshape(NT, 128, 64).transpose(1, 0, 2).reshape(128, NT * 64))
    c["c_sin"] = np.ascontiguousarray(
        sin2.reshape(NT, 128, 64).transpose(1, 0, 2).reshape(128, NT * 64))
    wr2 = np.concatenate([w_re / 8, w_re / 8]).astype(F)
    wi2 = np.concatenate([w_im / 8, w_im / 8]).astype(F)
    c["c_wr8"] = np.broadcast_to(wr2, (128, 64)).copy()
    c["c_wi8"] = np.broadcast_to(wi2, (128, 64)).copy()
    c["c_logdecay"] = log_decay
    lens = np.arange(1, T + 1, dtype=np.float64)
    frac = np.clip(ZR / np.maximum(lens, KK + 1.0), 1e-9, 1 - 1e-9)
    za = _norm_ppf(1 - frac)
    dens = np.maximum(np.exp(-0.5 * za * za) / math.sqrt(2 * math.pi) * lens, 1.0)
    zt = np.zeros((T, 8), dtype=F)
    zt[:, 0] = za
    zt[:, 1] = 1.0 / dens
    zt[:, 2] = (lens <= KK).astype(F)
    zt[:, 3] = (lens <= 160).astype(F)
    c["c_zst"] = np.ascontiguousarray(
        zt.reshape(NT, 128, 8).transpose(1, 0, 2).reshape(128, NT * 8))
    c["c_iota16"] = np.broadcast_to(np.arange(1, 17, dtype=F), (128, 16)).copy()
    ii = np.arange(128)
    c["c_tri"] = np.where(ii[:, None] >= ii[None, :], 0.0, NEG).astype(F)
    return c


def host_weights(W):
    out = {}
    def perm_qk(w):
        w = w.reshape(D, H // 2, 2, 2, KF)
        return None
    def perm(w):
        w4 = w.reshape(D, H // 2, 2, KF, 2)
        w4 = w4.transpose(0, 1, 2, 4, 3)
        return np.ascontiguousarray(w4.reshape(D, D))
    out["Wq"] = perm(W["L0_attn_a_Wq"])
    out["Wk"] = perm(W["L0_attn_a_Wk"])
    out["Wv"] = np.ascontiguousarray(W["L0_attn_a_Wv"])
    out["Wo"] = np.ascontiguousarray(W["L0_attn_a_Wo"])
    out["w_qkv"] = np.ascontiguousarray(W["L0_attn_b_w_qkv"])
    out["w_fc"] = np.ascontiguousarray(W["L0_attn_b_w_fc"])
    out["b_fc"] = np.ascontiguousarray(
        np.repeat(W["L0_attn_b_b_fc"].reshape(1, D), 128, axis=0))
    out["w_conv"] = np.ascontiguousarray(
        np.repeat(W["L0_attn_b_w_conv"][:, 0, :, 0], 64, axis=0))
    out["wg_t"] = np.ascontiguousarray(W["L0_ffn_w_gate"].T)
    out["wu_t"] = np.ascontiguousarray(W["L0_ffn_w_up"].T)
    out["wd_t"] = np.ascontiguousarray(W["L0_ffn_w_down"].T)
    return out



class Sem:
    __slots__ = ("h", "n")

    def __init__(self, nc, name):
        self.h = nc.alloc_semaphore(name)
        self.n = 0


class Ctx:
    def __init__(self, stage="full"):
        self.stage = stage
        self.nc = bass.Bass("TRN2", target_bir_lowering=False,
                            detect_race_conditions=False)
        self.q = {e: [] for e in ENGS}
        self.sm = {e: Sem(self.nc, "sm_" + e) for e in ENGS}
        self.qd = {"sync": Sem(self.nc, "q_sync"),
                   "gpsimd": Sem(self.nc, "q_gp"),
                   "scalar": Sem(self.nc, "q_act")}

    def op(self, eng, fn, waits=(), mark=False, small=False, fresh=False,
           nohz=False):
        if eng == "vector_s":
            eng, small = "vector", True
        for s, v in waits:
            assert v <= s.n, f"wait-on-future {v}>{s.n} eng={eng}"
        inc = None
        if mark:
            self.sm[eng].n += 1
            inc = (self.sm[eng], 1)
        self.q[eng].append((fn, list(waits), inc, small, fresh, nohz))
        return self.sm[eng].n

    def dma(self, eng, dst, src, waits=()):
        for s, v in waits:
            assert v <= s.n
        qs = self.qd[eng]
        qs.n += 16
        self.q[eng].append(
            (lambda e, d=dst, s_=src: e.dma_start(d, s_), list(waits),
             (qs, 16), False, False, True))
        return qs.n

    def wv(self, eng):
        return self.sm[eng].n

    def materialize(self):
        nc = self.nc
        hz = {e: Sem(nc, "hz_" + e) for e in ("vector", "gpsimd")}
        with nc.Block() as blk:
            for ename in ENGS:
                stream = self.q[ename]

                def run(e, stream=stream, ename=ename):
                    last = {}
                    hzs = hz.get(ename)
                    own = self.sm[ename]
                    hzc = 0
                    mkc = 0
                    lsync = None
                    n = len(stream)
                    for i, (fn, waits, inc, small, fresh, nohz) in enumerate(stream):
                        if small and hzs is not None and lsync is not None:
                            s, v = lsync
                            if last.get(id(s), -1) < v:
                                last[id(s)] = v
                                e.wait_ge(s.h, v)
                        for s, v in waits:
                            if not fresh and last.get(id(s), -1) >= v:
                                continue
                            last[id(s)] = max(last.get(id(s), -1), v)
                            e.wait_ge(s.h, v)
                        ins = fn(e)
                        if inc is not None:
                            ins.then_inc(inc[0].h, inc[1])
                            if inc[0] is own:
                                mkc += inc[1]
                                lsync = (inc[0], mkc)
                        elif hzs is not None and not nohz:
                            hzc += 1
                            ins.then_inc(hzs.h, 1)
                            lsync = (hzs, hzc)

                getattr(blk, ename)(run)
        return nc


def sbt(nc, name, shape, dtype, off):
    return nc.alloc_sbuf_tensor_at(name, shape, dtype, offset=off)



class Mem:
    def __init__(self, c):
        nc = c.nc
        d = {}
        d["x"] = nc.dram_tensor("x", [T, D], F32, kind="ExternalInput")
        for n, shp in [("Wq", [D, D]), ("Wk", [D, D]), ("Wv", [D, D]),
                       ("Wo", [D, D]), ("w_qkv", [D, 3 * D]), ("w_fc", [D, D]),
                       ("b_fc", [128, D]), ("wg_t", [D, 4096]), ("wu_t", [D, 4096]),
                       ("wd_t", [4096, D]), ("w_conv", [D, 13]),
                       ("c_ident", [128, 128]), ("c_cos", [128, NT * 64]),
                       ("c_sin", [128, NT * 64]), ("c_wr8", [128, 64]),
                       ("c_wi8", [128, 64]), ("c_logdecay", [T, T]),
                       ("c_zst", [128, NT * 8]), ("c_iota16", [128, 16]),
                       ("c_tri", [128, 128])]:
            d[n] = nc.dram_tensor(n, shp, F32, kind="ExternalInput")
        d["out"] = nc.dram_tensor("out", [T, D], F32, kind="ExternalOutput")
        d["dbg"] = nc.dram_tensor("dbg", [128, 8192], F32, kind="ExternalOutput")
        d["s_xlnT"] = nc.dram_tensor("s_xlnT", [D, T], F32)
        d["s_att"] = nc.dram_tensor("s_att", [D, T], F32)
        d["s_qg"] = nc.dram_tensor("s_qg", [D, 4096], BF16)
        d["s_qu"] = nc.dram_tensor("s_qu", [D, 4096], BF16)
        d["s_qd"] = nc.dram_tensor("s_qd", [4096, D], BF16)
        d["s_qkvb"] = nc.dram_tensor("s_qkvb", [3 * D, T], F32)
        d["s_outb"] = nc.dram_tensor("s_outb", [D, T], F32)
        d["s_hg"] = nc.dram_tensor("s_hg", [T, 4096], F32)
        d["s_xq"] = nc.dram_tensor("s_xq", [D, T], BF16)
        d["s_qhT"] = nc.dram_tensor("s_qhT", [4096, T], BF16)
        d["s_gmax"] = nc.dram_tensor("s_gmax", [1, 2], F32)
        d["s_gmax2"] = nc.dram_tensor("s_gmax2", [1, 2], F32)
        self.d = {k: v.ap() for k, v in d.items()}

        arena = nc.alloc_sbuf_tensor("arena_reserve", [128, 0x2F400], mybir.dt.uint8)
        base = None
        for al in nc.m.functions[0].allocations:
            if isinstance(al, mybir.MemoryLocationSet):
                for ml in al.memorylocations:
                    if ml.name.startswith("arena_reserve"):
                        base = ml.addr
        assert base is not None and base % 32 == 0, base
        self.base = base

        def sbt(nc_, name, shape, dtype, off, _b=base):
            return nc_.alloc_sbuf_tensor_at(name, shape, dtype, offset=_b + off)
        self.ident = sbt(nc, "ident", [128, 128], F32, 0x00000)
        self.cosT = sbt(nc, "cosT", [128, NT * 64], F32, 0x00200)
        self.sinT = sbt(nc, "sinT", [128, NT * 64], F32, 0x01200)
        self.wr8 = sbt(nc, "wr8", [128, 64], F32, 0x02200)
        self.wi8 = sbt(nc, "wi8", [128, 64], F32, 0x02300)
        self.zst = sbt(nc, "zst", [128, NT * 8], F32, 0x02400)
        self.iota16 = sbt(nc, "iota16", [128, 16], F32, 0x02600)
        self.tri = sbt(nc, "tri", [128, 128], F32, 0x03600)
        self.pic = sbt(nc, "pic", [128, 4], F32, 0x02640)
        self.cc = sbt(nc, "cc", [128, 108], F32, 0x02660)
        self.bfc = sbt(nc, "bfc", [128, D], F32, 0x02820)

        A = 0x03820
        self.A = A
        self.pQ = [sbt(nc, f"pQ{i}", [128, T], F32, A + i * 0x6000) for i in range(4)]
        self.pK = [sbt(nc, f"pK{i}", [128, T], F32, A + i * 0x6000 + 0x2000) for i in range(4)]
        self.pV = [sbt(nc, f"pV{i}", [128, T], F32, A + i * 0x6000 + 0x4000) for i in range(4)]
        self.X = sbt(nc, "X", [128, NT * 1024], F32, A)
        self.A2 = sbt(nc, "A2", [128, 8192], F32, A + 0x10000)

        WB = A + 0x18000
        self.WB = WB
        self.dcy = sbt(nc, "dcy", [128, T], F32, WB)
        self.st = [sbt(nc, "st0", [128, T], F32, WB + 0x2000),
                   sbt(nc, "st1", [128, T], F32, WB + 0x4000)]
        self.Eb = sbt(nc, "Eb", [128, T], F32, WB + 0x6000)
        self.Pb = sbt(nc, "Pb", [128, T], F32, WB + 0x8000)
        self.pT = sbt(nc, "pT", [128, T], F32, WB + 0xA000)
        self.msc = sbt(nc, "msc", [128, T], F32, WB + 0xC000)
        self.msc2 = sbt(nc, "msc2", [128, T], F32, WB + 0xE000)
        self.W64 = sbt(nc, "W64", [128, NT * 1024], F32, WB)
        self.xlnT = sbt(nc, "xlnT", [128, NT * 1024], F32, WB)

        TL = WB + 0x10000
        self.rtmp = sbt(nc, "rtmp", [128, 256], F32, TL)
        self.avst = sbt(nc, "avst", [128, 256], F32, TL + 0x400)
        self.wproj = sbt(nc, "wproj", [128, 3072], F32, TL + 0x800)
        self.wproj2 = self.wproj

        self.wq_sb = sbt(nc, "wq_sb", [128, 32768], BF16, WB)
        self.wqbf = sbt(nc, "wqbf", [128, 4096], BF16, WB + 0xC000)
        A2b = A + 0x10000
        self.f_h = sbt(nc, "f_h", [128, 4096], F32, A2b)
        self.qhT_bf = sbt(nc, "qhT_bf", [128, 4096], BF16, A2b + 0x4000)
        self.xqT_bf = sbt(nc, "xqT_bf", [128, 1024], BF16, A2b + 0x6000)
        self.f_stage = sbt(nc, "f_stage", [128, 1024], F32, A2b + 0x6800)

        self.bq = sbt(nc, "bq", [64, T], F32, WB)
        self.bk = sbt(nc, "bk", [64, T], F32, WB + 0x2000)
        self.bv = sbt(nc, "bv", [64, T], F32, WB + 0x4000)
        self.a1T = sbt(nc, "a1T", [128, 4096], F32, WB + 0x6000)
        self.a3T = sbt(nc, "a3T", [128, 4096], F32, WB + 0xA000)
        self.bs = sbt(nc, "bs", [128, T], F32, WB + 0xE000)
        self.b_a2 = sbt(nc, "b_a2", [128, 512], F32, A2b + 0x4000)
        self.b_z = sbt(nc, "b_z", [128, 512], F32, A2b + 0x4800)
        self.b_az = sbt(nc, "b_az", [128, 512], F32, A2b + 0x5000)
        self.b_t = sbt(nc, "b_t", [128, 512], F32, A2b + 0x5800)
        self.b_aT = sbt(nc, "b_aT", [128, 512], F32, A2b + 0x6000)
        self.b_azT = sbt(nc, "b_azT", [128, 512], F32, A2b + 0x6800)
        self.b_zT = sbt(nc, "b_zT", [128, 512], F32, A2b + 0x7000)
        self.bql = sbt(nc, "bql", [64, 256], F32, TL + 0x800)
        self.bkl = sbt(nc, "bkl", [64, 256], F32, TL + 0xC00)
        self.wcv = sbt(nc, "wcv", [64, 16], F32, TL + 0x1000)
        self.bY = sbt(nc, "bY", [128, 128], F32, TL + 0x1040)
        self.bZ2 = sbt(nc, "bZ2", [128, 128], F32, TL + 0x1240)

        self.psA = nc.alloc_psum_tensor("psA", [128, 2048], F32)
        self.psB = nc.alloc_psum_tensor("psB", [128, 2048], F32)


CC = dict(t=0, c=2, lo=4, hi=6, chi=8, tperf=10, hasperf=12, mu=14, sig=16,
          g=18, gn=20, pf=22, negt=24, sk=26, Z=28, craw=30, r=32,
          u1=34, u2=36, u3=38, lnS=40, lnQ=42, lnmu=44, lnvar=46, lnr=48,
          lnt1=50, lnt2=52, t16a=56, t16b=72, qv=88)


def ccs(m, name, k=2):
    base = CC[name]
    return m.cc[:, base:base + k]



def vsel(c, m, out, g, a, b):
    k = out.shape[1]
    u3 = ccs(m, "u3")[:, 0:k]
    c.op("vector_s", lambda e: e.tensor_tensor(u3, a, b, AT.subtract))
    c.op("vector_s", lambda e: e.tensor_tensor(u3, g, u3, AT.mult))
    c.op("vector_s", lambda e: e.tensor_tensor(out, b, u3, AT.add))


def emit_rstd(c, vcol, out, t1, t2):
    c.op("vector_s", lambda e: e.tensor_scalar(t2, vcol, 1e-5, None, AT.add))
    c.op("vector_s", lambda e: e.tensor_scalar(out, t2, 1.0, 0.5, AT.add, AT.mult))
    for _ in range(NEWTON):
        c.op("vector_s", lambda e: e.reciprocal(t1, out))
        c.op("vector_s", lambda e: e.tensor_tensor(t1, t2, t1, AT.mult))
        c.op("vector_s", lambda e: e.tensor_tensor(out, out, t1, AT.add))
        c.op("vector_s", lambda e: e.tensor_scalar(out, out, 0.5, None, AT.mult))
    return c.op("vector_s", lambda e: e.reciprocal(out, out), mark=True)



def emit_ln_transpose(c, m, src_getter):
    prev_spill = 0
    prev_ve = 0
    for t in range(NT):
        stage = m.msc[:, 0:1024]
        raw = m.msc2[:, 0:1024]
        ldw = src_getter(t, raw, [(c.qd["scalar"], prev_spill)] if prev_spill else [])
        c.op("vector", lambda e, r=raw, o=stage: e.scalar_tensor_tensor(
            o, r, 1.0, r, AT.bypass, AT.bypass, accum_out=ccs(m, "lnS", 1)),
            waits=ldw)
        c.op("vector", lambda e, r=raw, o=stage: e.scalar_tensor_tensor(
            o, r, 1.0, r, AT.bypass, AT.mult, accum_out=ccs(m, "lnQ", 1)))
        c.op("vector_s", lambda e: e.engine_nop())
        c.op("vector_s", lambda e: e.tensor_scalar(ccs(m, "lnmu", 1), ccs(m, "lnS", 1), 1.0 / D, None, AT.mult))
        c.op("vector_s", lambda e: e.tensor_scalar(ccs(m, "lnvar", 1), ccs(m, "lnQ", 1), 1.0 / D, None, AT.mult))
        c.op("vector_s", lambda e: e.tensor_tensor(ccs(m, "lnt1", 1), ccs(m, "lnmu", 1), ccs(m, "lnmu", 1), AT.mult))
        c.op("vector_s", lambda e: e.tensor_tensor(ccs(m, "lnvar", 1), ccs(m, "lnvar", 1), ccs(m, "lnt1", 1), AT.subtract))
        rm = emit_rstd(c, ccs(m, "lnvar", 1), ccs(m, "lnr", 1), ccs(m, "lnt1", 1), ccs(m, "lnt2", 1))
        vn = c.op("vector", lambda e, r=raw, o=stage: e.tensor_scalar(
            o, r, ccs(m, "lnmu", 1), ccs(m, "lnr", 1), AT.subtract, AT.mult),
            waits=[(c.sm["vector"], rm)], mark=True)
        tm = 0
        for b in range(8):
            tw = []
            if b == 0:
                tw = [(c.sm["vector"], vn)]
                if prev_ve:
                    tw.append((c.sm["scalar"], prev_ve))
            tm = c.op("tensor", lambda e, b=b, s=stage: e.transpose(
                m.psA[:, b * 128:(b + 1) * 128], s[:, b * 128:(b + 1) * 128], m.ident[:]),
                waits=tw, mark=(b == 7))
        c.op("scalar", lambda e, r=raw: e.activation(
            r[:, 0:512], m.psA[:, 0:512], AF.Copy, bias=0.0, scale=1.0),
            waits=[(c.sm["tensor"], tm)])
        ve = c.op("scalar", lambda e, r=raw: e.activation(
            r[:, 512:1024], m.psA[:, 512:1024], AF.Copy, bias=0.0, scale=1.0),
            mark=True)
        prev_ve = ve
        dst = m.d["s_xlnT"].rearrange("(cc p) t2 -> p cc t2", p=128)[:, :, t * 128:(t + 1) * 128]
        src = raw.rearrange("p (cc w) -> p cc w", w=128)
        prev_spill = c.dma("scalar", dst, src)



def emit_attn_a(c, m):
    d = m.d

    def src(t, raw, extra):
        w = c.dma("sync", raw, d["x"][t * 128:(t + 1) * 128, :], waits=extra)
        return [(c.qd["sync"], w)]
    emit_ln_transpose(c, m, src)
    ln_done_q = c.qd["scalar"].n

    for g in range(2):
        wx = c.dma("sync", m.xlnT[:].rearrange("p (cc t2) -> p cc t2", t2=T),
                   d["s_xlnT"].rearrange("(cc p) t2 -> p cc t2", p=128),
                   waits=[(c.qd["scalar"], c.qd["scalar"].n),
                          (c.sm["scalar"], c.wv("scalar")),
                          (c.sm["vector"], c.wv("vector")),
                          (c.sm["gpsimd"], c.wv("gpsimd")),
                          (c.sm["tensor"], c.wv("tensor"))])
        xw = [(c.qd["sync"], wx)]
        c.op("gpsimd", lambda e: e.engine_nop(),
             waits=[(c.qd["scalar"], c.qd["scalar"].n)], mark=True)
        for i in range(4):
            emit_project_pair(c, m, g * 4 + i, i, xw)
            if c.stage == "proj":
                return
        for qt in range(NT):
            emit_attn_a_qt(c, m, g, qt)
            if (c.stage == f"qt{qt}" or c.stage.startswith(f"probe{qt}s")
                    or c.stage.startswith(f"fin{qt}s")):
                return
    emit_attn_a_out(c, m)


def emit_project_pair(c, m, p, slot, xw):
    d = m.d
    wp = m.wproj
    ws = []
    for wi, wn in enumerate(("Wq", "Wk", "Wv")):
        dst = wp[:, wi * 1024:(wi + 1) * 1024].rearrange("q (k w) -> q k w", w=128)
        src = d[wn][:, p * 128:(p + 1) * 128].rearrange("(k q) w -> q k w", q=128)
        ws.append(c.dma("sync", dst, src,
                        waits=list(xw) + [(c.sm["tensor"], c.wv("tensor"))]))
    wld = [(c.qd["sync"], ws[-1])]
    outs = (m.pQ[slot], m.pK[slot], m.pV[slot])
    prev_ev = 0
    for t in range(NT):
        mmlast = 0
        for wi in range(3):
            for kc in range(8):
                tw = []
                if wi == 0 and kc == 0:
                    if t == 0:
                        tw += wld
                    if prev_ev:
                        tw.append((c.sm["scalar"], prev_ev))
                mmlast = c.op("tensor", lambda e, wi=wi, kc=kc, t=t: e.matmul(
                    m.psA[:, wi * 128:(wi + 1) * 128],
                    m.xlnT[:, kc * 2048 + t * 128: kc * 2048 + (t + 1) * 128],
                    wp[:, wi * 1024 + kc * 128: wi * 1024 + (kc + 1) * 128],
                    start=(kc == 0), stop=(kc == 7)),
                    waits=tw,
                    mark=(wi == 2 and kc == 7))
        ev = 0
        for wi in range(3):
            ev = c.op("scalar", lambda e, wi=wi, t=t, o=outs[wi]: e.activation(
                o[:, t * 128:(t + 1) * 128], m.psA[:, wi * 128:(wi + 1) * 128],
                AF.Copy, bias=0.0, scale=1.0),
                waits=[(c.sm["tensor"], mmlast)] if wi == 0 else (),
                mark=(wi == 2))
        prev_ev = ev
        qtile = m.pQ[slot][:, t * 128:(t + 1) * 128]
        ktile = m.pK[slot][:, t * 128:(t + 1) * 128]
        def hv(ap, lohalf):
            v3 = ap.rearrange("p (h c) -> p h c", h=2)
            return v3[:, :, 0:32] if lohalf else v3[:, :, 32:64]
        q_re = hv(qtile, True); q_im = hv(qtile, False)
        k_re = hv(ktile, True); k_im = hv(ktile, False)
        cosb = m.cosT[:, t * 64:(t + 1) * 64].rearrange("p (h c) -> p h c", h=2)
        sinb = m.sinT[:, t * 64:(t + 1) * 64].rearrange("p (h c) -> p h c", h=2)
        wr = m.wr8[:].rearrange("p (h c) -> p h c", h=2)
        wi = m.wi8[:].rearrange("p (h c) -> p h c", h=2)
        def h3(ap):
            return ap.rearrange("p (h c) -> p h c", h=2)
        s0 = h3(m.rtmp[:, 0:64]); s1 = h3(m.rtmp[:, 64:128])
        s2 = h3(m.rtmp[:, 128:192]); s3 = h3(m.rtmp[:, 192:256])
        a0 = h3(m.avst[:, 0:64]); a1 = h3(m.avst[:, 64:128])
        a2 = h3(m.avst[:, 128:192]); a3 = h3(m.avst[:, 192:256])
        aw = [(c.sm["scalar"], ev)]
        def vop(o, i0, i1, alu, w=()):
            return c.op("vector_s", lambda e, o=o, i0=i0, i1=i1, alu=alu: e.tensor_tensor(o, i0, i1, alu), waits=w)
        def gop(o, i0, i1, alu, w=(), mark=False):
            return c.op("gpsimd", lambda e, o=o, i0=i0, i1=i1, alu=alu: e.tensor_tensor(o, i0, i1, alu), waits=w, mark=mark, small=True)
        vop(s0, cosb, q_re, AT.mult, aw)
        vop(s2, sinb, q_im, AT.mult)
        vop(s0, s0, s2, AT.add)
        vop(s1, cosb, q_im, AT.mult)
        vop(s2, sinb, q_re, AT.mult)
        vop(s1, s1, s2, AT.subtract)
        vop(s2, wr, s0, AT.mult)
        vop(s3, wi, s1, AT.mult)
        vop(q_re, s2, s3, AT.subtract)
        vop(s2, wr, s1, AT.mult)
        vop(s3, wi, s0, AT.mult)
        vq = c.op("vector_s", lambda e, o=q_im, i0=s2, i1=s3: e.tensor_tensor(o, i0, i1, AT.add), mark=True)
        gop(a0, cosb, k_re, AT.mult, aw)
        gop(a1, sinb, k_im, AT.mult)
        gop(a2, cosb, k_im, AT.mult)
        gop(a3, sinb, k_re, AT.mult)
        gop(k_re, a0, a1, AT.add)
        vk = gop(k_im, a2, a3, AT.subtract, mark=True)
        for which, (buf, wmk, weng) in enumerate(((m.pQ[slot], vq, "vector"), (m.pK[slot], vk, "gpsimd"))):
            pslot = m.psB[:, 0:128] if which == 0 else m.psB[:, 1536:1664]
            tt = c.op("tensor", lambda e, b=buf, t=t, ps=pslot: e.transpose(
                ps, b[:, t * 128:(t + 1) * 128], m.ident[:]),
                waits=[(c.sm[weng], wmk), (c.sm["scalar"], c.wv("scalar"))],
                mark=True)
            c.op("scalar", lambda e, b=buf, t=t, ps=pslot: e.activation(
                b[:, t * 128:(t + 1) * 128], ps, AF.Copy, bias=0.0, scale=1.0),
                waits=[(c.sm["tensor"], tt)], mark=True)


PROBE_ENG = ["vector"] * 6
PROBE_DENS = 4


def _emit_probes(c, m, W, za, idens, losel):
    t_ = ccs(m, "t")
    cc_ = ccs(m, "c")
    lo = ccs(m, "lo")
    hi = ccs(m, "hi")
    chi = ccs(m, "chi")
    tp = ccs(m, "tperf")
    hp = ccs(m, "hasperf")
    mu = ccs(m, "mu")
    sg = ccs(m, "sig")
    gg = ccs(m, "g")
    gn = ccs(m, "gn")
    pf = ccs(m, "pf")
    nt_ = ccs(m, "negt")
    u1 = ccs(m, "u1")
    u2 = ccs(m, "u2")
    gw = [(c.sm["scalar"], c.wv("scalar")), (c.sm["vector"], c.wv("vector")),
          (c.sm["gpsimd"], c.wv("gpsimd"))]
    c.op("vector_s", lambda e: e.tensor_scalar(mu, mu, 1.0 / W, None, AT.mult), waits=gw)
    c.op("vector_s", lambda e: e.tensor_scalar(sg, sg, SQ2PI / W, None, AT.mult))
    c.op("vector_s", lambda e: e.tensor_scalar(u1, sg, -2.0, None, AT.mult))
    c.op("vector_s", lambda e: e.tensor_tensor(lo, mu, u1, AT.add))
    lsm = c.op("vector_s", lambda e: e.tensor_scalar(u1, lo, 300.0, -1.0, AT.add, AT.mult), mark=True)
    c.op("vector_s", lambda e: e.tensor_scalar(u1, u1, losel, None, AT.mult),
         waits=[(c.sm["vector"], lsm)])
    c.op("vector_s", lambda e: e.tensor_tensor(lo, lo, u1, AT.add))
    c.op("vector_s", lambda e: e.tensor_scalar(u1, sg, 12.0, None, AT.mult))
    c.op("vector_s", lambda e: e.tensor_tensor(hi, mu, u1, AT.add))
    c.op("vector_s", lambda e: e.memset(chi, -1e4))
    c.op("vector_s", lambda e: e.memset(tp, 0.0))
    hpm = c.op("vector_s", lambda e: e.memset(hp, 0.0), mark=True)
    c.op("vector_s", lambda e: e.tensor_scalar(u1, sg, za, None, AT.mult),
         waits=[(c.sm["vector"], hpm)])
    c.op("vector_s", lambda e: e.tensor_tensor(t_, mu, u1, AT.add))
    c.op("vector_s", lambda e: e.tensor_tensor(t_, t_, lo, AT.max))
    c.op("vector_s", lambda e: e.tensor_tensor(t_, t_, hi, AT.min))
    vp = c.op("vector_s", lambda e: e.tensor_scalar(nt_, t_, -1.0, None, AT.mult), mark=True)

    for j in range(len(PROBE_ENG)):
        eng = PROBE_ENG[j]
        if eng == "scalar":
            cl = 0
            for h01 in range(2):
                cl = c.op("scalar", lambda e, h=h01: e.activation(
                    m.msc[:, 0:W], m.st[h][:, 0:W], AF.Sign,
                    bias=m.cc[:, CC["negt"] + h:CC["negt"] + h + 1], scale=1.0,
                    accum_out=m.cc[:, CC["craw"] + h:CC["craw"] + h + 1]),
                    waits=[(c.sm["vector"], vp)], mark=(h01 == 1))
            vp2 = c.op("vector_s", lambda e: e.tensor_scalar(
                cc_, ccs(m, "craw"), float(W), 0.5, AT.add, AT.mult),
                waits=[(c.sm["scalar"], cl)], mark=True)
        else:
            dump = m.msc if eng != "vector" else m.msc2
            cl = 0
            for h01 in range(2):
                cl = c.op(eng, lambda e, h=h01, dp=dump: e.tensor_scalar(
                    dp[:, 0:W], m.st[h][:, 0:W],
                    m.cc[:, CC["t"] + h:CC["t"] + h + 1], None,
                    AT.is_ge, AT.add,
                    accum_out=m.cc[:, CC["c"] + h:CC["c"] + h + 1]),
                    waits=[(c.sm["vector"], vp)], mark=(h01 == 1))
            vp2 = c.op("vector_s", lambda e: e.engine_nop(),
                       waits=[(c.sm[eng], cl)], mark=True)
        c.op("vector_s", lambda e: e.tensor_scalar(gg, cc_, KK - 0.5, None, AT.is_ge))
        c.op("vector_s", lambda e: e.tensor_scalar(gn, gg, -1.0, 1.0, AT.mult, AT.add))
        vsel(c, m, lo, gg, t_, lo)
        vsel(c, m, hi, gn, t_, hi)
        vsel(c, m, chi, gn, cc_, chi)
        c.op("vector_s", lambda e: e.tensor_scalar(pf, cc_, float(KK), None, AT.is_equal))
        vsel(c, m, tp, pf, t_, tp)
        c.op("vector_s", lambda e: e.tensor_tensor(hp, hp, pf, AT.max))
        if j < len(PROBE_ENG) - 1:
            if j < PROBE_DENS - 1:
                c.op("vector_s", lambda e: e.tensor_scalar(u1, cc_, ZR, None, AT.subtract))
                idm = c.op("vector_s", lambda e: e.tensor_tensor(u1, u1, sg, AT.mult), mark=True)
                c.op("vector_s", lambda e: e.tensor_scalar(u1, u1, idens, None, AT.mult),
                     waits=[(c.sm["vector"], idm)])
                c.op("vector_s", lambda e: e.tensor_tensor(t_, t_, u1, AT.add))
                c.op("vector_s", lambda e: e.tensor_tensor(t_, t_, lo, AT.max))
                c.op("vector_s", lambda e: e.tensor_tensor(t_, t_, hi, AT.min))
            else:
                c.op("vector_s", lambda e: e.tensor_tensor(u1, hi, lo, AT.subtract))
                c.op("vector_s", lambda e: e.tensor_scalar(u1, u1, 0.62, None, AT.mult))
                c.op("vector_s", lambda e: e.tensor_tensor(t_, lo, u1, AT.add))
                c.op("vector_s", lambda e: e.tensor_scalar(u2, sg, 12.0, None, AT.mult))
                c.op("vector_s", lambda e: e.tensor_tensor(u2, mu, u2, AT.add))
                c.op("vector_s", lambda e: e.tensor_tensor(t_, t_, u2, AT.min))
            vp = c.op("vector_s", lambda e: e.tensor_scalar(nt_, t_, -1.0, None, AT.mult), mark=True)
        else:
            vp = c.op("vector_s", lambda e: e.engine_nop(), mark=True)



def emit_attn_a_qt(c, m, g, qt):
    d = m.d
    W = 128 * (qt + 1)
    zoff = qt * 8
    za = m.zst[:, zoff + 0:zoff + 1]
    idens = m.zst[:, zoff + 1:zoff + 2]
    ka = m.zst[:, zoff + 2:zoff + 3]
    losel = m.zst[:, zoff + 3:zoff + 4]
    wdc = c.dma("sync", m.msc2[:, 0:W], d["c_logdecay"][qt * 128:(qt + 1) * 128, 0:W],
                waits=[(c.sm["scalar"], c.wv("scalar")),
                       (c.sm["vector"], c.wv("vector")),
                       (c.sm["gpsimd"], c.wv("gpsimd")),
                       (c.sm["tensor"], c.wv("tensor"))])
    dce = c.op("scalar", lambda e: e.activation(m.dcy[:, 0:W], m.msc2[:, 0:W],
                                                AF.Exp, bias=0.0, scale=1.0),
               waits=[(c.qd["sync"], wdc),
                      (c.sm["gpsimd"], c.wv("gpsimd"))], mark=True)

    for slot in range(4):
        p = g * 4 + slot
        for h01 in range(2):
            lhs = m.pQ[slot][64 * h01:64 * (h01 + 1), qt * 128:(qt + 1) * 128]
            stt_ = m.st[h01]
            mml = 0
            for j0 in range(0, W, 512):
                j1 = min(j0 + 512, W)
                mml = c.op("tensor", lambda e, j0=j0, j1=j1, lhs=lhs, s=slot, h=h01: e.matmul(
                    m.psA[:, j0:j1], lhs, m.pK[s][64 * h:64 * (h + 1), j0:j1],
                    start=True, stop=True),
                    waits=[(c.sm["scalar"], c.wv("scalar"))] if j0 == 0 else (),
                    mark=(j1 == W))
            rawb = m.Eb if h01 == 0 else m.Pb
            evl = 0
            for j0 in range(0, W, 512):
                j1 = min(j0 + 512, W)
                evl = c.op("scalar", lambda e, j0=j0, j1=j1, rb=rawb: e.activation(
                    rb[:, j0:j1], m.psA[:, j0:j1], AF.Copy, bias=0.0, scale=1.0),
                    waits=[(c.sm["tensor"], mml)] if j0 == 0 else (),
                    mark=(j1 == W))
            c.op("vector", lambda e, o=stt_, rb=rawb, h=h01: e.scalar_tensor_tensor(
                o[:, 0:W], rb[:, 0:W], 1.0, m.dcy[:, 0:W], AT.mult, AT.mult,
                accum_out=m.cc[:, CC["mu"] + h:CC["mu"] + h + 1]),
                waits=[(c.sm["scalar"], evl), (c.sm["scalar"], dce)])
            vab = c.op("vector", lambda e, o=stt_, h=h01: e.reduce_sum(
                m.cc[:, CC["sig"] + h:CC["sig"] + h + 1], o[:, 0:W],
                axis=mybir.AxisListType.X, apply_absolute_value=True), mark=True)
            gpm = c.op("gpsimd", lambda e, o=stt_: e.tensor_tensor(
                o[:, qt * 128:(qt + 1) * 128], o[:, qt * 128:(qt + 1) * 128],
                m.tri[:], AT.add),
                waits=[(c.sm["vector"], vab)], mark=True)

        _emit_probes(c, m, W, za, idens, losel)

        if c.stage == f"probe{qt}s{slot}":
            return
        for h01 in range(2):
            _emit_finisher_av(c, m, W, slot, h01, ka)
        if c.stage == f"fin{qt}s{slot}":
            return
        dst = m.d["s_att"][p * 128:(p + 1) * 128, qt * 128:(qt + 1) * 128] \
            .rearrange("(h r) w -> r h w", h=2)
        c.last_av_dma = c.dma("scalar", dst,
                              m.avst[0:64, 0:256].rearrange("r (h w) -> r h w", h=2))


def _emit_finisher_av(c, m, W, slot, h01, ka):
    stt_ = m.st[h01]
    ebuf = m.Eb if h01 == 0 else m.Pb
    hicol = m.cc[:, CC["hi"] + h01:CC["hi"] + h01 + 1]
    skcol = m.cc[:, CC["sk"] + h01:CC["sk"] + h01 + 1]
    zcol = m.cc[:, CC["Z"] + h01:CC["Z"] + h01 + 1]
    rcol = m.cc[:, CC["r"] + h01:CC["r"] + h01 + 1]
    hpcol = m.cc[:, CC["hasperf"] + h01:CC["hasperf"] + h01 + 1]
    tpcol = m.cc[:, CC["tperf"] + h01:CC["tperf"] + h01 + 1]
    t16 = m.cc[:, CC["t16a"]:CC["t16a"] + 16]
    gpp = c.op("gpsimd", lambda e: e.tensor_scalar(
        m.msc[:, 0:W], stt_[:, 0:W], hicol, NEG, AT.is_ge, AT.mult),
        waits=[(c.sm["vector"], c.wv("vector"))], mark=True, small=True)
    gpp = c.op("gpsimd", lambda e: e.tensor_tensor(
        m.msc[:, 0:W], stt_[:, 0:W], m.msc[:, 0:W], AT.add), mark=True,
        small=True)
    c.op("vector_s", lambda e: e.max(out=t16[:, 0:8], in_=m.msc[:, 0:W]),
         waits=[(c.sm["gpsimd"], gpp)])
    c.op("vector_s", lambda e: e.match_replace(
        out=m.msc2[:, 0:W], in_to_replace=t16[:, 0:8],
        in_values=m.msc[:, 0:W], imm_value=NEG))
    c.op("vector_s", lambda e: e.max(out=t16[:, 8:16], in_=m.msc2[:, 0:W]))
    rnp = c.op("vector_s", lambda e: e.tensor_scalar(
        ccs(m, "r"), ccs(m, "chi"), float(KK), -1.0, AT.subtract, AT.mult), mark=True)
    c.op("vector_s", lambda e: e.scalar_tensor_tensor(
        m.msc2[:, 0:16], m.iota16[:], rcol, t16,
        AT.is_equal, AT.mult, accum_out=skcol),
        waits=[(c.sm["vector"], rnp)])
    vsel(c, m, skcol, hpcol, tpcol, skcol)
    c.op("vector_s", lambda e: e.memset(ccs(m, "u1", 1), -7e8))
    vsel(c, m, skcol, ka, ccs(m, "u1", 1), skcol)
    c.op("vector_s", lambda e: e.engine_nop(), mark=True)
    vfin = c.op("vector_s", lambda e: e.engine_nop(), mark=True)
    c.op("gpsimd", lambda e: e.tensor_scalar(
        m.msc2[:, 0:W], stt_[:, 0:W], skcol, NEG, AT.is_lt, AT.mult),
        waits=[(c.sm["vector"], vfin)], small=True)
    gpk = c.op("gpsimd", lambda e: e.tensor_tensor(
        stt_[:, 0:W], stt_[:, 0:W], m.msc2[:, 0:W], AT.add), mark=True,
        small=True)
    aex = c.op("scalar", lambda e: e.activation(
        ebuf[:, 0:W], stt_[:, 0:W], AF.Exp, bias=m.pic[:, 3:4], scale=1.0),
        waits=[(c.sm["gpsimd"], gpk)], mark=True)
    c.op("vector", lambda e: e.tensor_scalar(
        m.msc[:, 0:W], ebuf[:, 0:W], 0.0, None, AT.add, AT.add,
        accum_out=zcol),
        waits=[(c.sm["scalar"], aex)], mark=True)
    c.op("vector_s", lambda e: e.engine_nop())
    c.op("vector_s", lambda e: e.reciprocal(zcol, zcol), mark=True)
    vsc = c.wv("vector")
    asc = c.op("scalar", lambda e: e.activation(
        stt_[:, 0:W], ebuf[:, 0:W], AF.Copy, bias=0.0, scale=zcol),
        waits=[(c.sm["vector"], vsc)], mark=True)
    nblk = W // 128
    avps = m.psB[0:64, 512:640] if h01 == 0 else m.psB[0:64, 1024:1152]
    for b0 in range(0, nblk, 4):
        bN = min(b0 + 4, nblk)
        ps = m.psB[:, 0:512] if (b0 // 4) % 2 == 0 else m.psB[:, 1536:2048]
        tl = 0
        for b in range(b0, bN):
            tl = c.op("tensor", lambda e, b=b, b0=b0, ps=ps: e.transpose(
                ps[:, (b - b0) * 128:(b - b0 + 1) * 128],
                stt_[:, b * 128:(b + 1) * 128], m.ident[:]),
                waits=[(c.sm["scalar"], asc)] if b == b0 else (),
                mark=(b == bN - 1))
        el = c.op("scalar", lambda e, b0=b0, bN=bN, ps=ps: e.activation(
            m.pT[:, b0 * 128:bN * 128], ps[:, 0:(bN - b0) * 128],
            AF.Copy, bias=0.0, scale=1.0),
            waits=[(c.sm["tensor"], tl)], mark=True)
        for b in range(b0, bN):
            c.op("tensor", lambda e, b=b, ps=ps: e.matmul(
                avps, m.pV[slot][:, b * 128 + 64 * h01:b * 128 + 64 * (h01 + 1)],
                m.pT[:, b * 128:(b + 1) * 128],
                start=(b == 0), stop=(b == nblk - 1)),
                waits=[(c.sm["scalar"], el)], mark=(b == bN - 1))
    avm = c.wv("tensor")
    avw = [(c.sm["tensor"], avm)]
    if h01 == 0 and getattr(c, "last_av_dma", 0):
        avw.append((c.qd["scalar"], c.last_av_dma))
    c.op("scalar", lambda e: e.activation(
        m.avst[0:64, h01 * 128:(h01 + 1) * 128], avps, AF.Copy,
        bias=0.0, scale=1.0),
        waits=avw, mark=True)


def emit_attn_a_out(c, m):
    d = m.d
    wo = m.W64
    wl = c.dma("sync", wo[:, 0:8192].rearrange("q (k w) -> q k w", w=1024),
               d["Wo"].rearrange("(k q) w -> q k w", q=128),
               waits=[(c.sm["scalar"], c.wv("scalar")),
                      (c.sm["vector"], c.wv("vector")),
                      (c.sm["gpsimd"], c.wv("gpsimd")),
                      (c.sm["tensor"], c.wv("tensor")),
                      (c.qd["scalar"], c.qd["scalar"].n)])
    prev_vr = 0
    for t in range(NT):
        ps = m.psA[:, 0:1024] if t % 2 == 0 else m.psA[:, 1024:2048]
        wa = c.dma("sync", m.wproj[:, 0:1024].rearrange("q (k w) -> q k w", w=128),
                   d["s_att"].rearrange("(k q) t2 -> q k t2", q=128)[:, :, t * 128:(t + 1) * 128],
                   waits=[(c.qd["sync"], wl)] if t == 0
                   else [(c.sm["tensor"], c.wv("tensor"))])
        wxr = c.dma("sync", m.msc[:, 1024:2048],
                    d["x"][t * 128:(t + 1) * 128, :],
                    waits=[(c.sm["vector"], prev_vr)] if prev_vr else ())
        mml = 0
        for half in range(2):
            for kc in range(8):
                tw = []
                if half == 0 and kc == 0:
                    tw = [(c.qd["sync"], wa)]
                    if prev_vr:
                        tw.append((c.sm["vector"], prev_vr))
                mml = c.op("tensor", lambda e, kc=kc, half=half, ps=ps: e.matmul(
                    ps[:, half * 512:(half + 1) * 512],
                    m.wproj[:, kc * 128:(kc + 1) * 128],
                    wo[:, kc * 1024 + half * 512: kc * 1024 + (half + 1) * 512],
                    start=(kc == 0), stop=(kc == 7)),
                    waits=tw, mark=(half == 1 and kc == 7))
        vr = c.op("vector", lambda e, t=t, ps=ps: e.tensor_tensor(
            m.X[:, t * 1024:(t + 1) * 1024], m.msc[:, 1024:2048], ps, AT.add),
            waits=[(c.sm["tensor"], mml), (c.qd["sync"], wxr)], mark=True)
        prev_vr = vr
        if c.stage == "attn_a":
            c.dma("scalar", d["out"][t * 128:(t + 1) * 128, :],
                  m.X[:, t * 1024:(t + 1) * 1024],
                  waits=[(c.sm["vector"], vr)])
    if c.stage == "attn_a":
        fin = c.qd["scalar"].n
        c.op("sync", lambda e: e.nop(), waits=[(c.qd["scalar"], fin)])



WGI, WUI, WDI = 104, 105, 106
INV127 = 1.0 / 127.0


def _bar(c, dma=True):
    w = [(c.sm[e], c.wv(e)) for e in ("vector", "scalar", "gpsimd", "tensor")]
    if dma:
        w += [(c.qd[q], c.qd[q].n) for q in ("sync", "scalar", "gpsimd")]
    return w


def emit_wquant(c, m):
    d = m.d
    onesr = m.rtmp[0:1, 128:256]
    onesc = m.rtmp[:, 130:131]
    c.op("vector_s", lambda e: e.memset(m.rtmp[0:1, 128:256], 1.0), waits=_bar(c))
    c.op("vector_s", lambda e: e.memset(m.rtmp[:, 130:131], 1.0), mark=True)
    sA = m.W64[:, 0:4096]
    sB = m.W64[:, 4096:8192]
    sC = m.W64[:, 8192:12288]
    for wn, qn, rows, invcol in (("wg_t", "s_qg", 1024, WGI),
                                 ("wu_t", "s_qu", 1024, WUI),
                                 ("wd_t", "s_qd", 4096, WDI)):
        nt = rows // 128
        cols = 4096 if rows == 1024 else 1024
        prev = c.op("vector_s", lambda e: e.memset(m.cc[:, 0:2], 0.0), mark=True)
        for i in range(nt):
            wld = c.dma("sync", sA[:, 0:cols], d[wn][i * 128:(i + 1) * 128, :],
                        waits=[(c.sm["vector"], prev)])
            c.op("vector", lambda e, cols=cols: e.reduce_sum(
                m.cc[:, 1:2], sA[:, 0:cols], axis=mybir.AxisListType.X,
                apply_absolute_value=True),
                waits=[(c.qd["sync"], wld)])
            prev = c.op("vector_s", lambda e: e.tensor_tensor(
                m.cc[:, 0:1], m.cc[:, 0:1], m.cc[:, 1:2], AT.add), mark=True)
        mm1 = c.op("tensor", lambda e: e.matmul(
            m.psA[0:1, 0:1], m.cc[:, 0:1], onesc, start=True, stop=True),
            waits=[(c.sm["vector"], prev)], mark=True)
        c.op("vector_s", lambda e: e.tensor_scalar(
            m.cc[0:1, 2:3], m.psA[0:1, 0:1], float(2.0 ** -22), 1e-5,
            AT.mult, AT.max),
            waits=[(c.sm["tensor"], mm1)])
        c.op("vector_s", lambda e: e.reciprocal(m.cc[0:1, 3:4], m.cc[0:1, 2:3]))
        for _ in range(2):
            c.op("vector_s", lambda e: e.tensor_tensor(
                m.cc[0:1, 4:5], m.cc[0:1, 2:3], m.cc[0:1, 3:4], AT.mult))
            c.op("vector_s", lambda e: e.tensor_scalar(
                m.cc[0:1, 4:5], m.cc[0:1, 4:5], -1.0, 2.0, AT.mult, AT.add))
            c.op("vector_s", lambda e: e.tensor_tensor(
                m.cc[0:1, 3:4], m.cc[0:1, 3:4], m.cc[0:1, 4:5], AT.mult))
        vb = c.op("vector_s", lambda e: e.engine_nop(), mark=True)
        bc = c.op("tensor", lambda e: e.matmul(
            m.psA[:, 4:6], onesr, m.cc[0:1, 2:4], start=True, stop=True),
            waits=[(c.sm["vector"], vb)], mark=True)
        c.op("vector_s", lambda e, ic=invcol: e.tensor_scalar(
            m.cc[:, ic:ic + 1], m.psA[:, 4:5], 1.0, None, AT.mult),
            waits=[(c.sm["tensor"], bc)])
        bv = c.op("vector_s", lambda e: e.tensor_scalar(
            m.cc[:, 7:8], m.psA[:, 5:6], 1.0, None, AT.mult), mark=True)
        wq_dma = 0
        t1m = bv
        for i in range(nt):
            wld = c.dma("sync", sA[:, 0:cols], d[wn][i * 128:(i + 1) * 128, :],
                        waits=[(c.sm["vector"], t1m)])
            t1m = c.op("vector", lambda e, cols=cols: e.tensor_scalar(
                sB[:, 0:cols], sA[:, 0:cols], m.cc[:, 7:8], None, AT.mult),
                waits=[(c.qd["sync"], wld), (c.sm["vector"], bv)], mark=True)
            c.op("vector", lambda e, cols=cols: e.tensor_scalar(
                sC[:, 0:cols], sB[:, 0:cols], 0.5, None, AT.is_ge))
            c.op("vector", lambda e, cols=cols: e.tensor_scalar(
                sB[:, 0:cols], sB[:, 0:cols], -1.0, 0.5, AT.mult, AT.is_ge))
            c.op("vector", lambda e, cols=cols: e.tensor_tensor(
                sC[:, 0:cols], sC[:, 0:cols], sB[:, 0:cols], AT.subtract))
            cast = c.op("vector", lambda e, cols=cols: e.tensor_scalar(
                m.wqbf[:, 0:cols], sC[:, 0:cols], 1.0, None, AT.mult),
                waits=[(c.qd["scalar"], wq_dma)] if wq_dma else (), mark=True)
            wq_dma = c.dma("scalar", d[qn][i * 128:(i + 1) * 128, :],
                           m.wqbf[:, 0:cols], waits=[(c.sm["vector"], cast)])
    c.op("sync", lambda e: e.nop(), waits=[(c.qd["scalar"], c.qd["scalar"].n)])


def _ffn_scale_col(c, m, src_col, inv_col, out_col, waits=()):
    c.op("vector_s", lambda e: e.tensor_scalar(
        out_col, src_col, INV127, None, AT.mult), waits=waits)
    return c.op("vector_s", lambda e, ic=inv_col: e.tensor_tensor(
        out_col, out_col, m.cc[:, ic:ic + 1], AT.mult), mark=True)


def emit_ffn_f0(c, m):
    d = m.d
    stage = m.f_stage[:]
    pi = m.pic[:, 0:1]
    prev_dma = 0
    prev_ev = 0
    for t in range(NT):
        raw = m.X[:, t * 1024:(t + 1) * 1024]
        w0 = _bar(c) if t == 0 else [(c.sm["vector"], c.wv("vector"))]
        c.op("vector", lambda e, r=raw, o=stage: e.scalar_tensor_tensor(
            o, r, 1.0, r, AT.bypass, AT.bypass, accum_out=ccs(m, "lnS", 1)),
            waits=w0)
        c.op("vector", lambda e, r=raw, o=stage: e.scalar_tensor_tensor(
            o, r, 1.0, r, AT.bypass, AT.mult, accum_out=ccs(m, "lnQ", 1)))
        c.op("vector_s", lambda e: e.tensor_scalar(ccs(m, "lnmu", 1), ccs(m, "lnS", 1), 1.0 / D, None, AT.mult))
        c.op("vector_s", lambda e: e.tensor_scalar(ccs(m, "lnvar", 1), ccs(m, "lnQ", 1), 1.0 / D, None, AT.mult))
        c.op("vector_s", lambda e: e.tensor_tensor(ccs(m, "lnt1", 1), ccs(m, "lnmu", 1), ccs(m, "lnmu", 1), AT.mult))
        c.op("vector_s", lambda e: e.tensor_tensor(ccs(m, "lnvar", 1), ccs(m, "lnvar", 1), ccs(m, "lnt1", 1), AT.subtract))
        rm = emit_rstd(c, ccs(m, "lnvar", 1), ccs(m, "lnr", 1), ccs(m, "lnt1", 1), ccs(m, "lnt2", 1))
        vn = c.op("vector", lambda e, r=raw, o=stage: e.tensor_scalar(
            o, r, ccs(m, "lnmu", 1), ccs(m, "lnr", 1), AT.subtract, AT.mult),
            waits=[(c.sm["vector"], rm)], mark=True)
        c.op("vector_s", lambda e: e.reduce_max(
            m.cc[:, 0:1], stage, axis=mybir.AxisListType.X,
            apply_absolute_value=True))
        c.op("vector_s", lambda e: e.tensor_scalar(
            m.cc[:, 0:1], m.cc[:, 0:1], 1e-5, None, AT.max))
        c.op("vector_s", lambda e, t=t: e.tensor_scalar(
            m.rtmp[:, t:t + 1], m.cc[:, 0:1], 1.0, None, AT.mult))
        c.op("vector_s", lambda e: e.reciprocal(m.cc[:, 1:2], m.cc[:, 0:1]))
        for _ in range(2):
            c.op("vector_s", lambda e: e.tensor_tensor(
                m.cc[:, 2:3], m.cc[:, 0:1], m.cc[:, 1:2], AT.mult))
            c.op("vector_s", lambda e: e.tensor_scalar(
                m.cc[:, 2:3], m.cc[:, 2:3], -1.0, 2.0, AT.mult, AT.add))
            c.op("vector_s", lambda e: e.tensor_tensor(
                m.cc[:, 1:2], m.cc[:, 1:2], m.cc[:, 2:3], AT.mult))
        scm = c.op("vector_s", lambda e: e.tensor_scalar(
            m.cc[:, 3:4], m.cc[:, 1:2], 127.0, None, AT.mult), mark=True)
        s0 = m.f_h[:, 0:1024]
        s1 = m.f_h[:, 1024:2048]
        s2 = m.f_h[:, 2048:3072]
        s3 = m.f_h[:, 3072:4096]
        c.op("vector", lambda e: e.tensor_scalar(
            s0, stage, m.cc[:, 3:4], None, AT.mult),
            waits=[(c.sm["vector"], scm)], fresh=True)
        c.op("vector", lambda e: e.tensor_scalar(s1, s0, C52, C52, AT.add, AT.subtract))
        c.op("vector", lambda e: e.scalar_tensor_tensor(
            s2, s0, pi, s1, AT.mult, AT.subtract))
        c.op("vector", lambda e: e.tensor_scalar(s3, s0, C23, C23, AT.add, AT.subtract))
        vq = c.op("vector", lambda e: e.tensor_tensor(s3, s3, s2, AT.subtract), mark=True)
        tl = 0
        for b in range(8):
            tw = [(c.sm["vector"], vq)] if b == 0 else []
            if b == 0 and prev_ev:
                tw.append((c.sm["scalar"], prev_ev))
            tl = c.op("tensor", lambda e, b=b: e.transpose(
                m.psB[:, b * 128:(b + 1) * 128], s3[:, b * 128:(b + 1) * 128],
                m.ident[:]), waits=tw, mark=(b == 7))
        ev = c.op("scalar", lambda e: e.activation(
            m.xqT_bf[:], m.psB[:, 0:1024], AF.Copy, bias=0.0, scale=1.0),
            waits=[(c.sm["tensor"], tl)] + ([(c.qd["scalar"], prev_dma)] if prev_dma else []),
            mark=True)
        prev_ev = ev
        prev_dma = c.dma("scalar",
                         d["s_xq"].rearrange("(kc p) tk -> p kc tk", p=128)[:, :, t * 128:(t + 1) * 128],
                         m.xqT_bf[:].rearrange("p (kc tk) -> p kc tk", tk=128),
                         waits=[(c.sm["scalar"], ev)])
    c.op("sync", lambda e: e.nop(), waits=[(c.qd["scalar"], c.qd["scalar"].n)])


def _ffn_load_resident(c, m, qn, nchunk, cols):
    return c.dma("sync", m.wq_sb[:, 0:nchunk * cols].rearrange(
        "p (kc h) -> p kc h", h=cols),
        m.d[qn].rearrange("(kc p) h -> p kc h", p=128),
        waits=_bar(c))


def emit_ffn_f1(c, m):
    d = m.d
    wl = _ffn_load_resident(c, m, "s_qg", 8, 4096)
    prev_hg = 0
    prev_ev = 0
    prev_vg = 0
    for t in range(NT):
        xl = c.dma("sync", m.xqT_bf[:].rearrange("p (kc tk) -> p kc tk", tk=128),
                   d["s_xq"].rearrange("(kc p) tk -> p kc tk", p=128)[:, :, t * 128:(t + 1) * 128],
                   waits=[(c.qd["sync"], wl)] if t == 0 else [(c.sm["tensor"], c.wv("tensor"))])
        mml = 0
        for j in range(8):
            ps = m.psA[:, (j % 4) * 512:(j % 4 + 1) * 512] if j < 4 \
                else m.psB[:, (j - 4) * 512:(j - 3) * 512]
            for kc in range(8):
                tw = []
                if j == 0 and kc == 0:
                    tw = [(c.qd["sync"], xl)]
                    if prev_ev:
                        tw.append((c.sm["scalar"], prev_ev))
                    if prev_vg:
                        tw.append((c.sm["vector"], prev_vg))
                mml = c.op("tensor", lambda e, j=j, kc=kc, ps=ps: e.matmul(
                    ps, m.xqT_bf[:, kc * 128:(kc + 1) * 128],
                    m.wq_sb[:, kc * 4096 + j * 512:kc * 4096 + (j + 1) * 512],
                    start=(kc == 0), stop=(kc == 7)),
                    waits=tw, mark=(j == 7 and kc == 7))
        sccol = m.cc[:, 4 + (t % 2):5 + (t % 2)]
        nsccol = m.cc[:, 6 + (t % 2):7 + (t % 2)]
        scm = _ffn_scale_col(c, m, m.rtmp[:, t:t + 1], WGI, sccol,
                             waits=[(c.sm["scalar"], prev_ev)] if prev_ev else ())
        scm = c.op("vector_s", lambda e, sc=sccol, ns=nsccol: e.tensor_scalar(
            ns, sc, -1.0, None, AT.mult), mark=True)
        ev = 0
        for j in range(8):
            ps = m.psA[:, (j % 4) * 512:(j % 4 + 1) * 512] if j < 4 \
                else m.psB[:, (j - 4) * 512:(j - 3) * 512]
            tw = [(c.sm["tensor"], mml), (c.sm["vector"], scm)]
            if j == 0 and prev_hg:
                tw.append((c.qd["scalar"], prev_hg))
            ev = c.op("scalar", lambda e, j=j, ps=ps, ns=nsccol: e.activation(
                m.f_h[:, j * 512:(j + 1) * 512], ps, AF.Exp,
                bias=0.0, scale=ns),
                waits=tw, mark=(j == 7), fresh=True)
        vg = 0
        for half, pst in ((0, m.psA), (1, m.psB)):
            hh = m.f_h[:, half * 2048:(half + 1) * 2048]
            c.op("vector", lambda e, hh=hh: e.tensor_scalar(
                hh, hh, 1.0, None, AT.add),
                waits=[(c.sm["scalar"], ev)] if half == 0 else ())
            c.op("vector", lambda e, hh=hh: e.reciprocal(hh, hh))
            vg = c.op("vector", lambda e, hh=hh, pst=pst, sc=sccol: e.scalar_tensor_tensor(
                hh, pst[:], sc, hh, AT.mult, AT.mult),
                mark=(half == 1), fresh=True,
                waits=[(c.sm["vector"], scm)])
        prev_ev = c.wv("scalar")
        prev_vg = vg
        prev_hg = c.dma("scalar", d["s_hg"][t * 128:(t + 1) * 128, :],
                        m.f_h[:], waits=[(c.sm["vector"], vg)])
    c.op("sync", lambda e: e.nop(), waits=[(c.qd["scalar"], c.qd["scalar"].n)])


def emit_ffn_f2(c, m):
    d = m.d
    pi = m.pic[:, 0:1]
    wl = _ffn_load_resident(c, m, "s_qu", 8, 4096)
    prev_qh = 0
    prev_ev = 0
    prev_vh = 0
    for t in range(NT):
        xl = c.dma("sync", m.xqT_bf[:].rearrange("p (kc tk) -> p kc tk", tk=128),
                   d["s_xq"].rearrange("(kc p) tk -> p kc tk", p=128)[:, :, t * 128:(t + 1) * 128],
                   waits=[(c.qd["sync"], wl)] if t == 0 else [(c.sm["tensor"], c.wv("tensor"))])
        hg = c.dma("sync", m.f_h[:], d["s_hg"][t * 128:(t + 1) * 128, :],
                   waits=[(c.sm["scalar"], prev_ev)] if prev_ev else [])
        mml = 0
        for j in range(8):
            ps = m.psA[:, (j % 4) * 512:(j % 4 + 1) * 512] if j < 4 \
                else m.psB[:, (j - 4) * 512:(j - 3) * 512]
            for kc in range(8):
                tw = []
                if j == 0 and kc == 0:
                    tw = [(c.qd["sync"], xl), (c.sm["scalar"], c.wv("scalar")),
                          (c.sm["vector"], c.wv("vector"))]
                mml = c.op("tensor", lambda e, j=j, kc=kc, ps=ps: e.matmul(
                    ps, m.xqT_bf[:, kc * 128:(kc + 1) * 128],
                    m.wq_sb[:, kc * 4096 + j * 512:kc * 4096 + (j + 1) * 512],
                    start=(kc == 0), stop=(kc == 7)),
                    waits=tw, mark=(j == 7 and kc == 7))
        sccol = m.cc[:, 4 + (t % 2):5 + (t % 2)]
        scm = _ffn_scale_col(c, m, m.rtmp[:, t:t + 1], WUI, sccol)
        c.op("vector", lambda e, sc=sccol: e.scalar_tensor_tensor(
            m.f_h[:, 0:2048], m.psA[:], sc, m.f_h[:, 0:2048],
            AT.mult, AT.mult),
            waits=[(c.sm["tensor"], mml), (c.sm["vector"], scm), (c.qd["sync"], hg)],
            fresh=True)
        vh = c.op("vector", lambda e, sc=sccol: e.scalar_tensor_tensor(
            m.f_h[:, 2048:4096], m.psB[:], sc, m.f_h[:, 2048:4096],
            AT.mult, AT.mult),
            waits=[(c.sm["tensor"], mml), (c.sm["vector"], scm)],
            mark=True, fresh=True)
        c.op("vector_s", lambda e: e.reduce_max(
            m.cc[:, 0:1], m.f_h[:], axis=mybir.AxisListType.X,
            apply_absolute_value=True))
        c.op("vector_s", lambda e: e.tensor_scalar(
            m.cc[:, 0:1], m.cc[:, 0:1], 1e-5, None, AT.max))
        c.op("vector_s", lambda e, t=t: e.tensor_scalar(
            m.rtmp[:, 16 + t:17 + t], m.cc[:, 0:1], 1.0, None, AT.mult))
        c.op("vector_s", lambda e: e.reciprocal(m.cc[:, 1:2], m.cc[:, 0:1]))
        for _ in range(2):
            c.op("vector_s", lambda e: e.tensor_tensor(
                m.cc[:, 2:3], m.cc[:, 0:1], m.cc[:, 1:2], AT.mult))
            c.op("vector_s", lambda e: e.tensor_scalar(
                m.cc[:, 2:3], m.cc[:, 2:3], -1.0, 2.0, AT.mult, AT.add))
            c.op("vector_s", lambda e: e.tensor_tensor(
                m.cc[:, 1:2], m.cc[:, 1:2], m.cc[:, 2:3], AT.mult))
        s2col = m.cc[:, 6 + (t % 2):7 + (t % 2)]
        scm2 = c.op("vector_s", lambda e, sc=s2col: e.tensor_scalar(
            sc, m.cc[:, 1:2], 127.0, None, AT.mult), mark=True)
        for half, pst in ((0, m.psA), (1, m.psB)):
            hh = m.f_h[:, half * 2048:(half + 1) * 2048]
            c.op("vector", lambda e, pst=pst, hh=hh, sc=s2col: e.tensor_scalar(
                pst[:], hh, sc, None, AT.mult),
                waits=[(c.sm["vector"], scm2)], fresh=True)
            c.op("vector", lambda e, pst=pst, hh=hh: e.tensor_scalar(
                hh, pst[:], C52, C52, AT.add, AT.subtract))
            c.op("vector", lambda e, pst=pst, hh=hh: e.scalar_tensor_tensor(
                hh, pst[:], pi, hh, AT.mult, AT.subtract))
            c.op("vector", lambda e, pst=pst: e.tensor_scalar(
                pst[:], pst[:], C23, C23, AT.add, AT.subtract))
            c.op("vector", lambda e, pst=pst, hh=hh: e.tensor_tensor(
                hh, pst[:], hh, AT.subtract), mark=(half == 1))
        vq = c.wv("vector")
        ev = 0
        for b0 in range(0, 32, 8):
            tl = 0
            for b in range(b0, b0 + 8):
                tw = []
                if b == b0:
                    tw = [(c.sm["vector"], vq)] if b0 == 0 else [(c.sm["scalar"], ev)]
                tl = c.op("tensor", lambda e, b=b, b0=b0: e.transpose(
                    m.psA[:, (b - b0) * 128:(b - b0 + 1) * 128],
                    m.f_h[:, b * 128:(b + 1) * 128], m.ident[:]),
                    waits=tw, mark=(b == b0 + 7))
            ev = c.op("scalar", lambda e, b0=b0: e.activation(
                m.qhT_bf[:, b0 * 128:(b0 + 8) * 128], m.psA[:, 0:1024],
                AF.Copy, bias=0.0, scale=1.0),
                waits=[(c.sm["tensor"], tl)] + ([(c.qd["scalar"], prev_qh)] if (b0 == 0 and prev_qh) else []),
                mark=True)
        prev_vh = vq
        prev_ev = ev
        prev_qh = c.dma("scalar",
                        d["s_qhT"].rearrange("(hc p) tk -> p hc tk", p=128)[:, :, t * 128:(t + 1) * 128],
                        m.qhT_bf[:].rearrange("p (hc tk) -> p hc tk", tk=128),
                        waits=[(c.sm["scalar"], ev)])
    c.op("sync", lambda e: e.nop(), waits=[(c.qd["scalar"], c.qd["scalar"].n)])


def emit_ffn_f3(c, m, final=False):
    d = m.d
    wl = _ffn_load_resident(c, m, "s_qd", 32, 1024)
    prev_vr = 0
    prev_out = 0
    for t in range(NT):
        ql = c.dma("sync", m.qhT_bf[:].rearrange("p (hc tk) -> p hc tk", tk=128),
                   d["s_qhT"].rearrange("(hc p) tk -> p hc tk", p=128)[:, :, t * 128:(t + 1) * 128],
                   waits=[(c.qd["sync"], wl)] if t == 0 else [(c.sm["tensor"], c.wv("tensor"))])
        mml = 0
        for half in range(2):
            for ch in range(32):
                tw = []
                if half == 0 and ch == 0:
                    tw = [(c.qd["sync"], ql), (c.sm["scalar"], c.wv("scalar"))]
                mml = c.op("tensor", lambda e, ch=ch, half=half: e.matmul(
                    m.psA[:, half * 512:(half + 1) * 512],
                    m.qhT_bf[:, ch * 128:(ch + 1) * 128],
                    m.wq_sb[:, ch * 1024 + half * 512:ch * 1024 + (half + 1) * 512],
                    start=(ch == 0), stop=(ch == 31)),
                    waits=tw, mark=(half == 1 and ch == 31))
        sccol = m.cc[:, 4 + (t % 2):5 + (t % 2)]
        scm = _ffn_scale_col(c, m, m.rtmp[:, 16 + t:17 + t], WDI, sccol,
                             waits=[(c.sm["scalar"], c.wv("scalar"))])
        ev = c.op("scalar", lambda e, sc=sccol: e.activation(
            m.f_h[:, 0:1024], m.psA[:, 0:1024], AF.Copy,
            bias=0.0, scale=sc),
            waits=[(c.sm["tensor"], mml), (c.sm["vector"], scm)] +
                  ([(c.sm["vector"], prev_vr)] if prev_vr else []),
            mark=True, fresh=True)
        vr = c.op("vector", lambda e, t=t: e.tensor_tensor(
            m.X[:, t * 1024:(t + 1) * 1024], m.X[:, t * 1024:(t + 1) * 1024],
            m.f_h[:, 0:1024], AT.add),
            waits=[(c.sm["scalar"], ev)] + ([(c.qd["scalar"], prev_out)] if prev_out else []),
            mark=True)
        prev_vr = vr
        if final:
            prev_out = c.dma("scalar", d["out"][t * 128:(t + 1) * 128, :],
                             m.X[:, t * 1024:(t + 1) * 1024],
                             waits=[(c.sm["vector"], vr)])
    if final:
        c.op("sync", lambda e: e.nop(), waits=[(c.qd["scalar"], c.qd["scalar"].n)])


def emit_ffn(c, m, final=False):
    emit_ffn_f0(c, m)
    emit_ffn_f1(c, m)
    emit_ffn_f2(c, m)
    emit_ffn_f3(c, m, final=final)



def _softmax_psum(c, m, ps, outap, W, colbase, wpre, gate):
    cm = m.cc[:, colbase:colbase + 1]
    cn = m.cc[:, colbase + 1:colbase + 2]
    cz = m.cc[:, colbase + 2:colbase + 3]
    cr = m.cc[:, colbase + 3:colbase + 4]
    ct = m.cc[:, colbase + 4:colbase + 5]
    c.op("vector_s", lambda e: e.reduce_max(
        cm, ps, axis=mybir.AxisListType.X), waits=list(wpre) + list(gate))
    ngm = c.op("vector_s", lambda e: e.tensor_scalar(
        cn, cm, -1.0, None, AT.mult), mark=True)
    aex = c.op("scalar", lambda e: e.activation(
        outap, ps, AF.Exp, bias=cn, scale=1.0),
        waits=[(c.sm["vector"], ngm)], mark=True, fresh=True)
    c.op("vector", lambda e: e.tensor_scalar(
        ps, outap, 0.0, None, AT.add, AT.add, accum_out=cz),
        waits=[(c.sm["scalar"], aex)])
    c.op("vector_s", lambda e: e.reciprocal(cr, cz))
    for _ in range(2):
        c.op("vector_s", lambda e: e.tensor_tensor(ct, cz, cr, AT.mult))
        c.op("vector_s", lambda e: e.tensor_scalar(ct, ct, -1.0, 2.0, AT.mult, AT.add))
        c.op("vector_s", lambda e: e.tensor_tensor(cr, cr, ct, AT.mult))
    zm = c.op("vector_s", lambda e: e.engine_nop(), mark=True)
    vm = c.op("vector", lambda e: e.tensor_scalar(
        outap, outap, cr, None, AT.mult),
        waits=[(c.sm["vector"], zm)], mark=True, fresh=True)
    return vm


def emit_attn_b_ln_qkv(c, m):
    d = m.d
    stage = m.f_stage[:]
    prev_ev = 0
    prev_tl = 0
    for t in range(NT):
        raw = m.X[:, t * 1024:(t + 1) * 1024]
        w0 = _bar(c) if t == 0 else [(c.sm["vector"], c.wv("vector")),
                                     (c.sm["tensor"], prev_tl)]
        c.op("vector", lambda e, r=raw, o=stage: e.scalar_tensor_tensor(
            o, r, 1.0, r, AT.bypass, AT.bypass, accum_out=ccs(m, "lnS", 1)),
            waits=w0)
        c.op("vector", lambda e, r=raw, o=stage: e.scalar_tensor_tensor(
            o, r, 1.0, r, AT.bypass, AT.mult, accum_out=ccs(m, "lnQ", 1)))
        c.op("vector_s", lambda e: e.tensor_scalar(ccs(m, "lnmu", 1), ccs(m, "lnS", 1), 1.0 / D, None, AT.mult))
        c.op("vector_s", lambda e: e.tensor_scalar(ccs(m, "lnvar", 1), ccs(m, "lnQ", 1), 1.0 / D, None, AT.mult))
        c.op("vector_s", lambda e: e.tensor_tensor(ccs(m, "lnt1", 1), ccs(m, "lnmu", 1), ccs(m, "lnmu", 1), AT.mult))
        c.op("vector_s", lambda e: e.tensor_tensor(ccs(m, "lnvar", 1), ccs(m, "lnvar", 1), ccs(m, "lnt1", 1), AT.subtract))
        rm = emit_rstd(c, ccs(m, "lnvar", 1), ccs(m, "lnr", 1), ccs(m, "lnt1", 1), ccs(m, "lnt2", 1))
        vn = c.op("vector", lambda e, r=raw, o=stage: e.tensor_scalar(
            o, r, ccs(m, "lnmu", 1), ccs(m, "lnr", 1), AT.subtract, AT.mult),
            waits=[(c.sm["vector"], rm)], mark=True, fresh=True)
        tl = 0
        for b in range(8):
            tw = [(c.sm["vector"], vn)] if b == 0 else []
            if b == 0 and prev_ev:
                tw.append((c.sm["scalar"], prev_ev))
            tl = c.op("tensor", lambda e, b=b: e.transpose(
                m.psA[:, b * 128:(b + 1) * 128], stage[:, b * 128:(b + 1) * 128],
                m.ident[:]), waits=tw, mark=(b == 7))
        prev_tl = tl
        prev_ev = c.op("scalar", lambda e, t=t: e.activation(
            m.xlnT[:].rearrange("p (kc tk) -> p kc tk", tk=T)[:, :, t * 128:(t + 1) * 128],
            m.psA[:, 0:1024].rearrange("p (kc w) -> p kc w", w=128),
            AF.Copy, bias=0.0, scale=1.0),
            waits=[(c.sm["tensor"], tl)], mark=True)
    wbuf = (m.f_h[:, 0:1024], m.f_h[:, 1024:2048])
    evb = m.f_h[:, 2048:4096]
    prev_q = 0
    prev_ev2 = 0
    for ob in range(24):
        wb = wbuf[ob % 2]
        wld = c.dma("sync", wb.rearrange("p (k w) -> p k w", w=128),
                    d["w_qkv"][:, ob * 128:(ob + 1) * 128].rearrange("(k p) w -> p k w", p=128),
                    waits=([(c.sm["scalar"], prev_ev)] if ob == 0 else [])
                    + ([(c.sm["tensor"], c.wv("tensor"))] if ob >= 2 else []))
        mml = 0
        for tk in range(4):
            for kc in range(8):
                tw = []
                if tk == 0 and kc == 0:
                    tw = [(c.qd["sync"], wld)]
                    if prev_ev2:
                        tw.append((c.sm["scalar"], prev_ev2))
                mml = c.op("tensor", lambda e, tk=tk, kc=kc, wb=wb: e.matmul(
                    m.psA[:, tk * 512:(tk + 1) * 512],
                    wb[:, kc * 128:(kc + 1) * 128],
                    m.xlnT[:, kc * 2048 + tk * 512:kc * 2048 + (tk + 1) * 512],
                    start=(kc == 0), stop=(kc == 7)),
                    waits=tw, mark=(tk == 3 and kc == 7))
        ev2 = c.op("scalar", lambda e, ob=ob: e.activation(
            evb, m.psA[:], AF.Copy, bias=0.0, scale=0.125 if ob < 8 else 1.0),
            waits=[(c.sm["tensor"], mml)] + ([(c.qd["scalar"], prev_q)] if prev_q else []),
            mark=True)
        prev_ev2 = ev2
        prev_q = c.dma("scalar", d["s_qkvb"][ob * 128:(ob + 1) * 128, :], evb,
                       waits=[(c.sm["scalar"], ev2)])
    c.op("sync", lambda e: e.nop(), waits=[(c.qd["scalar"], c.qd["scalar"].n)])


def _diag_add(c, m, buf, r, val):
    sub = buf[:, r * 256 + r * 128:r * 256 + (r + 1) * 128]
    c.op("vector_s", lambda e, sub=sub, val=val: e.scalar_tensor_tensor(
        sub, m.ident[:], float(val), sub, AT.mult, AT.add), mark=True)


def _mp_prod(c, m, lhsTbuf, rhsbuf, outbuf, neg_into=None, diag=None, scale=None):
    ev = 0
    for ib in range(2):
        mm = 0
        for kb in range(2):
            mm = c.op("tensor", lambda e, ib=ib, kb=kb: e.matmul(
                m.psB[:, ib * 512:ib * 512 + 256],
                lhsTbuf[:, (kb * 2 + ib) * 128:(kb * 2 + ib + 1) * 128],
                rhsbuf[:, kb * 256:(kb + 1) * 256],
                start=(kb == 0), stop=(kb == 1)),
                waits=[(c.sm["scalar"], c.wv("scalar")),
                       (c.sm["vector"], c.wv("vector"))] if (ib == 0 and kb == 0) else (),
                mark=(kb == 1))
        ev = c.op("scalar", lambda e, ib=ib, sc=scale: e.activation(
            outbuf[:, ib * 256:(ib + 1) * 256], m.psB[:, ib * 512:ib * 512 + 256],
            AF.Copy, bias=0.0, scale=1.0 if sc is None else sc),
            waits=[(c.sm["tensor"], mm)], mark=True)
    if neg_into is not None:
        c.op("vector_s", lambda e: e.tensor_scalar(
            neg_into[:, 0:512], outbuf[:, 0:512], -1.0, None, AT.mult),
            waits=[(c.sm["scalar"], ev)])
        for r in range(2):
            _diag_add(c, m, neg_into, r, diag)
    return ev


def _transpose_blocks(c, m, src, dst, wpre):
    for ib in range(2):
        for kb in range(2):
            tt = c.op("tensor", lambda e, ib=ib, kb=kb: e.transpose(
                m.psB[:, 1792:1920], src[:, ib * 256 + kb * 128:ib * 256 + (kb + 1) * 128],
                m.ident[:]),
                waits=list(wpre) if (ib == 0 and kb == 0) else [(c.sm["scalar"], c.wv("scalar"))],
                mark=True)
            c.op("scalar", lambda e, ib=ib, kb=kb: e.activation(
                dst[:, (kb * 2 + ib) * 128:(kb * 2 + ib + 1) * 128],
                m.psB[:, 1792:1920], AF.Copy, bias=0.0, scale=1.0),
                waits=[(c.sm["tensor"], tt)], mark=True)


def emit_attn_b_premax(c, m, ncores=8):
    d = m.d
    bar0 = [(c.sm[e], c.wv(e)) for e in ("vector", "scalar", "gpsimd", "tensor")]
    bar0.append((c.qd["scalar"], c.qd["scalar"].n))
    c.op("vector_s", lambda e: e.memset(m.cc[:, 23:25], 0.0), waits=bar0)
    c.op("vector_s", lambda e: e.memset(m.msc[:, 256:257], 1.0), mark=True)
    for h in range(16):
        wq = c.dma("sync", m.bq[:], d["s_qkvb"][h * 64:(h + 1) * 64, :],
                   waits=[(c.sm[e], c.wv(e)) for e in ("vector", "scalar", "tensor")])
        wk = c.dma("sync", m.bk[:], d["s_qkvb"][1024 + h * 64:1024 + (h + 1) * 64, :])
        plm = 0
        for src_, dst, wdma in ((m.bq, m.bql, wq), (m.bk, m.bkl, wk)):
            v3 = src_[:].rearrange("p (mm w) -> p mm w", w=8)
            d3 = dst[:].rearrange("p (mm w) -> p mm w", w=1)
            c.op("vector_s", lambda e, v3=v3, d3=d3: e.tensor_tensor(
                d3, v3[:, :, 0:1], v3[:, :, 1:2], AT.add),
                waits=[(c.qd["sync"], wdma)])
            for i in range(2, 8):
                c.op("vector_s", lambda e, v3=v3, d3=d3, i=i: e.tensor_tensor(
                    d3, d3, v3[:, :, i:i + 1], AT.add))
            plm = c.op("vector_s", lambda e, dst=dst: e.tensor_scalar(
                dst[:], dst[:], 0.125, None, AT.mult), mark=True)
        for lmb in range(2):
            mm = c.op("tensor", lambda e, lmb=lmb: e.matmul(
                m.psA[:, 0:256], m.bql[:, lmb * 128:(lmb + 1) * 128], m.bkl[:],
                start=True, stop=True),
                waits=[(c.sm["vector"], plm) if lmb == 0 else
                       (c.sm["vector"], c.wv("vector")),
                       (c.sm["scalar"], c.wv("scalar"))], mark=True)
            _softmax_psum(c, m, m.psA[:, 0:256], m.b_a2[:, lmb * 256:(lmb + 1) * 256],
                          256, 8, [(c.sm["tensor"], mm)],
                          [(c.sm["scalar"], c.wv("scalar"))])
        c.op("vector_s", lambda e: e.reduce_sum(
            m.cc[:, 16:17], m.b_a2[:, 0:256], axis=mybir.AxisListType.X))
        c.op("vector_s", lambda e: e.reduce_sum(
            m.cc[:, 17:18], m.b_a2[:, 256:512], axis=mybir.AxisListType.X))
        c.op("vector_s", lambda e: e.tensor_tensor(
            m.cc[:, 16:17], m.cc[:, 16:17], m.cc[:, 17:18], AT.max))
        c.op("vector_s", lambda e: e.tensor_tensor(
            m.cc[:, 23:24], m.cc[:, 23:24], m.cc[:, 16:17], AT.max), mark=True)
        mm = c.op("tensor", lambda e: e.matmul(
            m.psB[0:1, 1792:2048], m.msc[:, 256:257], m.b_a2[:, 0:256],
            start=True, stop=False),
            waits=[(c.sm["vector"], c.wv("vector"))], mark=True)
        mm = c.op("tensor", lambda e: e.matmul(
            m.psB[0:1, 1792:2048], m.msc[:, 256:257], m.b_a2[:, 256:512],
            start=False, stop=True), mark=True)
        csum = c.op("scalar", lambda e: e.activation(
            m.msc[0:1, 768:1024], m.psB[0:1, 1792:2048], AF.Copy, bias=0.0, scale=1.0),
            waits=[(c.sm["tensor"], mm)], mark=True)
        c.op("vector_s", lambda e: e.reduce_max(
            m.cc[0:1, 16:17], m.msc[0:1, 768:1024], axis=mybir.AxisListType.X),
            waits=[(c.sm["scalar"], csum)])
        c.op("vector_s", lambda e: e.tensor_tensor(
            m.cc[0:1, 24:25], m.cc[0:1, 24:25], m.cc[0:1, 16:17], AT.max),
            mark=True)
    c.op("vector_s", lambda e: e.memset(m.msc[:, 0:128], 0.0))
    cpm = c.op("vector_s", lambda e: e.tensor_scalar(
        m.msc[:, 0:1], m.cc[:, 23:24], 1.0, None, AT.mult), mark=True)
    tt = c.op("tensor", lambda e: e.transpose(
        m.psB[:, 1792:1920], m.msc[:, 0:128], m.ident[:]),
        waits=[(c.sm["vector"], cpm)], mark=True)
    cev = c.op("scalar", lambda e: e.activation(
        m.msc[:, 128:256], m.psB[:, 1792:1920], AF.Copy, bias=0.0, scale=1.0),
        waits=[(c.sm["tensor"], tt)], mark=True)
    c.op("vector_s", lambda e: e.reduce_max(
        m.cc[0:1, 23:24], m.msc[0:1, 128:256], axis=mybir.AxisListType.X),
        waits=[(c.sm["scalar"], cev)])
    c.op("vector_s", lambda e: e.tensor_scalar(
        m.rtmp[0:1, 240:241], m.cc[0:1, 23:24], 1.0, None, AT.mult))
    vmk = c.op("vector_s", lambda e: e.tensor_scalar(
        m.rtmp[0:1, 241:242], m.cc[0:1, 24:25], 1.0, None, AT.mult), mark=True)
    if ncores > 1:
        gin = c.dma("gpsimd", d["s_gmax"], m.rtmp[0:1, 240:242],
                    waits=[(c.sm["vector"], vmk)])
        ccm = c.op("gpsimd", lambda e: e.collective_compute(
            "AllReduce", AT.max, replica_groups=[list(range(8))],
            ins=[d["s_gmax"]], outs=[d["s_gmax2"]]),
            waits=[(c.qd["gpsimd"], gin)], mark=True, nohz=True)
        gl = c.dma("sync", m.rtmp[0:1, 248:250], d["s_gmax2"],
                   waits=[(c.sm["gpsimd"], ccm)])
        gm = c.op("vector_s", lambda e: e.engine_nop(),
                  waits=[(c.qd["sync"], gl)], mark=True)
    else:
        gm = c.op("vector_s", lambda e: e.tensor_scalar(
            m.rtmp[0:1, 248:250], m.rtmp[0:1, 240:242], 1.0, None, AT.mult),
            waits=[(c.sm["vector"], vmk)], mark=True)
    c.op("vector_s", lambda e: e.tensor_tensor(
        m.cc[0:1, 20:21], m.rtmp[0:1, 248:249], m.rtmp[0:1, 249:250], AT.mult))
    c.op("vector_s", lambda e: e.reciprocal(m.cc[0:1, 21:22], m.cc[0:1, 20:21]))
    for _ in range(2):
        c.op("vector_s", lambda e: e.tensor_tensor(
            m.cc[0:1, 22:23], m.cc[0:1, 20:21], m.cc[0:1, 21:22], AT.mult))
        c.op("vector_s", lambda e: e.tensor_scalar(
            m.cc[0:1, 22:23], m.cc[0:1, 22:23], -1.0, 2.0, AT.mult, AT.add))
        c.op("vector_s", lambda e: e.tensor_tensor(
            m.cc[0:1, 21:22], m.cc[0:1, 21:22], m.cc[0:1, 22:23], AT.mult))
    c.op("vector_s", lambda e: e.memset(m.msc[0:1, 512:640], 1.0), mark=True)
    mm = c.op("tensor", lambda e: e.matmul(
        m.psB[:, 1792:1793], m.msc[0:1, 512:640], m.cc[0:1, 21:22],
        start=True, stop=True),
        waits=[(c.sm["vector"], c.wv("vector"))], mark=True)
    c.op("vector_s", lambda e: e.tensor_scalar(
        m.cc[:, 107:108], m.psB[:, 1792:1793], 1.0, None, AT.mult),
        waits=[(c.sm["tensor"], mm)], mark=True)


def emit_attn_b_head(c, m, h):
    d = m.d
    CB = 8
    bar0 = [(c.sm[e], c.wv(e)) for e in ("vector", "scalar", "gpsimd", "tensor")]
    bar0.append((c.qd["scalar"], c.qd["scalar"].n))
    wq = c.dma("sync", m.bq[:], d["s_qkvb"][h * 64:(h + 1) * 64, :], waits=bar0)
    wk = c.dma("sync", m.bk[:], d["s_qkvb"][1024 + h * 64:1024 + (h + 1) * 64, :])
    wv_ = c.dma("sync", m.bv[:], d["s_qkvb"][2048 + h * 64:2048 + (h + 1) * 64, :])
    wc = c.dma("sync", m.wcv[:, 0:13], d["w_conv"][h * 64:(h + 1) * 64, :])
    plm = 0
    for src_, dst, wdma in ((m.bq, m.bql, wq), (m.bk, m.bkl, wk)):
        v3 = src_[:].rearrange("p (mm w) -> p mm w", w=8)
        d3 = dst[:].rearrange("p (mm w) -> p mm w", w=1)
        c.op("vector_s", lambda e, v3=v3, d3=d3: e.tensor_tensor(
            d3, v3[:, :, 0:1], v3[:, :, 1:2], AT.add),
            waits=[(c.qd["sync"], wdma)])
        for i in range(2, 8):
            c.op("vector_s", lambda e, v3=v3, d3=d3, i=i: e.tensor_tensor(
                d3, d3, v3[:, :, i:i + 1], AT.add))
        plm = c.op("vector_s", lambda e, dst=dst: e.tensor_scalar(
            dst[:], dst[:], 0.125, None, AT.mult), mark=True)
    prev_sc = 0
    for t in range(NT):
        mm = c.op("tensor", lambda e, t=t: e.matmul(
            m.psA[:, 0:256], m.bq[:, t * 128:(t + 1) * 128], m.bkl[:],
            start=True, stop=True),
            waits=[(c.sm["vector"], plm), (c.sm["scalar"], c.wv("scalar"))] if t == 0
            else [(c.sm["scalar"], c.wv("scalar")), (c.sm["vector"], c.wv("vector"))],
            mark=True)
        a1t = m.f_h[:, t * 256:(t + 1) * 256]
        vm = _softmax_psum(c, m, m.psA[:, 0:256], a1t, 256, CB,
                           [(c.sm["tensor"], mm)],
                           [(c.sm["scalar"], prev_sc)] if prev_sc else [])
        for lmb in range(2):
            tt = c.op("tensor", lambda e, t=t, lmb=lmb, a1t=a1t: e.transpose(
                m.psB[:, 1792:1920], a1t[:, lmb * 128:(lmb + 1) * 128], m.ident[:]),
                waits=[(c.sm["vector"], vm)] if lmb == 0 else [(c.sm["scalar"], c.wv("scalar"))],
                mark=True)
            prev_sc = c.op("scalar", lambda e, t=t, lmb=lmb: e.activation(
                m.a1T[:, lmb * 2048 + t * 128:lmb * 2048 + (t + 1) * 128],
                m.psB[:, 1792:1920], AF.Copy, bias=0.0, scale=1.0),
                waits=[(c.sm["tensor"], tt)], mark=True)
    for lmb in range(2):
        mm = c.op("tensor", lambda e, lmb=lmb: e.matmul(
            m.psA[:, 0:256], m.bql[:, lmb * 128:(lmb + 1) * 128], m.bkl[:],
            start=True, stop=True),
            waits=[(c.sm["scalar"], c.wv("scalar")),
                   (c.sm["vector"], c.wv("vector"))], mark=True)
        _softmax_psum(c, m, m.psA[:, 0:256], m.b_a2[:, lmb * 256:(lmb + 1) * 256],
                      256, CB, [(c.sm["tensor"], mm)],
                      [(c.sm["scalar"], c.wv("scalar"))])
    for lmb in range(2):
        mm = 0
        for tk in range(4):
            mm = c.op("tensor", lambda e, lmb=lmb, tk=tk: e.matmul(
                m.psA[:, tk * 512:(tk + 1) * 512],
                m.bql[:, lmb * 128:(lmb + 1) * 128],
                m.bk[:, tk * 512:(tk + 1) * 512], start=True, stop=True),
                waits=[(c.sm["vector"], c.wv("vector")),
                       (c.sm["scalar"], c.wv("scalar"))] if tk == 0 else (),
                mark=(tk == 3))
        vm = _softmax_psum(c, m, m.psA[:], m.bs[:, 0:T], T, CB,
                           [(c.sm["tensor"], mm)],
                           [(c.sm["scalar"], c.wv("scalar"))])
        for t in range(NT):
            tt = c.op("tensor", lambda e, t=t: e.transpose(
                m.psB[:, 1792:1920], m.bs[:, t * 128:(t + 1) * 128], m.ident[:]),
                waits=[(c.sm["vector"], vm)] if t == 0 else [(c.sm["scalar"], c.wv("scalar"))],
                mark=True)
            c.op("scalar", lambda e, t=t, lmb=lmb: e.activation(
                m.a3T[:, t * 256 + lmb * 128:t * 256 + (lmb + 1) * 128],
                m.psB[:, 1792:1920], AF.Copy, bias=0.0, scale=1.0),
                waits=[(c.sm["tensor"], tt)], mark=True)
    for t in range(NT):
        tt = c.op("tensor", lambda e, t=t: e.transpose(
            m.psB[:, 1792:1856], m.bv[:, t * 128:(t + 1) * 128], m.ident[0:64, 0:64]),
            waits=[(c.qd["sync"], wv_), (c.sm["scalar"], c.wv("scalar"))] if t == 0
            else [(c.sm["scalar"], c.wv("scalar"))],
            mark=True)
        c.op("scalar", lambda e, t=t: e.activation(
            m.bs[:, t * 64:(t + 1) * 64], m.psB[:, 1792:1856],
            AF.Copy, bias=0.0, scale=1.0),
            waits=[(c.sm["tensor"], tt)], mark=True)
    for lmb in range(2):
        mm = 0
        for t in range(NT):
            mm = c.op("tensor", lambda e, t=t, lmb=lmb: e.matmul(
                m.psA[:, 0:64],
                m.a3T[:, t * 256 + lmb * 128:t * 256 + (lmb + 1) * 128],
                m.bs[:, t * 64:(t + 1) * 64],
                start=(t == 0), stop=(t == NT - 1)),
                waits=[(c.sm["scalar"], c.wv("scalar"))] if t == 0 else (),
                mark=(t == NT - 1))
        c.op("scalar", lambda e, lmb=lmb: e.activation(
            m.bY[:, lmb * 64:(lmb + 1) * 64], m.psA[:, 0:64],
            AF.Copy, bias=0.0, scale=1.0),
            waits=[(c.sm["tensor"], mm)], mark=True)
    s0m = c.op("vector_s", lambda e: e.engine_nop(), mark=True)
    _transpose_blocks(c, m, m.b_a2, m.b_aT, [(c.sm["vector"], s0m)])
    zin = c.op("vector", lambda e: e.tensor_scalar(
        m.b_z[:], m.b_aT[:], m.cc[:, 107:108], None, AT.mult),
        waits=[(c.sm["scalar"], c.wv("scalar")), (c.sm["vector"], s0m)],
        mark=True, fresh=True)
    for it in range(6):
        _mp_prod(c, m, m.b_aT, m.b_z, m.b_az)
        c.op("vector_s", lambda e: e.tensor_scalar(
            m.b_t[:], m.b_az[:], -1.0, None, AT.mult),
            waits=[(c.sm["scalar"], c.wv("scalar"))])
        for r in range(2):
            _diag_add(c, m, m.b_t, r, 7.0)
        _transpose_blocks(c, m, m.b_az, m.b_azT, [(c.sm["scalar"], c.wv("scalar"))])
        _mp_prod(c, m, m.b_azT, m.b_t, m.b_t, neg_into=m.b_t, diag=15.0)
        _mp_prod(c, m, m.b_azT, m.b_t, m.b_t, neg_into=m.b_t, diag=13.0)
        _transpose_blocks(c, m, m.b_z, m.b_zT, [(c.sm["vector"], c.wv("vector"))])
        _mp_prod(c, m, m.b_zT, m.b_t, m.b_z, scale=0.25)
    _transpose_blocks(c, m, m.b_z, m.b_zT, [(c.sm["scalar"], c.wv("scalar"))])
    for ib in range(2):
        mm = 0
        for kb in range(2):
            mm = c.op("tensor", lambda e, ib=ib, kb=kb: e.matmul(
                m.psA[:, 64:128],
                m.b_zT[:, (kb * 2 + ib) * 128:(kb * 2 + ib + 1) * 128],
                m.bY[:, kb * 64:(kb + 1) * 64],
                start=(kb == 0), stop=(kb == 1)),
                waits=[(c.sm["scalar"], c.wv("scalar"))] if kb == 0 else (),
                mark=(kb == 1))
        c.op("scalar", lambda e, ib=ib: e.activation(
            m.bZ2[:, ib * 64:(ib + 1) * 64], m.psA[:, 64:128],
            AF.Copy, bias=0.0, scale=1.0),
            waits=[(c.sm["tensor"], mm)], mark=True)
    mm = 0
    for tk in range(4):
        for lmb in range(2):
            mm = c.op("tensor", lambda e, tk=tk, lmb=lmb: e.matmul(
                m.psA[0:64, tk * 512:(tk + 1) * 512],
                m.bZ2[:, lmb * 64:(lmb + 1) * 64],
                m.a1T[:, lmb * 2048 + tk * 512:lmb * 2048 + (tk + 1) * 512],
                start=(lmb == 0), stop=(lmb == 1)),
                waits=[(c.sm["scalar"], c.wv("scalar"))] if (tk == 0 and lmb == 0) else (),
                mark=(tk == 3 and lmb == 1))
    bo = m.bs[0:64, 0:T]
    ev = c.op("scalar", lambda e: e.activation(
        bo, m.psA[0:64, 0:T], AF.Copy, bias=0.0, scale=1.0),
        waits=[(c.sm["tensor"], mm)], mark=True)
    cw = c.op("vector", lambda e: e.tensor_scalar(
        m.psB[0:64, 0:T], bo, m.wcv[:, 6:7], None, AT.mult),
        waits=[(c.sm["scalar"], ev), (c.qd["sync"], wc)], mark=True, fresh=True)
    for tap in range(13):
        if tap == 6:
            continue
        dd = tap - 6
        lo = max(0, -dd)
        hi = T - max(0, dd)
        cw = c.op("vector", lambda e, tap=tap, dd=dd, lo=lo, hi=hi: e.scalar_tensor_tensor(
            m.psB[0:64, lo:hi], bo[:, lo + dd:hi + dd], m.wcv[:, tap:tap + 1],
            m.psB[0:64, lo:hi], AT.mult, AT.add),
            mark=True, fresh=True, waits=[(c.sm["vector"], cw)])
    vo = c.op("vector", lambda e: e.tensor_tensor(
        bo, bo, m.psB[0:64, 0:T], AT.add),
        waits=[(c.sm["vector"], cw)], mark=True)
    return c.dma("scalar", d["s_outb"][h * 64:(h + 1) * 64, :], bo,
                 waits=[(c.sm["vector"], vo)])


def emit_attn_b_fc(c, m):
    d = m.d
    wl = c.dma("sync", m.W64[:, 0:8192].rearrange("p (kc w) -> p kc w", w=1024),
               d["w_fc"].rearrange("(kc p) w -> p kc w", p=128),
               waits=_bar(c))
    bl = c.dma("sync", m.bfc[:], d["b_fc"])
    prev_vr = 0
    for t in range(NT):
        ol = c.dma("sync", m.bs[:, 0:1024].rearrange("p (kc w) -> p kc w", w=128),
                   d["s_outb"].rearrange("(kc p) tk -> p kc tk", p=128)[:, :, t * 128:(t + 1) * 128],
                   waits=[(c.qd["sync"], bl)] if t == 0
                   else [(c.sm["tensor"], c.wv("tensor"))])
        mml = 0
        for half in range(2):
            for kc in range(8):
                tw = []
                if half == 0 and kc == 0:
                    tw = [(c.qd["sync"], ol)]
                    if prev_vr:
                        tw.append((c.sm["scalar"], c.wv("scalar")))
                mml = c.op("tensor", lambda e, half=half, kc=kc: e.matmul(
                    m.psA[:, half * 512:(half + 1) * 512],
                    m.bs[:, kc * 128:(kc + 1) * 128],
                    m.W64[:, kc * 1024 + half * 512:kc * 1024 + (half + 1) * 512],
                    start=(kc == 0), stop=(kc == 7)),
                    waits=tw, mark=(half == 1 and kc == 7))
        ev = c.op("scalar", lambda e: e.activation(
            m.f_h[:, 0:1024], m.psA[:, 0:1024], AF.Copy, bias=0.0, scale=1.0),
            waits=[(c.sm["tensor"], mml)] + ([(c.sm["vector"], prev_vr)] if prev_vr else []),
            mark=True)
        c.op("vector", lambda e: e.tensor_tensor(
            m.f_h[:, 0:1024], m.f_h[:, 0:1024], m.bfc[:], AT.add),
            waits=[(c.sm["scalar"], ev)])
        prev_vr = c.op("vector", lambda e, t=t: e.tensor_tensor(
            m.X[:, t * 1024:(t + 1) * 1024], m.X[:, t * 1024:(t + 1) * 1024],
            m.f_h[:, 0:1024], AT.add), mark=True)


def emit_attn_b(c, m, ncores=8):
    emit_attn_b_ln_qkv(c, m)
    emit_attn_b_premax(c, m, ncores=ncores)
    for h in range(16):
        emit_attn_b_head(c, m, h)
    c.op("sync", lambda e: e.nop(), waits=[(c.qd["scalar"], c.qd["scalar"].n)])
    emit_attn_b_fc(c, m)



def emit_setup(c, m):
    d = m.d
    w = 0
    for dst, src in [(m.ident[:], d["c_ident"]), (m.cosT[:], d["c_cos"]),
                     (m.sinT[:], d["c_sin"]), (m.wr8[:], d["c_wr8"]),
                     (m.wi8[:], d["c_wi8"]), (m.zst[:], d["c_zst"]),
                     (m.iota16[:], d["c_iota16"]), (m.tri[:], d["c_tri"])]:
        w = c.dma("sync", dst, src)
    wm = 0
    for b in range(32):
        ps = m.psA if b < 16 else m.psB
        bb = b % 16
        wm = c.op("tensor", lambda e, ps=ps, bb=bb: e.transpose(
            ps[:, bb * 128:(bb + 1) * 128], m.ident[:], m.ident[:]),
            waits=[(c.qd["sync"], w)] if b == 0 else (),
            mark=(b == 31))
    c.op("scalar", lambda e: e.activation(
        m.A2[:, 0:2048], m.psA[:], AF.Copy, bias=0.0, scale=1.0),
        waits=[(c.sm["tensor"], wm)])
    c.op("scalar", lambda e: e.activation(
        m.A2[:, 0:2048], m.psB[:], AF.Copy, bias=0.0, scale=1.0), mark=True)
    c.op("vector_s", lambda e: e.memset(m.pic[:], 0.0), waits=[(c.qd["sync"], w)])
    c.op("vector_s", lambda e: e.memset(m.cc[:], 0.0))
    c.op("vector_s", lambda e: e.memset(m.pic[:, 2:3], 1.0))
    c.op("vector_s", lambda e: e.tensor_scalar(m.pic[:, 0:1], m.pic[:, 2:3], C52, C52, AT.add, AT.subtract))
    c.op("vector_s", lambda e: e.tensor_scalar(m.pic[:, 1:2], m.pic[:, 0:1], -1.0, None, AT.mult))
    c.op("vector_s", lambda e: e.memset(m.pic[:, 3:4], -4.0), mark=True)


def build(stage="full", ncores=8):
    c = Ctx(stage)
    m = Mem(c)
    emit_setup(c, m)
    d = m.d
    if stage in ("wq", "ffn_only", "attn_b_only") or stage.startswith("headb"):
        wx = 0
        for t in range(NT):
            wx = c.dma("sync", m.X[:, t * 1024:(t + 1) * 1024],
                       d["x"][t * 128:(t + 1) * 128, :],
                       waits=_bar(c, dma=False) if t == 0 else ())
        c.op("vector_s", lambda e: e.engine_nop(),
             waits=[(c.qd["sync"], wx)], mark=True)
        if stage.startswith("headb"):
            hh = int(stage[5:])
            emit_attn_b_ln_qkv(c, m)
            for h2 in range(hh + 1):
                emit_attn_b_head(c, m, h2)
            w = _bar(c)
            c.dma("sync", d["dbg"][:, 0:512], m.b_a2[:], waits=w)
            c.dma("sync", d["dbg"][:, 512:1024], m.b_z[:])
            c.dma("sync", d["dbg"][:, 1024:1152], m.bY[:])
            c.dma("sync", d["dbg"][:, 1152:1280], m.bZ2[:])
            c.dma("sync", d["dbg"][0:64, 1280:1536], m.bql[:])
            c.dma("sync", d["dbg"][0:64, 1536:1792], m.bkl[:])
            c.dma("sync", d["dbg"][:, 2048:6144], m.a1T[:])
            fin = c.dma("sync", d["dbg"][:, 6144:7168], m.f_h[:, 0:1024])
            c.op("sync", lambda e: e.nop(), waits=[(c.qd["sync"], fin)])
            return c, m
        if stage == "attn_b_only":
            emit_attn_b(c, m, ncores=ncores)
            prev = 0
            for t in range(NT):
                prev = c.dma("scalar", d["out"][t * 128:(t + 1) * 128, :],
                             m.X[:, t * 1024:(t + 1) * 1024],
                             waits=_bar(c, dma=False) if t == 0 else ())
            c.op("sync", lambda e: e.nop(), waits=[(c.qd["scalar"], prev)])
            return c, m
        emit_wquant(c, m)
        if stage == "wq":
            fin = c.dma("sync", d["dbg"][:, 4096:4204], m.cc[:],
                        waits=_bar(c))
            c.op("sync", lambda e: e.nop(), waits=[(c.qd["sync"], fin)])
        else:
            emit_ffn(c, m, final=True)
        return c, m
    emit_attn_a(c, m)
    if stage not in ("full", "ffn1"):
        if stage != "attn_a":
            w = [(c.sm[e], c.wv(e)) for e in ("vector", "scalar", "gpsimd", "tensor")]
            c.dma("sync", m.d["dbg"][:, 0:2048], m.st[0][:], waits=w)
            c.dma("sync", m.d["dbg"][:, 2048:4096], m.st[1][:])
            c.dma("sync", m.d["dbg"][:, 4096:4204], m.cc[:])
            fin = c.dma("sync", m.d["dbg"][:, 6144:8192], m.Eb[:])
            c.op("sync", lambda e: e.nop(), waits=[(c.qd["sync"], fin)])
        return c, m
    emit_wquant(c, m)
    emit_ffn(c, m, final=(stage == "ffn1"))
    if stage == "ffn1":
        return c, m
    emit_attn_b(c, m, ncores=ncores)
    if stage == "attn_b":
        prev = 0
        for t in range(NT):
            prev = c.dma("scalar", d["out"][t * 128:(t + 1) * 128, :],
                         m.X[:, t * 1024:(t + 1) * 1024],
                         waits=_bar(c, dma=False) if t == 0 else ())
        c.op("sync", lambda e: e.nop(), waits=[(c.qd["scalar"], prev)])
        return c, m
    emit_ffn(c, m, final=True)
    return c, m


def kernel(**inputs):
    consts = host_constants()
    wts = host_weights(inputs)
    c, m = build("full")
    nc = c.materialize()
    in_maps = []
    for b in range(8):
        im = {"x": np.ascontiguousarray(inputs["L0_attn_a_x"][b])}
        im.update(wts)
        im.update(consts)
        in_maps.append(im)
    res = run_bass_kernel_spmd(nc, in_maps, core_ids=list(range(8)))
    return np.stack([r["out"] for r in res.results]).astype(np.float32)


# version 08 (final)
# speedup vs baseline: 1.1813x; 1.1179x over previous
import math
import numpy as np
import ml_dtypes
import concourse.bass as bass
import concourse.mybir as mybir
from concourse.bass_utils import run_bass_kernel_spmd

F32 = mybir.dt.float32
BF16 = mybir.dt.bfloat16
AT = mybir.AluOpType
AF = mybir.ActivationFunctionType

T, D, NT = 2048, 1024, 16
H, DH, KF = 16, 64, 32
KK = 100
NEG = -1.0e9
C23 = float(1.5 * 2.0 ** 23)
C52 = float(1.5 * 2.0 ** 52)
ZR = 92.0
NEWTON = 4
SQ2PI = float(math.sqrt(math.pi / 2.0))

ENGS = ("tensor", "vector", "scalar", "gpsimd", "sync")



def _riemann_zeros(K):
    n = np.arange(1, K + 1, dtype=np.float64)
    t = 10.0 + 6.0 * n
    for _ in range(60):
        f = t / (2 * np.pi) * np.log(t / (2 * np.pi * np.e)) - (n - 0.875)
        fp = np.log(t / (2 * np.pi)) / (2 * np.pi)
        t = t - f / fp
    return t


def _norm_ppf(p):
    a = [-3.969683028665376e+01, 2.209460984245205e+02, -2.759285104469687e+02,
         1.383577518672690e+02, -3.066479806614716e+01, 2.506628277459239e+00]
    b = [-5.447609879822406e+01, 1.615858368580409e+02, -1.556989798598866e+02,
         6.680131188771972e+01, -1.328068155288572e+01]
    cc = [-7.784894002430293e-03, -3.223964580411365e-01, -2.400758277161838e+00,
          -2.549732539343734e+00, 4.374664141464968e+00, 2.938163982698783e+00]
    dd = [7.784695709041462e-03, 3.224671290700398e-01, 2.445134137142996e+00,
          3.754408661907416e+00]
    p = np.asarray(p, dtype=np.float64)
    out = np.empty_like(p)
    pl = 0.02425
    lo = p < pl
    hi = p > 1 - pl
    mid = ~(lo | hi)
    q = np.sqrt(-2 * np.log(np.where(lo, p, 0.5)))
    out[lo] = ((((((cc[0] * q + cc[1]) * q + cc[2]) * q + cc[3]) * q + cc[4]) * q + cc[5]) /
               ((((dd[0] * q + dd[1]) * q + dd[2]) * q + dd[3]) * q + 1))[lo]
    q = np.sqrt(-2 * np.log(np.where(hi, 1 - p, 0.5)))
    out[hi] = -(((((((cc[0] * q + cc[1]) * q + cc[2]) * q + cc[3]) * q + cc[4]) * q + cc[5]) /
                ((((dd[0] * q + dd[1]) * q + dd[2]) * q + dd[3]) * q + 1)))[hi]
    q = p - 0.5
    r = q * q
    out[mid] = (((((((a[0] * r + a[1]) * r + a[2]) * r + a[3]) * r + a[4]) * r + a[5]) * q) /
                (((((b[0] * r + b[1]) * r + b[2]) * r + b[3]) * r + b[4]) * r + 1))[mid]
    return out


def host_constants():
    F = np.float32
    g = _riemann_zeros(KF)
    g = g / g[0]
    denom = 0.25 + g * g
    w_re = (0.5 / denom).astype(F)
    w_im = (-g / denom).astype(F)
    pos = np.arange(T, dtype=F)
    k_idx = np.arange(KF, dtype=F)
    inv_freq = 10000.0 ** (-k_idx / KF)
    phase = pos[:, None] * inv_freq[None, :]
    cos_p = np.cos(phase).astype(F)
    sin_p = np.sin(phase).astype(F)
    log1p = np.log1p(pos)
    log_decay = (0.5 * (log1p[None, :] - log1p[:, None])).astype(F)

    c = {}
    c["c_ident"] = np.eye(128, dtype=F)
    cos2 = np.concatenate([cos_p, cos_p], axis=1)
    sin2 = np.concatenate([sin_p, sin_p], axis=1)
    c["c_cos"] = np.ascontiguousarray(
        cos2.reshape(NT, 128, 64).transpose(1, 0, 2).reshape(128, NT * 64))
    c["c_sin"] = np.ascontiguousarray(
        sin2.reshape(NT, 128, 64).transpose(1, 0, 2).reshape(128, NT * 64))
    wr2 = np.concatenate([w_re / 8, w_re / 8]).astype(F)
    wi2 = np.concatenate([w_im / 8, w_im / 8]).astype(F)
    c["c_wr8"] = np.broadcast_to(wr2, (128, 64)).copy()
    c["c_wi8"] = np.broadcast_to(wi2, (128, 64)).copy()
    c["c_logdecay"] = log_decay
    lens = np.arange(1, T + 1, dtype=np.float64)
    frac = np.clip(ZR / np.maximum(lens, KK + 1.0), 1e-9, 1 - 1e-9)
    za = _norm_ppf(1 - frac)
    dens = np.maximum(np.exp(-0.5 * za * za) / math.sqrt(2 * math.pi) * lens, 1.0)
    zt = np.zeros((T, 8), dtype=F)
    zt[:, 0] = za
    zt[:, 1] = 1.0 / dens
    zt[:, 2] = (lens <= KK).astype(F)
    zt[:, 3] = (lens <= 160).astype(F)
    c["c_zst"] = np.ascontiguousarray(
        zt.reshape(NT, 128, 8).transpose(1, 0, 2).reshape(128, NT * 8))
    c["c_iota16"] = np.broadcast_to(np.arange(1, 17, dtype=F), (128, 16)).copy()
    ii = np.arange(128)
    c["c_tri"] = np.where(ii[:, None] >= ii[None, :], 0.0, NEG).astype(F)
    return c


def host_weights(W):
    out = {}
    def perm_qk(w):
        w = w.reshape(D, H // 2, 2, 2, KF)
        return None
    def perm(w):
        w4 = w.reshape(D, H // 2, 2, KF, 2)
        w4 = w4.transpose(0, 1, 2, 4, 3)
        return np.ascontiguousarray(w4.reshape(D, D))
    out["Wq"] = perm(W["L0_attn_a_Wq"])
    out["Wk"] = perm(W["L0_attn_a_Wk"])
    out["Wv"] = np.ascontiguousarray(W["L0_attn_a_Wv"])
    out["Wo"] = np.ascontiguousarray(W["L0_attn_a_Wo"])
    out["w_qkv"] = np.ascontiguousarray(W["L0_attn_b_w_qkv"])
    out["w_fc"] = np.ascontiguousarray(W["L0_attn_b_w_fc"])
    out["b_fc"] = np.ascontiguousarray(
        np.repeat(W["L0_attn_b_b_fc"].reshape(1, D), 128, axis=0))
    out["w_conv"] = np.ascontiguousarray(
        np.repeat(W["L0_attn_b_w_conv"][:, 0, :, 0], 64, axis=0))
    out["wg_t"] = np.ascontiguousarray(W["L0_ffn_w_gate"].T)
    out["wu_t"] = np.ascontiguousarray(W["L0_ffn_w_up"].T)
    out["wd_t"] = np.ascontiguousarray(W["L0_ffn_w_down"].T)
    return out



class Sem:
    __slots__ = ("h", "n")

    def __init__(self, nc, name):
        self.h = nc.alloc_semaphore(name)
        self.n = 0


class Ctx:
    def __init__(self, stage="full"):
        self.stage = stage
        self.nc = bass.Bass("TRN2", target_bir_lowering=False,
                            detect_race_conditions=False)
        self.q = {e: [] for e in ENGS}
        self.sm = {e: Sem(self.nc, "sm_" + e) for e in ENGS}
        self.qd = {"sync": Sem(self.nc, "q_sync"),
                   "gpsimd": Sem(self.nc, "q_gp"),
                   "scalar": Sem(self.nc, "q_act")}

    def op(self, eng, fn, waits=(), mark=False, small=False, fresh=False,
           nohz=False):
        if eng == "vector_s":
            eng, small = "vector", True
        for s, v in waits:
            assert v <= s.n, f"wait-on-future {v}>{s.n} eng={eng}"
        inc = None
        if mark:
            self.sm[eng].n += 1
            inc = (self.sm[eng], 1)
        self.q[eng].append((fn, list(waits), inc, small, fresh, nohz))
        return self.sm[eng].n

    def dma(self, eng, dst, src, waits=()):
        for s, v in waits:
            assert v <= s.n
        qs = self.qd[eng]
        qs.n += 16
        self.q[eng].append(
            (lambda e, d=dst, s_=src: e.dma_start(d, s_), list(waits),
             (qs, 16), False, False, True))
        return qs.n

    def wv(self, eng):
        return self.sm[eng].n

    def materialize(self):
        nc = self.nc
        hz = {e: Sem(nc, "hz_" + e) for e in ("vector", "gpsimd")}
        with nc.Block() as blk:
            for ename in ENGS:
                stream = self.q[ename]

                def run(e, stream=stream, ename=ename):
                    last = {}
                    hzs = hz.get(ename)
                    own = self.sm[ename]
                    hzc = 0
                    mkc = 0
                    lsync = None
                    n = len(stream)
                    for i, (fn, waits, inc, small, fresh, nohz) in enumerate(stream):
                        if small and hzs is not None and lsync is not None:
                            s, v = lsync
                            if last.get(id(s), -1) < v:
                                last[id(s)] = v
                                e.wait_ge(s.h, v)
                        for s, v in waits:
                            if not fresh and last.get(id(s), -1) >= v:
                                continue
                            last[id(s)] = max(last.get(id(s), -1), v)
                            e.wait_ge(s.h, v)
                        ins = fn(e)
                        if inc is not None:
                            ins.then_inc(inc[0].h, inc[1])
                            if inc[0] is own:
                                mkc += inc[1]
                                lsync = (inc[0], mkc)
                        elif hzs is not None and not nohz:
                            hzc += 1
                            ins.then_inc(hzs.h, 1)
                            lsync = (hzs, hzc)

                getattr(blk, ename)(run)
        return nc


def sbt(nc, name, shape, dtype, off):
    return nc.alloc_sbuf_tensor_at(name, shape, dtype, offset=off)



class Mem:
    def __init__(self, c):
        nc = c.nc
        d = {}
        d["x"] = nc.dram_tensor("x", [T, D], F32, kind="ExternalInput")
        for n, shp in [("Wq", [D, D]), ("Wk", [D, D]), ("Wv", [D, D]),
                       ("Wo", [D, D]), ("w_qkv", [D, 3 * D]), ("w_fc", [D, D]),
                       ("b_fc", [128, D]), ("wg_t", [D, 4096]), ("wu_t", [D, 4096]),
                       ("wd_t", [4096, D]), ("w_conv", [D, 13]),
                       ("c_ident", [128, 128]), ("c_cos", [128, NT * 64]),
                       ("c_sin", [128, NT * 64]), ("c_wr8", [128, 64]),
                       ("c_wi8", [128, 64]), ("c_logdecay", [T, T]),
                       ("c_zst", [128, NT * 8]), ("c_iota16", [128, 16]),
                       ("c_tri", [128, 128])]:
            d[n] = nc.dram_tensor(n, shp, F32, kind="ExternalInput")
        d["out"] = nc.dram_tensor("out", [T, D], F32, kind="ExternalOutput")
        d["dbg"] = nc.dram_tensor("dbg", [128, 8192], F32, kind="ExternalOutput")
        d["s_xlnT"] = nc.dram_tensor("s_xlnT", [D, T], F32)
        d["s_att"] = nc.dram_tensor("s_att", [D, T], F32)
        d["s_qg"] = nc.dram_tensor("s_qg", [D, 4096], BF16)
        d["s_qu"] = nc.dram_tensor("s_qu", [D, 4096], BF16)
        d["s_qd"] = nc.dram_tensor("s_qd", [4096, D], BF16)
        d["s_qkvb"] = nc.dram_tensor("s_qkvb", [3 * D, T], F32)
        d["s_outb"] = nc.dram_tensor("s_outb", [D, T], F32)
        d["s_hg"] = nc.dram_tensor("s_hg", [T, 4096], F32)
        d["s_xq"] = nc.dram_tensor("s_xq", [D, T], BF16)
        d["s_qhT"] = nc.dram_tensor("s_qhT", [4096, T], BF16)
        d["s_gmax"] = nc.dram_tensor("s_gmax", [1, 2], F32)
        d["s_gmax2"] = nc.dram_tensor("s_gmax2", [1, 2], F32)
        self.d = {k: v.ap() for k, v in d.items()}

        arena = nc.alloc_sbuf_tensor("arena_reserve", [128, 0x2F400], mybir.dt.uint8)
        base = None
        for al in nc.m.functions[0].allocations:
            if isinstance(al, mybir.MemoryLocationSet):
                for ml in al.memorylocations:
                    if ml.name.startswith("arena_reserve"):
                        base = ml.addr
        assert base is not None and base % 32 == 0, base
        self.base = base

        def sbt(nc_, name, shape, dtype, off, _b=base):
            return nc_.alloc_sbuf_tensor_at(name, shape, dtype, offset=_b + off)
        self.ident = sbt(nc, "ident", [128, 128], F32, 0x00000)
        self.cosT = sbt(nc, "cosT", [128, NT * 64], F32, 0x00200)
        self.sinT = sbt(nc, "sinT", [128, NT * 64], F32, 0x01200)
        self.wr8 = sbt(nc, "wr8", [128, 64], F32, 0x02200)
        self.wi8 = sbt(nc, "wi8", [128, 64], F32, 0x02300)
        self.zst = sbt(nc, "zst", [128, NT * 8], F32, 0x02400)
        self.iota16 = sbt(nc, "iota16", [128, 16], F32, 0x02600)
        self.tri = sbt(nc, "tri", [128, 128], F32, 0x03600)
        self.pic = sbt(nc, "pic", [128, 4], F32, 0x02640)
        self.cc = sbt(nc, "cc", [128, 108], F32, 0x02660)
        self.bfc = sbt(nc, "bfc", [128, D], F32, 0x02820)

        A = 0x03820
        self.A = A
        self.pQ = [sbt(nc, f"pQ{i}", [128, T], F32, A + i * 0x6000) for i in range(4)]
        self.pK = [sbt(nc, f"pK{i}", [128, T], F32, A + i * 0x6000 + 0x2000) for i in range(4)]
        self.pV = [sbt(nc, f"pV{i}", [128, T], F32, A + i * 0x6000 + 0x4000) for i in range(4)]
        self.X = sbt(nc, "X", [128, NT * 1024], F32, A)
        self.A2 = sbt(nc, "A2", [128, 8192], F32, A + 0x10000)

        WB = A + 0x18000
        self.WB = WB
        self.dcy = sbt(nc, "dcy", [128, T], F32, WB)
        self.st = [sbt(nc, "st0", [128, T], F32, WB + 0x2000),
                   sbt(nc, "st1", [128, T], F32, WB + 0x4000)]
        self.Eb = sbt(nc, "Eb", [128, T], F32, WB + 0x6000)
        self.Pb = sbt(nc, "Pb", [128, T], F32, WB + 0x8000)
        self.pT = sbt(nc, "pT", [128, T], F32, WB + 0xA000)
        self.msc = sbt(nc, "msc", [128, T], F32, WB + 0xC000)
        self.msc2 = sbt(nc, "msc2", [128, T], F32, WB + 0xE000)
        self.W64 = sbt(nc, "W64", [128, NT * 1024], F32, WB)
        self.xlnT = sbt(nc, "xlnT", [128, NT * 1024], F32, WB)

        TL = WB + 0x10000
        self.rtmp = sbt(nc, "rtmp", [128, 256], F32, TL)
        self.avst = sbt(nc, "avst", [128, 256], F32, TL + 0x400)
        self.wproj = sbt(nc, "wproj", [128, 3072], F32, TL + 0x800)
        self.wproj2 = self.wproj

        self.wq_sb = sbt(nc, "wq_sb", [128, 32768], BF16, WB)
        self.wqbf = sbt(nc, "wqbf", [128, 4096], BF16, WB + 0xC000)
        A2b = A + 0x10000
        self.f_h = sbt(nc, "f_h", [128, 4096], F32, A2b)
        self.qhT_bf = sbt(nc, "qhT_bf", [128, 4096], BF16, A2b + 0x4000)
        self.xqT_bf = sbt(nc, "xqT_bf", [128, 1024], BF16, A2b + 0x6000)
        self.f_stage = sbt(nc, "f_stage", [128, 1024], F32, A2b + 0x6800)

        self.bq = sbt(nc, "bq", [64, T], F32, WB)
        self.bk = sbt(nc, "bk", [64, T], F32, WB + 0x2000)
        self.bv = sbt(nc, "bv", [64, T], F32, WB + 0x4000)
        self.a1T = sbt(nc, "a1T", [128, 4096], F32, WB + 0x6000)
        self.a3T = sbt(nc, "a3T", [128, 4096], F32, WB + 0xA000)
        self.bs = sbt(nc, "bs", [128, T], F32, WB + 0xE000)
        self.b_a2 = sbt(nc, "b_a2", [128, 512], F32, A2b + 0x4000)
        self.b_z = sbt(nc, "b_z", [128, 512], F32, A2b + 0x4800)
        self.b_az = sbt(nc, "b_az", [128, 512], F32, A2b + 0x5000)
        self.b_t = sbt(nc, "b_t", [128, 512], F32, A2b + 0x5800)
        self.b_aT = sbt(nc, "b_aT", [128, 512], F32, A2b + 0x6000)
        self.b_azT = sbt(nc, "b_azT", [128, 512], F32, A2b + 0x6800)
        self.b_zT = sbt(nc, "b_zT", [128, 512], F32, A2b + 0x7000)
        self.bql = sbt(nc, "bql", [64, 256], F32, TL + 0x800)
        self.bkl = sbt(nc, "bkl", [64, 256], F32, TL + 0xC00)
        self.wcv = sbt(nc, "wcv", [64, 16], F32, TL + 0x1000)
        self.bY = sbt(nc, "bY", [128, 128], F32, TL + 0x1040)
        self.bZ2 = sbt(nc, "bZ2", [128, 128], F32, TL + 0x1240)

        self.psA = nc.alloc_psum_tensor("psA", [128, 2048], F32)
        self.psB = nc.alloc_psum_tensor("psB", [128, 2048], F32)


CC = dict(t=0, c=2, lo=4, hi=6, chi=8, tperf=10, hasperf=12, mu=14, sig=16,
          g=18, gn=20, pf=22, negt=24, sk=26, Z=28, craw=30, r=32,
          u1=34, u2=36, u3=38, lnS=40, lnQ=42, lnmu=44, lnvar=46, lnr=48,
          lnt1=50, lnt2=52, t16a=56, t16b=72, qv=88)


def ccs(m, name, k=2):
    base = CC[name]
    return m.cc[:, base:base + k]



def vsel(c, m, out, g, a, b):
    k = out.shape[1]
    u3 = ccs(m, "u3")[:, 0:k]
    c.op("vector_s", lambda e: e.tensor_tensor(u3, a, b, AT.subtract))
    c.op("vector_s", lambda e: e.tensor_tensor(u3, g, u3, AT.mult))
    c.op("vector_s", lambda e: e.tensor_tensor(out, b, u3, AT.add))


def emit_rstd(c, vcol, out, t1, t2):
    c.op("vector_s", lambda e: e.tensor_scalar(t2, vcol, 1e-5, None, AT.add))
    c.op("vector_s", lambda e: e.tensor_scalar(out, t2, 1.0, 0.5, AT.add, AT.mult))
    for _ in range(NEWTON):
        c.op("vector_s", lambda e: e.reciprocal(t1, out))
        c.op("vector_s", lambda e: e.tensor_tensor(t1, t2, t1, AT.mult))
        c.op("vector_s", lambda e: e.tensor_tensor(out, out, t1, AT.add))
        c.op("vector_s", lambda e: e.tensor_scalar(out, out, 0.5, None, AT.mult))
    return c.op("vector_s", lambda e: e.reciprocal(out, out), mark=True)



def emit_ln_transpose(c, m, src_getter):
    prev_spill = 0
    prev_ve = 0
    for t in range(NT):
        stage = m.msc[:, 0:1024]
        raw = m.msc2[:, 0:1024]
        ldw = src_getter(t, raw, [(c.qd["scalar"], prev_spill)] if prev_spill else [])
        c.op("vector", lambda e, r=raw, o=stage: e.scalar_tensor_tensor(
            o, r, 1.0, r, AT.bypass, AT.bypass, accum_out=ccs(m, "lnS", 1)),
            waits=ldw)
        c.op("vector", lambda e, r=raw, o=stage: e.scalar_tensor_tensor(
            o, r, 1.0, r, AT.bypass, AT.mult, accum_out=ccs(m, "lnQ", 1)))
        c.op("vector_s", lambda e: e.engine_nop())
        c.op("vector_s", lambda e: e.tensor_scalar(ccs(m, "lnmu", 1), ccs(m, "lnS", 1), 1.0 / D, None, AT.mult))
        c.op("vector_s", lambda e: e.tensor_scalar(ccs(m, "lnvar", 1), ccs(m, "lnQ", 1), 1.0 / D, None, AT.mult))
        c.op("vector_s", lambda e: e.tensor_tensor(ccs(m, "lnt1", 1), ccs(m, "lnmu", 1), ccs(m, "lnmu", 1), AT.mult))
        c.op("vector_s", lambda e: e.tensor_tensor(ccs(m, "lnvar", 1), ccs(m, "lnvar", 1), ccs(m, "lnt1", 1), AT.subtract))
        rm = emit_rstd(c, ccs(m, "lnvar", 1), ccs(m, "lnr", 1), ccs(m, "lnt1", 1), ccs(m, "lnt2", 1))
        vn = c.op("vector", lambda e, r=raw, o=stage: e.tensor_scalar(
            o, r, ccs(m, "lnmu", 1), ccs(m, "lnr", 1), AT.subtract, AT.mult),
            waits=[(c.sm["vector"], rm)], mark=True)
        tm = 0
        for b in range(8):
            tw = []
            if b == 0:
                tw = [(c.sm["vector"], vn)]
                if prev_ve:
                    tw.append((c.sm["scalar"], prev_ve))
            tm = c.op("tensor", lambda e, b=b, s=stage: e.transpose(
                m.psA[:, b * 128:(b + 1) * 128], s[:, b * 128:(b + 1) * 128], m.ident[:]),
                waits=tw, mark=(b == 7))
        c.op("scalar", lambda e, r=raw: e.activation(
            r[:, 0:512], m.psA[:, 0:512], AF.Copy, bias=0.0, scale=1.0),
            waits=[(c.sm["tensor"], tm)])
        ve = c.op("scalar", lambda e, r=raw: e.activation(
            r[:, 512:1024], m.psA[:, 512:1024], AF.Copy, bias=0.0, scale=1.0),
            mark=True)
        prev_ve = ve
        dst = m.d["s_xlnT"].rearrange("(cc p) t2 -> p cc t2", p=128)[:, :, t * 128:(t + 1) * 128]
        src = raw.rearrange("p (cc w) -> p cc w", w=128)
        prev_spill = c.dma("scalar", dst, src)



def emit_attn_a(c, m):
    d = m.d

    def src(t, raw, extra):
        w = c.dma("sync", raw, d["x"][t * 128:(t + 1) * 128, :], waits=extra)
        return [(c.qd["sync"], w)]
    emit_ln_transpose(c, m, src)
    ln_done_q = c.qd["scalar"].n

    for g in range(2):
        wx = c.dma("sync", m.xlnT[:].rearrange("p (cc t2) -> p cc t2", t2=T),
                   d["s_xlnT"].rearrange("(cc p) t2 -> p cc t2", p=128),
                   waits=[(c.qd["scalar"], c.qd["scalar"].n),
                          (c.sm["scalar"], c.wv("scalar")),
                          (c.sm["vector"], c.wv("vector")),
                          (c.sm["gpsimd"], c.wv("gpsimd")),
                          (c.sm["tensor"], c.wv("tensor"))])
        xw = [(c.qd["sync"], wx)]
        c.op("gpsimd", lambda e: e.engine_nop(),
             waits=[(c.qd["scalar"], c.qd["scalar"].n)], mark=True)
        for i in range(4):
            emit_project_pair(c, m, g * 4 + i, i, xw)
            if c.stage == "proj":
                return
        for qt in range(NT):
            emit_attn_a_qt(c, m, g, qt)
            if (c.stage == f"qt{qt}" or c.stage.startswith(f"probe{qt}s")
                    or c.stage.startswith(f"fin{qt}s")):
                return
    emit_attn_a_out(c, m)


def emit_project_pair(c, m, p, slot, xw):
    d = m.d
    wp = m.wproj
    ws = []
    for wi, wn in enumerate(("Wq", "Wk", "Wv")):
        dst = wp[:, wi * 1024:(wi + 1) * 1024].rearrange("q (k w) -> q k w", w=128)
        src = d[wn][:, p * 128:(p + 1) * 128].rearrange("(k q) w -> q k w", q=128)
        ws.append(c.dma("sync", dst, src,
                        waits=list(xw) + [(c.sm["tensor"], c.wv("tensor"))]))
    wld = [(c.qd["sync"], ws[-1])]
    outs = (m.pQ[slot], m.pK[slot], m.pV[slot])
    prev_ev = 0
    for t in range(NT):
        mmlast = 0
        for wi in range(3):
            for kc in range(8):
                tw = []
                if wi == 0 and kc == 0:
                    if t == 0:
                        tw += wld
                    if prev_ev:
                        tw.append((c.sm["scalar"], prev_ev))
                mmlast = c.op("tensor", lambda e, wi=wi, kc=kc, t=t: e.matmul(
                    m.psA[:, wi * 128:(wi + 1) * 128],
                    m.xlnT[:, kc * 2048 + t * 128: kc * 2048 + (t + 1) * 128],
                    wp[:, wi * 1024 + kc * 128: wi * 1024 + (kc + 1) * 128],
                    start=(kc == 0), stop=(kc == 7)),
                    waits=tw,
                    mark=(wi == 2 and kc == 7))
        ev = 0
        for wi in range(3):
            ev = c.op("scalar", lambda e, wi=wi, t=t, o=outs[wi]: e.activation(
                o[:, t * 128:(t + 1) * 128], m.psA[:, wi * 128:(wi + 1) * 128],
                AF.Copy, bias=0.0, scale=1.0),
                waits=[(c.sm["tensor"], mmlast)] if wi == 0 else (),
                mark=(wi == 2))
        prev_ev = ev
        qtile = m.pQ[slot][:, t * 128:(t + 1) * 128]
        ktile = m.pK[slot][:, t * 128:(t + 1) * 128]
        def hv(ap, lohalf):
            v3 = ap.rearrange("p (h c) -> p h c", h=2)
            return v3[:, :, 0:32] if lohalf else v3[:, :, 32:64]
        q_re = hv(qtile, True); q_im = hv(qtile, False)
        k_re = hv(ktile, True); k_im = hv(ktile, False)
        cosb = m.cosT[:, t * 64:(t + 1) * 64].rearrange("p (h c) -> p h c", h=2)
        sinb = m.sinT[:, t * 64:(t + 1) * 64].rearrange("p (h c) -> p h c", h=2)
        wr = m.wr8[:].rearrange("p (h c) -> p h c", h=2)
        wi = m.wi8[:].rearrange("p (h c) -> p h c", h=2)
        def h3(ap):
            return ap.rearrange("p (h c) -> p h c", h=2)
        s0 = h3(m.rtmp[:, 0:64]); s1 = h3(m.rtmp[:, 64:128])
        s2 = h3(m.rtmp[:, 128:192]); s3 = h3(m.rtmp[:, 192:256])
        a0 = h3(m.avst[:, 0:64]); a1 = h3(m.avst[:, 64:128])
        a2 = h3(m.avst[:, 128:192]); a3 = h3(m.avst[:, 192:256])
        aw = [(c.sm["scalar"], ev)]
        def vop(o, i0, i1, alu, w=()):
            return c.op("vector_s", lambda e, o=o, i0=i0, i1=i1, alu=alu: e.tensor_tensor(o, i0, i1, alu), waits=w)
        def gop(o, i0, i1, alu, w=(), mark=False):
            return c.op("gpsimd", lambda e, o=o, i0=i0, i1=i1, alu=alu: e.tensor_tensor(o, i0, i1, alu), waits=w, mark=mark, small=True)
        vop(s0, cosb, q_re, AT.mult, aw)
        vop(s2, sinb, q_im, AT.mult)
        vop(s0, s0, s2, AT.add)
        vop(s1, cosb, q_im, AT.mult)
        vop(s2, sinb, q_re, AT.mult)
        vop(s1, s1, s2, AT.subtract)
        vop(s2, wr, s0, AT.mult)
        vop(s3, wi, s1, AT.mult)
        vop(q_re, s2, s3, AT.subtract)
        vop(s2, wr, s1, AT.mult)
        vop(s3, wi, s0, AT.mult)
        vq = c.op("vector_s", lambda e, o=q_im, i0=s2, i1=s3: e.tensor_tensor(o, i0, i1, AT.add), mark=True)
        gop(a0, cosb, k_re, AT.mult, aw)
        gop(a1, sinb, k_im, AT.mult)
        gop(a2, cosb, k_im, AT.mult)
        gop(a3, sinb, k_re, AT.mult)
        gop(k_re, a0, a1, AT.add)
        vk = gop(k_im, a2, a3, AT.subtract, mark=True)
        for which, (buf, wmk, weng) in enumerate(((m.pQ[slot], vq, "vector"), (m.pK[slot], vk, "gpsimd"))):
            pslot = m.psB[:, 0:128] if which == 0 else m.psB[:, 1536:1664]
            tt = c.op("tensor", lambda e, b=buf, t=t, ps=pslot: e.transpose(
                ps, b[:, t * 128:(t + 1) * 128], m.ident[:]),
                waits=[(c.sm[weng], wmk), (c.sm["scalar"], c.wv("scalar"))],
                mark=True)
            c.op("scalar", lambda e, b=buf, t=t, ps=pslot: e.activation(
                b[:, t * 128:(t + 1) * 128], ps, AF.Copy, bias=0.0, scale=1.0),
                waits=[(c.sm["tensor"], tt)], mark=True)


PROBE_ENG = ["vector"] * 6
PROBE_DENS = 4


def _emit_probes(c, m, W, za, idens, losel):
    t_ = ccs(m, "t")
    cc_ = ccs(m, "c")
    lo = ccs(m, "lo")
    hi = ccs(m, "hi")
    chi = ccs(m, "chi")
    tp = ccs(m, "tperf")
    hp = ccs(m, "hasperf")
    mu = ccs(m, "mu")
    sg = ccs(m, "sig")
    gg = ccs(m, "g")
    gn = ccs(m, "gn")
    pf = ccs(m, "pf")
    nt_ = ccs(m, "negt")
    u1 = ccs(m, "u1")
    u2 = ccs(m, "u2")
    gw = [(c.sm["scalar"], c.wv("scalar")), (c.sm["vector"], c.wv("vector")),
          (c.sm["gpsimd"], c.wv("gpsimd"))]
    c.op("vector_s", lambda e: e.tensor_scalar(mu, mu, 1.0 / W, None, AT.mult), waits=gw)
    c.op("vector_s", lambda e: e.tensor_scalar(sg, sg, SQ2PI / W, None, AT.mult))
    c.op("vector_s", lambda e: e.tensor_scalar(u1, sg, -2.0, None, AT.mult))
    c.op("vector_s", lambda e: e.tensor_tensor(lo, mu, u1, AT.add))
    lsm = c.op("vector_s", lambda e: e.tensor_scalar(u1, lo, 300.0, -1.0, AT.add, AT.mult), mark=True)
    c.op("vector_s", lambda e: e.tensor_scalar(u1, u1, losel, None, AT.mult),
         waits=[(c.sm["vector"], lsm)])
    c.op("vector_s", lambda e: e.tensor_tensor(lo, lo, u1, AT.add))
    c.op("vector_s", lambda e: e.tensor_scalar(u1, sg, 12.0, None, AT.mult))
    c.op("vector_s", lambda e: e.tensor_tensor(hi, mu, u1, AT.add))
    c.op("vector_s", lambda e: e.memset(chi, -1e4))
    c.op("vector_s", lambda e: e.memset(tp, 0.0))
    hpm = c.op("vector_s", lambda e: e.memset(hp, 0.0), mark=True)
    c.op("vector_s", lambda e: e.tensor_scalar(u1, sg, za, None, AT.mult),
         waits=[(c.sm["vector"], hpm)])
    c.op("vector_s", lambda e: e.tensor_tensor(t_, mu, u1, AT.add))
    c.op("vector_s", lambda e: e.tensor_tensor(t_, t_, lo, AT.max))
    c.op("vector_s", lambda e: e.tensor_tensor(t_, t_, hi, AT.min))
    vp = c.op("vector_s", lambda e: e.tensor_scalar(nt_, t_, -1.0, None, AT.mult), mark=True)

    for j in range(len(PROBE_ENG)):
        eng = PROBE_ENG[j]
        if eng == "scalar":
            cl = 0
            for h01 in range(2):
                cl = c.op("scalar", lambda e, h=h01: e.activation(
                    m.msc[:, 0:W], m.st[h][:, 0:W], AF.Sign,
                    bias=m.cc[:, CC["negt"] + h:CC["negt"] + h + 1], scale=1.0,
                    accum_out=m.cc[:, CC["craw"] + h:CC["craw"] + h + 1]),
                    waits=[(c.sm["vector"], vp)], mark=(h01 == 1))
            vp2 = c.op("vector_s", lambda e: e.tensor_scalar(
                cc_, ccs(m, "craw"), float(W), 0.5, AT.add, AT.mult),
                waits=[(c.sm["scalar"], cl)], mark=True)
        else:
            dump = m.msc if eng != "vector" else m.msc2
            cl = 0
            for h01 in range(2):
                cl = c.op(eng, lambda e, h=h01, dp=dump: e.tensor_scalar(
                    dp[:, 0:W], m.st[h][:, 0:W],
                    m.cc[:, CC["t"] + h:CC["t"] + h + 1], None,
                    AT.is_ge, AT.add,
                    accum_out=m.cc[:, CC["c"] + h:CC["c"] + h + 1]),
                    waits=[(c.sm["vector"], vp)], mark=(h01 == 1))
            vp2 = c.op("vector_s", lambda e: e.engine_nop(),
                       waits=[(c.sm[eng], cl)], mark=True)
        c.op("vector_s", lambda e: e.tensor_scalar(gg, cc_, KK - 0.5, None, AT.is_ge))
        c.op("vector_s", lambda e: e.tensor_scalar(gn, gg, -1.0, 1.0, AT.mult, AT.add))
        vsel(c, m, lo, gg, t_, lo)
        vsel(c, m, hi, gn, t_, hi)
        vsel(c, m, chi, gn, cc_, chi)
        c.op("vector_s", lambda e: e.tensor_scalar(pf, cc_, float(KK), None, AT.is_equal))
        vsel(c, m, tp, pf, t_, tp)
        c.op("vector_s", lambda e: e.tensor_tensor(hp, hp, pf, AT.max))
        if j < len(PROBE_ENG) - 1:
            if j < PROBE_DENS - 1:
                c.op("vector_s", lambda e: e.tensor_scalar(u1, cc_, ZR, None, AT.subtract))
                idm = c.op("vector_s", lambda e: e.tensor_tensor(u1, u1, sg, AT.mult), mark=True)
                c.op("vector_s", lambda e: e.tensor_scalar(u1, u1, idens, None, AT.mult),
                     waits=[(c.sm["vector"], idm)])
                c.op("vector_s", lambda e: e.tensor_tensor(t_, t_, u1, AT.add))
                c.op("vector_s", lambda e: e.tensor_tensor(t_, t_, lo, AT.max))
                c.op("vector_s", lambda e: e.tensor_tensor(t_, t_, hi, AT.min))
            else:
                c.op("vector_s", lambda e: e.tensor_tensor(u1, hi, lo, AT.subtract))
                c.op("vector_s", lambda e: e.tensor_scalar(u1, u1, 0.62, None, AT.mult))
                c.op("vector_s", lambda e: e.tensor_tensor(t_, lo, u1, AT.add))
                c.op("vector_s", lambda e: e.tensor_scalar(u2, sg, 12.0, None, AT.mult))
                c.op("vector_s", lambda e: e.tensor_tensor(u2, mu, u2, AT.add))
                c.op("vector_s", lambda e: e.tensor_tensor(t_, t_, u2, AT.min))
            vp = c.op("vector_s", lambda e: e.tensor_scalar(nt_, t_, -1.0, None, AT.mult), mark=True)
        else:
            vp = c.op("vector_s", lambda e: e.engine_nop(), mark=True)



def emit_attn_a_qt(c, m, g, qt):
    d = m.d
    W = 128 * (qt + 1)
    zoff = qt * 8
    za = m.zst[:, zoff + 0:zoff + 1]
    idens = m.zst[:, zoff + 1:zoff + 2]
    ka = m.zst[:, zoff + 2:zoff + 3]
    losel = m.zst[:, zoff + 3:zoff + 4]
    wdc = c.dma("sync", m.msc2[:, 0:W], d["c_logdecay"][qt * 128:(qt + 1) * 128, 0:W],
                waits=[(c.sm["scalar"], c.wv("scalar")),
                       (c.sm["vector"], c.wv("vector")),
                       (c.sm["gpsimd"], c.wv("gpsimd")),
                       (c.sm["tensor"], c.wv("tensor"))])
    dce = c.op("scalar", lambda e: e.activation(m.dcy[:, 0:W], m.msc2[:, 0:W],
                                                AF.Exp, bias=0.0, scale=1.0),
               waits=[(c.qd["sync"], wdc),
                      (c.sm["gpsimd"], c.wv("gpsimd"))], mark=True)

    for slot in range(4):
        p = g * 4 + slot
        for h01 in range(2):
            lhs = m.pQ[slot][64 * h01:64 * (h01 + 1), qt * 128:(qt + 1) * 128]
            stt_ = m.st[h01]
            mml = 0
            for j0 in range(0, W, 512):
                j1 = min(j0 + 512, W)
                mml = c.op("tensor", lambda e, j0=j0, j1=j1, lhs=lhs, s=slot, h=h01: e.matmul(
                    m.psA[:, j0:j1], lhs, m.pK[s][64 * h:64 * (h + 1), j0:j1],
                    start=True, stop=True),
                    waits=[(c.sm["scalar"], c.wv("scalar"))] if j0 == 0 else (),
                    mark=(j1 == W))
            rawb = m.Eb if h01 == 0 else m.Pb
            evl = 0
            for j0 in range(0, W, 512):
                j1 = min(j0 + 512, W)
                evl = c.op("scalar", lambda e, j0=j0, j1=j1, rb=rawb: e.activation(
                    rb[:, j0:j1], m.psA[:, j0:j1], AF.Copy, bias=0.0, scale=1.0),
                    waits=[(c.sm["tensor"], mml)] if j0 == 0 else (),
                    mark=(j1 == W))
            c.op("vector", lambda e, o=stt_, rb=rawb, h=h01: e.scalar_tensor_tensor(
                o[:, 0:W], rb[:, 0:W], 1.0, m.dcy[:, 0:W], AT.mult, AT.mult,
                accum_out=m.cc[:, CC["mu"] + h:CC["mu"] + h + 1]),
                waits=[(c.sm["scalar"], evl), (c.sm["scalar"], dce)])
            vab = c.op("vector", lambda e, o=stt_, h=h01: e.reduce_sum(
                m.cc[:, CC["sig"] + h:CC["sig"] + h + 1], o[:, 0:W],
                axis=mybir.AxisListType.X, apply_absolute_value=True), mark=True)
            gpm = c.op("vector", lambda e, o=stt_: e.tensor_tensor(
                o[:, qt * 128:(qt + 1) * 128], o[:, qt * 128:(qt + 1) * 128],
                m.tri[:], AT.add),
                waits=[(c.sm["vector"], vab)], mark=True)

        _emit_probes(c, m, W, za, idens, losel)

        if c.stage == f"probe{qt}s{slot}":
            return
        for h01 in range(2):
            _emit_finisher_av(c, m, W, slot, h01, ka)
        if c.stage == f"fin{qt}s{slot}":
            return
        dst = m.d["s_att"][p * 128:(p + 1) * 128, qt * 128:(qt + 1) * 128] \
            .rearrange("(h r) w -> r h w", h=2)
        c.last_av_dma = c.dma("scalar", dst,
                              m.avst[0:64, 0:256].rearrange("r (h w) -> r h w", h=2))


def _emit_finisher_av(c, m, W, slot, h01, ka):
    stt_ = m.st[h01]
    ebuf = m.Eb if h01 == 0 else m.Pb
    hicol = m.cc[:, CC["hi"] + h01:CC["hi"] + h01 + 1]
    skcol = m.cc[:, CC["sk"] + h01:CC["sk"] + h01 + 1]
    zcol = m.cc[:, CC["Z"] + h01:CC["Z"] + h01 + 1]
    rcol = m.cc[:, CC["r"] + h01:CC["r"] + h01 + 1]
    hpcol = m.cc[:, CC["hasperf"] + h01:CC["hasperf"] + h01 + 1]
    tpcol = m.cc[:, CC["tperf"] + h01:CC["tperf"] + h01 + 1]
    t16 = m.cc[:, CC["t16a"]:CC["t16a"] + 16]
    gpp = c.op("vector", lambda e: e.tensor_scalar(
        m.msc[:, 0:W], stt_[:, 0:W], hicol, NEG, AT.is_ge, AT.mult),
        waits=[(c.sm["vector"], c.wv("vector"))], mark=True, small=True,
        fresh=True)
    gpp = c.op("vector", lambda e: e.tensor_tensor(
        m.msc[:, 0:W], stt_[:, 0:W], m.msc[:, 0:W], AT.add), mark=True,
        small=True)
    c.op("vector_s", lambda e: e.max(out=t16[:, 0:8], in_=m.msc[:, 0:W]),
         waits=[(c.sm["vector"], gpp)])
    c.op("vector_s", lambda e: e.match_replace(
        out=m.msc2[:, 0:W], in_to_replace=t16[:, 0:8],
        in_values=m.msc[:, 0:W], imm_value=NEG))
    c.op("vector_s", lambda e: e.max(out=t16[:, 8:16], in_=m.msc2[:, 0:W]))
    rnp = c.op("vector_s", lambda e: e.tensor_scalar(
        ccs(m, "r"), ccs(m, "chi"), float(KK), -1.0, AT.subtract, AT.mult), mark=True)
    c.op("vector_s", lambda e: e.scalar_tensor_tensor(
        m.msc2[:, 0:16], m.iota16[:], rcol, t16,
        AT.is_equal, AT.mult, accum_out=skcol),
        waits=[(c.sm["vector"], rnp)])
    vsel(c, m, skcol, hpcol, tpcol, skcol)
    c.op("vector_s", lambda e: e.memset(ccs(m, "u1", 1), -7e8))
    vsel(c, m, skcol, ka, ccs(m, "u1", 1), skcol)
    c.op("vector_s", lambda e: e.engine_nop(), mark=True)
    vfin = c.op("vector_s", lambda e: e.engine_nop(), mark=True)
    c.op("gpsimd", lambda e: e.tensor_scalar(
        m.msc2[:, 0:W], stt_[:, 0:W], skcol, NEG, AT.is_lt, AT.mult),
        waits=[(c.sm["vector"], vfin)], small=True)
    gpk = c.op("gpsimd", lambda e: e.tensor_tensor(
        stt_[:, 0:W], stt_[:, 0:W], m.msc2[:, 0:W], AT.add), mark=True,
        small=True)
    aex = c.op("scalar", lambda e: e.activation(
        ebuf[:, 0:W], stt_[:, 0:W], AF.Exp, bias=m.pic[:, 3:4], scale=1.0),
        waits=[(c.sm["gpsimd"], gpk)], mark=True)
    c.op("vector", lambda e: e.tensor_scalar(
        m.msc[:, 0:W], ebuf[:, 0:W], 0.0, None, AT.add, AT.add,
        accum_out=zcol),
        waits=[(c.sm["scalar"], aex)], mark=True)
    c.op("vector_s", lambda e: e.engine_nop())
    c.op("vector_s", lambda e: e.reciprocal(zcol, zcol), mark=True)
    vsc = c.wv("vector")
    asc = c.op("scalar", lambda e: e.activation(
        stt_[:, 0:W], ebuf[:, 0:W], AF.Copy, bias=0.0, scale=zcol),
        waits=[(c.sm["vector"], vsc)], mark=True)
    nblk = W // 128
    avps = m.psB[0:64, 512:640] if h01 == 0 else m.psB[0:64, 1024:1152]
    for b0 in range(0, nblk, 4):
        bN = min(b0 + 4, nblk)
        ps = m.psB[:, 0:512] if (b0 // 4) % 2 == 0 else m.psB[:, 1536:2048]
        tl = 0
        for b in range(b0, bN):
            tl = c.op("tensor", lambda e, b=b, b0=b0, ps=ps: e.transpose(
                ps[:, (b - b0) * 128:(b - b0 + 1) * 128],
                stt_[:, b * 128:(b + 1) * 128], m.ident[:]),
                waits=[(c.sm["scalar"], asc)] if b == b0 else (),
                mark=(b == bN - 1))
        el = c.op("scalar", lambda e, b0=b0, bN=bN, ps=ps: e.activation(
            m.pT[:, b0 * 128:bN * 128], ps[:, 0:(bN - b0) * 128],
            AF.Copy, bias=0.0, scale=1.0),
            waits=[(c.sm["tensor"], tl)], mark=True)
        for b in range(b0, bN):
            c.op("tensor", lambda e, b=b, ps=ps: e.matmul(
                avps, m.pV[slot][:, b * 128 + 64 * h01:b * 128 + 64 * (h01 + 1)],
                m.pT[:, b * 128:(b + 1) * 128],
                start=(b == 0), stop=(b == nblk - 1)),
                waits=[(c.sm["scalar"], el)], mark=(b == bN - 1))
    avm = c.wv("tensor")
    avw = [(c.sm["tensor"], avm)]
    if h01 == 0 and getattr(c, "last_av_dma", 0):
        avw.append((c.qd["scalar"], c.last_av_dma))
    c.op("scalar", lambda e: e.activation(
        m.avst[0:64, h01 * 128:(h01 + 1) * 128], avps, AF.Copy,
        bias=0.0, scale=1.0),
        waits=avw, mark=True)


def emit_attn_a_out(c, m):
    d = m.d
    wo = m.W64
    wl = c.dma("sync", wo[:, 0:8192].rearrange("q (k w) -> q k w", w=1024),
               d["Wo"].rearrange("(k q) w -> q k w", q=128),
               waits=[(c.sm["scalar"], c.wv("scalar")),
                      (c.sm["vector"], c.wv("vector")),
                      (c.sm["gpsimd"], c.wv("gpsimd")),
                      (c.sm["tensor"], c.wv("tensor")),
                      (c.qd["scalar"], c.qd["scalar"].n)])
    prev_vr = 0
    for t in range(NT):
        ps = m.psA[:, 0:1024] if t % 2 == 0 else m.psA[:, 1024:2048]
        wa = c.dma("sync", m.wproj[:, 0:1024].rearrange("q (k w) -> q k w", w=128),
                   d["s_att"].rearrange("(k q) t2 -> q k t2", q=128)[:, :, t * 128:(t + 1) * 128],
                   waits=[(c.qd["sync"], wl)] if t == 0
                   else [(c.sm["tensor"], c.wv("tensor"))])
        wxr = c.dma("sync", m.msc[:, 1024:2048],
                    d["x"][t * 128:(t + 1) * 128, :],
                    waits=[(c.sm["vector"], prev_vr)] if prev_vr else ())
        mml = 0
        for half in range(2):
            for kc in range(8):
                tw = []
                if half == 0 and kc == 0:
                    tw = [(c.qd["sync"], wa)]
                    if prev_vr:
                        tw.append((c.sm["vector"], prev_vr))
                mml = c.op("tensor", lambda e, kc=kc, half=half, ps=ps: e.matmul(
                    ps[:, half * 512:(half + 1) * 512],
                    m.wproj[:, kc * 128:(kc + 1) * 128],
                    wo[:, kc * 1024 + half * 512: kc * 1024 + (half + 1) * 512],
                    start=(kc == 0), stop=(kc == 7)),
                    waits=tw, mark=(half == 1 and kc == 7))
        vr = c.op("vector", lambda e, t=t, ps=ps: e.tensor_tensor(
            m.X[:, t * 1024:(t + 1) * 1024], m.msc[:, 1024:2048], ps, AT.add),
            waits=[(c.sm["tensor"], mml), (c.qd["sync"], wxr)], mark=True)
        prev_vr = vr
        if c.stage == "attn_a":
            c.dma("scalar", d["out"][t * 128:(t + 1) * 128, :],
                  m.X[:, t * 1024:(t + 1) * 1024],
                  waits=[(c.sm["vector"], vr)])
    if c.stage == "attn_a":
        fin = c.qd["scalar"].n
        c.op("sync", lambda e: e.nop(), waits=[(c.qd["scalar"], fin)])



WGI, WUI, WDI = 104, 105, 106
INV127 = 1.0 / 127.0


def _bar(c, dma=True):
    w = [(c.sm[e], c.wv(e)) for e in ("vector", "scalar", "gpsimd", "tensor")]
    if dma:
        w += [(c.qd[q], c.qd[q].n) for q in ("sync", "scalar", "gpsimd")]
    return w


def emit_wquant(c, m):
    d = m.d
    onesr = m.rtmp[0:1, 128:256]
    onesc = m.rtmp[:, 130:131]
    c.op("vector_s", lambda e: e.memset(m.rtmp[0:1, 128:256], 1.0), waits=_bar(c))
    c.op("vector_s", lambda e: e.memset(m.rtmp[:, 130:131], 1.0), mark=True)
    sA = m.W64[:, 0:4096]
    sB = m.W64[:, 4096:8192]
    sC = m.W64[:, 8192:12288]
    for wn, qn, rows, invcol in (("wg_t", "s_qg", 1024, WGI),
                                 ("wu_t", "s_qu", 1024, WUI),
                                 ("wd_t", "s_qd", 4096, WDI)):
        nt = rows // 128
        cols = 4096 if rows == 1024 else 1024
        prev = c.op("vector_s", lambda e: e.memset(m.cc[:, 0:2], 0.0), mark=True)
        for i in range(nt):
            wld = c.dma("sync", sA[:, 0:cols], d[wn][i * 128:(i + 1) * 128, :],
                        waits=[(c.sm["vector"], prev)])
            c.op("vector", lambda e, cols=cols: e.reduce_sum(
                m.cc[:, 1:2], sA[:, 0:cols], axis=mybir.AxisListType.X,
                apply_absolute_value=True),
                waits=[(c.qd["sync"], wld)])
            prev = c.op("vector_s", lambda e: e.tensor_tensor(
                m.cc[:, 0:1], m.cc[:, 0:1], m.cc[:, 1:2], AT.add), mark=True)
        mm1 = c.op("tensor", lambda e: e.matmul(
            m.psA[0:1, 0:1], m.cc[:, 0:1], onesc, start=True, stop=True),
            waits=[(c.sm["vector"], prev)], mark=True)
        c.op("vector_s", lambda e: e.tensor_scalar(
            m.cc[0:1, 2:3], m.psA[0:1, 0:1], float(2.0 ** -22), 1e-5,
            AT.mult, AT.max),
            waits=[(c.sm["tensor"], mm1)])
        c.op("vector_s", lambda e: e.reciprocal(m.cc[0:1, 3:4], m.cc[0:1, 2:3]))
        for _ in range(2):
            c.op("vector_s", lambda e: e.tensor_tensor(
                m.cc[0:1, 4:5], m.cc[0:1, 2:3], m.cc[0:1, 3:4], AT.mult))
            c.op("vector_s", lambda e: e.tensor_scalar(
                m.cc[0:1, 4:5], m.cc[0:1, 4:5], -1.0, 2.0, AT.mult, AT.add))
            c.op("vector_s", lambda e: e.tensor_tensor(
                m.cc[0:1, 3:4], m.cc[0:1, 3:4], m.cc[0:1, 4:5], AT.mult))
        vb = c.op("vector_s", lambda e: e.engine_nop(), mark=True)
        bc = c.op("tensor", lambda e: e.matmul(
            m.psA[:, 4:6], onesr, m.cc[0:1, 2:4], start=True, stop=True),
            waits=[(c.sm["vector"], vb)], mark=True)
        c.op("vector_s", lambda e, ic=invcol: e.tensor_scalar(
            m.cc[:, ic:ic + 1], m.psA[:, 4:5], 1.0, None, AT.mult),
            waits=[(c.sm["tensor"], bc)])
        bv = c.op("vector_s", lambda e: e.tensor_scalar(
            m.cc[:, 7:8], m.psA[:, 5:6], 1.0, None, AT.mult), mark=True)
        wq_dma = 0
        t1m = bv
        for i in range(nt):
            wld = c.dma("sync", sA[:, 0:cols], d[wn][i * 128:(i + 1) * 128, :],
                        waits=[(c.sm["vector"], t1m)])
            t1m = c.op("vector", lambda e, cols=cols: e.tensor_scalar(
                sB[:, 0:cols], sA[:, 0:cols], m.cc[:, 7:8], None, AT.mult),
                waits=[(c.qd["sync"], wld), (c.sm["vector"], bv)], mark=True)
            c.op("vector", lambda e, cols=cols: e.tensor_scalar(
                sC[:, 0:cols], sB[:, 0:cols], 0.5, None, AT.is_ge))
            c.op("vector", lambda e, cols=cols: e.tensor_scalar(
                sB[:, 0:cols], sB[:, 0:cols], -1.0, 0.5, AT.mult, AT.is_ge))
            c.op("vector", lambda e, cols=cols: e.tensor_tensor(
                sC[:, 0:cols], sC[:, 0:cols], sB[:, 0:cols], AT.subtract))
            cast = c.op("vector", lambda e, cols=cols: e.tensor_scalar(
                m.wqbf[:, 0:cols], sC[:, 0:cols], 1.0, None, AT.mult),
                waits=[(c.qd["scalar"], wq_dma)] if wq_dma else (), mark=True)
            wq_dma = c.dma("scalar", d[qn][i * 128:(i + 1) * 128, :],
                           m.wqbf[:, 0:cols], waits=[(c.sm["vector"], cast)])
    c.op("sync", lambda e: e.nop(), waits=[(c.qd["scalar"], c.qd["scalar"].n)])


def _ffn_scale_col(c, m, src_col, inv_col, out_col, waits=()):
    c.op("vector_s", lambda e: e.tensor_scalar(
        out_col, src_col, INV127, None, AT.mult), waits=waits)
    return c.op("vector_s", lambda e, ic=inv_col: e.tensor_tensor(
        out_col, out_col, m.cc[:, ic:ic + 1], AT.mult), mark=True)


def emit_ffn_f0(c, m):
    d = m.d
    stage = m.f_stage[:]
    pi = m.pic[:, 0:1]
    prev_dma = 0
    prev_ev = 0
    for t in range(NT):
        raw = m.X[:, t * 1024:(t + 1) * 1024]
        w0 = _bar(c) if t == 0 else [(c.sm["vector"], c.wv("vector"))]
        c.op("vector", lambda e, r=raw, o=stage: e.scalar_tensor_tensor(
            o, r, 1.0, r, AT.bypass, AT.bypass, accum_out=ccs(m, "lnS", 1)),
            waits=w0)
        c.op("vector", lambda e, r=raw, o=stage: e.scalar_tensor_tensor(
            o, r, 1.0, r, AT.bypass, AT.mult, accum_out=ccs(m, "lnQ", 1)))
        c.op("vector_s", lambda e: e.tensor_scalar(ccs(m, "lnmu", 1), ccs(m, "lnS", 1), 1.0 / D, None, AT.mult))
        c.op("vector_s", lambda e: e.tensor_scalar(ccs(m, "lnvar", 1), ccs(m, "lnQ", 1), 1.0 / D, None, AT.mult))
        c.op("vector_s", lambda e: e.tensor_tensor(ccs(m, "lnt1", 1), ccs(m, "lnmu", 1), ccs(m, "lnmu", 1), AT.mult))
        c.op("vector_s", lambda e: e.tensor_tensor(ccs(m, "lnvar", 1), ccs(m, "lnvar", 1), ccs(m, "lnt1", 1), AT.subtract))
        rm = emit_rstd(c, ccs(m, "lnvar", 1), ccs(m, "lnr", 1), ccs(m, "lnt1", 1), ccs(m, "lnt2", 1))
        vn = c.op("vector", lambda e, r=raw, o=stage: e.tensor_scalar(
            o, r, ccs(m, "lnmu", 1), ccs(m, "lnr", 1), AT.subtract, AT.mult),
            waits=[(c.sm["vector"], rm)], mark=True)
        c.op("vector_s", lambda e: e.reduce_max(
            m.cc[:, 0:1], stage, axis=mybir.AxisListType.X,
            apply_absolute_value=True))
        c.op("vector_s", lambda e: e.tensor_scalar(
            m.cc[:, 0:1], m.cc[:, 0:1], 1e-5, None, AT.max))
        c.op("vector_s", lambda e, t=t: e.tensor_scalar(
            m.rtmp[:, t:t + 1], m.cc[:, 0:1], 1.0, None, AT.mult))
        c.op("vector_s", lambda e: e.reciprocal(m.cc[:, 1:2], m.cc[:, 0:1]))
        for _ in range(2):
            c.op("vector_s", lambda e: e.tensor_tensor(
                m.cc[:, 2:3], m.cc[:, 0:1], m.cc[:, 1:2], AT.mult))
            c.op("vector_s", lambda e: e.tensor_scalar(
                m.cc[:, 2:3], m.cc[:, 2:3], -1.0, 2.0, AT.mult, AT.add))
            c.op("vector_s", lambda e: e.tensor_tensor(
                m.cc[:, 1:2], m.cc[:, 1:2], m.cc[:, 2:3], AT.mult))
        scm = c.op("vector_s", lambda e: e.tensor_scalar(
            m.cc[:, 3:4], m.cc[:, 1:2], 127.0, None, AT.mult), mark=True)
        s0 = m.f_h[:, 0:1024]
        s1 = m.f_h[:, 1024:2048]
        s2 = m.f_h[:, 2048:3072]
        s3 = m.f_h[:, 3072:4096]
        c.op("vector", lambda e: e.tensor_scalar(
            s0, stage, m.cc[:, 3:4], None, AT.mult),
            waits=[(c.sm["vector"], scm)], fresh=True)
        c.op("vector", lambda e: e.tensor_scalar(s1, s0, C52, C52, AT.add, AT.subtract))
        c.op("vector", lambda e: e.scalar_tensor_tensor(
            s2, s0, pi, s1, AT.mult, AT.subtract))
        c.op("vector", lambda e: e.tensor_scalar(s3, s0, C23, C23, AT.add, AT.subtract))
        vq = c.op("vector", lambda e: e.tensor_tensor(s3, s3, s2, AT.subtract), mark=True)
        tl = 0
        for b in range(8):
            tw = [(c.sm["vector"], vq)] if b == 0 else []
            if b == 0 and prev_ev:
                tw.append((c.sm["scalar"], prev_ev))
            tl = c.op("tensor", lambda e, b=b: e.transpose(
                m.psB[:, b * 128:(b + 1) * 128], s3[:, b * 128:(b + 1) * 128],
                m.ident[:]), waits=tw, mark=(b == 7))
        ev = c.op("scalar", lambda e: e.activation(
            m.xqT_bf[:], m.psB[:, 0:1024], AF.Copy, bias=0.0, scale=1.0),
            waits=[(c.sm["tensor"], tl)] + ([(c.qd["scalar"], prev_dma)] if prev_dma else []),
            mark=True)
        prev_ev = ev
        prev_dma = c.dma("scalar",
                         d["s_xq"].rearrange("(kc p) tk -> p kc tk", p=128)[:, :, t * 128:(t + 1) * 128],
                         m.xqT_bf[:].rearrange("p (kc tk) -> p kc tk", tk=128),
                         waits=[(c.sm["scalar"], ev)])
    c.op("sync", lambda e: e.nop(), waits=[(c.qd["scalar"], c.qd["scalar"].n)])


def _ffn_load_resident(c, m, qn, nchunk, cols):
    return c.dma("sync", m.wq_sb[:, 0:nchunk * cols].rearrange(
        "p (kc h) -> p kc h", h=cols),
        m.d[qn].rearrange("(kc p) h -> p kc h", p=128),
        waits=_bar(c))


def emit_ffn_f1(c, m):
    d = m.d
    wl = _ffn_load_resident(c, m, "s_qg", 8, 4096)
    prev_hg = 0
    prev_ev = 0
    prev_vg = 0
    for t in range(NT):
        xl = c.dma("sync", m.xqT_bf[:].rearrange("p (kc tk) -> p kc tk", tk=128),
                   d["s_xq"].rearrange("(kc p) tk -> p kc tk", p=128)[:, :, t * 128:(t + 1) * 128],
                   waits=[(c.qd["sync"], wl)] if t == 0 else [(c.sm["tensor"], c.wv("tensor"))])
        mml = 0
        for j in range(8):
            ps = m.psA[:, (j % 4) * 512:(j % 4 + 1) * 512] if j < 4 \
                else m.psB[:, (j - 4) * 512:(j - 3) * 512]
            for kc in range(8):
                tw = []
                if j == 0 and kc == 0:
                    tw = [(c.qd["sync"], xl)]
                    if prev_ev:
                        tw.append((c.sm["scalar"], prev_ev))
                    if prev_vg:
                        tw.append((c.sm["vector"], prev_vg))
                mml = c.op("tensor", lambda e, j=j, kc=kc, ps=ps: e.matmul(
                    ps, m.xqT_bf[:, kc * 128:(kc + 1) * 128],
                    m.wq_sb[:, kc * 4096 + j * 512:kc * 4096 + (j + 1) * 512],
                    start=(kc == 0), stop=(kc == 7)),
                    waits=tw, mark=(j == 7 and kc == 7))
        sccol = m.cc[:, 4 + (t % 2):5 + (t % 2)]
        nsccol = m.cc[:, 6 + (t % 2):7 + (t % 2)]
        scm = _ffn_scale_col(c, m, m.rtmp[:, t:t + 1], WGI, sccol,
                             waits=[(c.sm["scalar"], prev_ev)] if prev_ev else ())
        scm = c.op("vector_s", lambda e, sc=sccol, ns=nsccol: e.tensor_scalar(
            ns, sc, -1.0, None, AT.mult), mark=True)
        ev = 0
        for j in range(8):
            ps = m.psA[:, (j % 4) * 512:(j % 4 + 1) * 512] if j < 4 \
                else m.psB[:, (j - 4) * 512:(j - 3) * 512]
            tw = [(c.sm["tensor"], mml), (c.sm["vector"], scm)]
            if j == 0 and prev_hg:
                tw.append((c.qd["scalar"], prev_hg))
            ev = c.op("scalar", lambda e, j=j, ps=ps, ns=nsccol: e.activation(
                m.f_h[:, j * 512:(j + 1) * 512], ps, AF.Exp,
                bias=0.0, scale=ns),
                waits=tw, mark=(j == 7), fresh=True)
        vg = 0
        for half, pst in ((0, m.psA), (1, m.psB)):
            hh = m.f_h[:, half * 2048:(half + 1) * 2048]
            c.op("vector", lambda e, hh=hh: e.tensor_scalar(
                hh, hh, 1.0, None, AT.add),
                waits=[(c.sm["scalar"], ev)] if half == 0 else ())
            c.op("vector", lambda e, hh=hh: e.reciprocal(hh, hh))
            vg = c.op("vector", lambda e, hh=hh, pst=pst, sc=sccol: e.scalar_tensor_tensor(
                hh, pst[:], sc, hh, AT.mult, AT.mult),
                mark=(half == 1), fresh=True,
                waits=[(c.sm["vector"], scm)])
        prev_ev = c.wv("scalar")
        prev_vg = vg
        prev_hg = c.dma("scalar", d["s_hg"][t * 128:(t + 1) * 128, :],
                        m.f_h[:], waits=[(c.sm["vector"], vg)])
    c.op("sync", lambda e: e.nop(), waits=[(c.qd["scalar"], c.qd["scalar"].n)])


def emit_ffn_f2(c, m):
    d = m.d
    pi = m.pic[:, 0:1]
    wl = _ffn_load_resident(c, m, "s_qu", 8, 4096)
    prev_qh = 0
    prev_ev = 0
    prev_vh = 0
    for t in range(NT):
        xl = c.dma("sync", m.xqT_bf[:].rearrange("p (kc tk) -> p kc tk", tk=128),
                   d["s_xq"].rearrange("(kc p) tk -> p kc tk", p=128)[:, :, t * 128:(t + 1) * 128],
                   waits=[(c.qd["sync"], wl)] if t == 0 else [(c.sm["tensor"], c.wv("tensor"))])
        hg = c.dma("sync", m.f_h[:], d["s_hg"][t * 128:(t + 1) * 128, :],
                   waits=[(c.sm["scalar"], prev_ev)] if prev_ev else [])
        mml = 0
        for j in range(8):
            ps = m.psA[:, (j % 4) * 512:(j % 4 + 1) * 512] if j < 4 \
                else m.psB[:, (j - 4) * 512:(j - 3) * 512]
            for kc in range(8):
                tw = []
                if j == 0 and kc == 0:
                    tw = [(c.qd["sync"], xl), (c.sm["scalar"], c.wv("scalar")),
                          (c.sm["vector"], c.wv("vector"))]
                mml = c.op("tensor", lambda e, j=j, kc=kc, ps=ps: e.matmul(
                    ps, m.xqT_bf[:, kc * 128:(kc + 1) * 128],
                    m.wq_sb[:, kc * 4096 + j * 512:kc * 4096 + (j + 1) * 512],
                    start=(kc == 0), stop=(kc == 7)),
                    waits=tw, mark=(j == 7 and kc == 7))
        sccol = m.cc[:, 4 + (t % 2):5 + (t % 2)]
        scm = _ffn_scale_col(c, m, m.rtmp[:, t:t + 1], WUI, sccol)
        c.op("vector", lambda e, sc=sccol: e.scalar_tensor_tensor(
            m.f_h[:, 0:2048], m.psA[:], sc, m.f_h[:, 0:2048],
            AT.mult, AT.mult),
            waits=[(c.sm["tensor"], mml), (c.sm["vector"], scm), (c.qd["sync"], hg)],
            fresh=True)
        vh = c.op("vector", lambda e, sc=sccol: e.scalar_tensor_tensor(
            m.f_h[:, 2048:4096], m.psB[:], sc, m.f_h[:, 2048:4096],
            AT.mult, AT.mult),
            waits=[(c.sm["tensor"], mml), (c.sm["vector"], scm)],
            mark=True, fresh=True)
        c.op("vector_s", lambda e: e.reduce_max(
            m.cc[:, 0:1], m.f_h[:], axis=mybir.AxisListType.X,
            apply_absolute_value=True))
        c.op("vector_s", lambda e: e.tensor_scalar(
            m.cc[:, 0:1], m.cc[:, 0:1], 1e-5, None, AT.max))
        c.op("vector_s", lambda e, t=t: e.tensor_scalar(
            m.rtmp[:, 16 + t:17 + t], m.cc[:, 0:1], 1.0, None, AT.mult))
        c.op("vector_s", lambda e: e.reciprocal(m.cc[:, 1:2], m.cc[:, 0:1]))
        for _ in range(2):
            c.op("vector_s", lambda e: e.tensor_tensor(
                m.cc[:, 2:3], m.cc[:, 0:1], m.cc[:, 1:2], AT.mult))
            c.op("vector_s", lambda e: e.tensor_scalar(
                m.cc[:, 2:3], m.cc[:, 2:3], -1.0, 2.0, AT.mult, AT.add))
            c.op("vector_s", lambda e: e.tensor_tensor(
                m.cc[:, 1:2], m.cc[:, 1:2], m.cc[:, 2:3], AT.mult))
        s2col = m.cc[:, 6 + (t % 2):7 + (t % 2)]
        scm2 = c.op("vector_s", lambda e, sc=s2col: e.tensor_scalar(
            sc, m.cc[:, 1:2], 127.0, None, AT.mult), mark=True)
        for half, pst in ((0, m.psA), (1, m.psB)):
            hh = m.f_h[:, half * 2048:(half + 1) * 2048]
            c.op("vector", lambda e, pst=pst, hh=hh, sc=s2col: e.tensor_scalar(
                pst[:], hh, sc, None, AT.mult),
                waits=[(c.sm["vector"], scm2)], fresh=True)
            c.op("vector", lambda e, pst=pst, hh=hh: e.tensor_scalar(
                hh, pst[:], C52, C52, AT.add, AT.subtract))
            c.op("vector", lambda e, pst=pst, hh=hh: e.scalar_tensor_tensor(
                hh, pst[:], pi, hh, AT.mult, AT.subtract))
            c.op("vector", lambda e, pst=pst: e.tensor_scalar(
                pst[:], pst[:], C23, C23, AT.add, AT.subtract))
            c.op("vector", lambda e, pst=pst, hh=hh: e.tensor_tensor(
                hh, pst[:], hh, AT.subtract), mark=(half == 1))
        vq = c.wv("vector")
        ev = 0
        for b0 in range(0, 32, 8):
            tl = 0
            for b in range(b0, b0 + 8):
                tw = []
                if b == b0:
                    tw = [(c.sm["vector"], vq)] if b0 == 0 else [(c.sm["scalar"], ev)]
                tl = c.op("tensor", lambda e, b=b, b0=b0: e.transpose(
                    m.psA[:, (b - b0) * 128:(b - b0 + 1) * 128],
                    m.f_h[:, b * 128:(b + 1) * 128], m.ident[:]),
                    waits=tw, mark=(b == b0 + 7))
            ev = c.op("scalar", lambda e, b0=b0: e.activation(
                m.qhT_bf[:, b0 * 128:(b0 + 8) * 128], m.psA[:, 0:1024],
                AF.Copy, bias=0.0, scale=1.0),
                waits=[(c.sm["tensor"], tl)] + ([(c.qd["scalar"], prev_qh)] if (b0 == 0 and prev_qh) else []),
                mark=True)
        prev_vh = vq
        prev_ev = ev
        prev_qh = c.dma("scalar",
                        d["s_qhT"].rearrange("(hc p) tk -> p hc tk", p=128)[:, :, t * 128:(t + 1) * 128],
                        m.qhT_bf[:].rearrange("p (hc tk) -> p hc tk", tk=128),
                        waits=[(c.sm["scalar"], ev)])
    c.op("sync", lambda e: e.nop(), waits=[(c.qd["scalar"], c.qd["scalar"].n)])


def emit_ffn_f3(c, m, final=False):
    d = m.d
    wl = _ffn_load_resident(c, m, "s_qd", 32, 1024)
    prev_vr = 0
    prev_out = 0
    for t in range(NT):
        ql = c.dma("sync", m.qhT_bf[:].rearrange("p (hc tk) -> p hc tk", tk=128),
                   d["s_qhT"].rearrange("(hc p) tk -> p hc tk", p=128)[:, :, t * 128:(t + 1) * 128],
                   waits=[(c.qd["sync"], wl)] if t == 0 else [(c.sm["tensor"], c.wv("tensor"))])
        mml = 0
        for half in range(2):
            for ch in range(32):
                tw = []
                if half == 0 and ch == 0:
                    tw = [(c.qd["sync"], ql), (c.sm["scalar"], c.wv("scalar"))]
                mml = c.op("tensor", lambda e, ch=ch, half=half: e.matmul(
                    m.psA[:, half * 512:(half + 1) * 512],
                    m.qhT_bf[:, ch * 128:(ch + 1) * 128],
                    m.wq_sb[:, ch * 1024 + half * 512:ch * 1024 + (half + 1) * 512],
                    start=(ch == 0), stop=(ch == 31)),
                    waits=tw, mark=(half == 1 and ch == 31))
        sccol = m.cc[:, 4 + (t % 2):5 + (t % 2)]
        scm = _ffn_scale_col(c, m, m.rtmp[:, 16 + t:17 + t], WDI, sccol,
                             waits=[(c.sm["scalar"], c.wv("scalar"))])
        ev = c.op("scalar", lambda e, sc=sccol: e.activation(
            m.f_h[:, 0:1024], m.psA[:, 0:1024], AF.Copy,
            bias=0.0, scale=sc),
            waits=[(c.sm["tensor"], mml), (c.sm["vector"], scm)] +
                  ([(c.sm["vector"], prev_vr)] if prev_vr else []),
            mark=True, fresh=True)
        vr = c.op("vector", lambda e, t=t: e.tensor_tensor(
            m.X[:, t * 1024:(t + 1) * 1024], m.X[:, t * 1024:(t + 1) * 1024],
            m.f_h[:, 0:1024], AT.add),
            waits=[(c.sm["scalar"], ev)] + ([(c.qd["scalar"], prev_out)] if prev_out else []),
            mark=True)
        prev_vr = vr
        if final:
            prev_out = c.dma("scalar", d["out"][t * 128:(t + 1) * 128, :],
                             m.X[:, t * 1024:(t + 1) * 1024],
                             waits=[(c.sm["vector"], vr)])
    if final:
        c.op("sync", lambda e: e.nop(), waits=[(c.qd["scalar"], c.qd["scalar"].n)])


def emit_ffn(c, m, final=False):
    emit_ffn_f0(c, m)
    emit_ffn_f1(c, m)
    emit_ffn_f2(c, m)
    emit_ffn_f3(c, m, final=final)



def _softmax_psum(c, m, ps, outap, W, colbase, wpre, gate):
    cm = m.cc[:, colbase:colbase + 1]
    cn = m.cc[:, colbase + 1:colbase + 2]
    cz = m.cc[:, colbase + 2:colbase + 3]
    cr = m.cc[:, colbase + 3:colbase + 4]
    ct = m.cc[:, colbase + 4:colbase + 5]
    c.op("vector_s", lambda e: e.reduce_max(
        cm, ps, axis=mybir.AxisListType.X), waits=list(wpre) + list(gate))
    ngm = c.op("vector_s", lambda e: e.tensor_scalar(
        cn, cm, -1.0, None, AT.mult), mark=True)
    aex = c.op("scalar", lambda e: e.activation(
        outap, ps, AF.Exp, bias=cn, scale=1.0),
        waits=[(c.sm["vector"], ngm)], mark=True, fresh=True)
    c.op("vector", lambda e: e.tensor_scalar(
        ps, outap, 0.0, None, AT.add, AT.add, accum_out=cz),
        waits=[(c.sm["scalar"], aex)])
    c.op("vector_s", lambda e: e.reciprocal(cr, cz))
    for _ in range(2):
        c.op("vector_s", lambda e: e.tensor_tensor(ct, cz, cr, AT.mult))
        c.op("vector_s", lambda e: e.tensor_scalar(ct, ct, -1.0, 2.0, AT.mult, AT.add))
        c.op("vector_s", lambda e: e.tensor_tensor(cr, cr, ct, AT.mult))
    zm = c.op("vector_s", lambda e: e.engine_nop(), mark=True)
    vm = c.op("vector", lambda e: e.tensor_scalar(
        outap, outap, cr, None, AT.mult),
        waits=[(c.sm["vector"], zm)], mark=True, fresh=True)
    return vm


def emit_attn_b_ln_qkv(c, m):
    d = m.d
    stage = m.f_stage[:]
    prev_ev = 0
    prev_tl = 0
    for t in range(NT):
        raw = m.X[:, t * 1024:(t + 1) * 1024]
        w0 = _bar(c) if t == 0 else [(c.sm["vector"], c.wv("vector")),
                                     (c.sm["tensor"], prev_tl)]
        c.op("vector", lambda e, r=raw, o=stage: e.scalar_tensor_tensor(
            o, r, 1.0, r, AT.bypass, AT.bypass, accum_out=ccs(m, "lnS", 1)),
            waits=w0)
        c.op("vector", lambda e, r=raw, o=stage: e.scalar_tensor_tensor(
            o, r, 1.0, r, AT.bypass, AT.mult, accum_out=ccs(m, "lnQ", 1)))
        c.op("vector_s", lambda e: e.tensor_scalar(ccs(m, "lnmu", 1), ccs(m, "lnS", 1), 1.0 / D, None, AT.mult))
        c.op("vector_s", lambda e: e.tensor_scalar(ccs(m, "lnvar", 1), ccs(m, "lnQ", 1), 1.0 / D, None, AT.mult))
        c.op("vector_s", lambda e: e.tensor_tensor(ccs(m, "lnt1", 1), ccs(m, "lnmu", 1), ccs(m, "lnmu", 1), AT.mult))
        c.op("vector_s", lambda e: e.tensor_tensor(ccs(m, "lnvar", 1), ccs(m, "lnvar", 1), ccs(m, "lnt1", 1), AT.subtract))
        rm = emit_rstd(c, ccs(m, "lnvar", 1), ccs(m, "lnr", 1), ccs(m, "lnt1", 1), ccs(m, "lnt2", 1))
        vn = c.op("vector", lambda e, r=raw, o=stage: e.tensor_scalar(
            o, r, ccs(m, "lnmu", 1), ccs(m, "lnr", 1), AT.subtract, AT.mult),
            waits=[(c.sm["vector"], rm)], mark=True, fresh=True)
        tl = 0
        for b in range(8):
            tw = [(c.sm["vector"], vn)] if b == 0 else []
            if b == 0 and prev_ev:
                tw.append((c.sm["scalar"], prev_ev))
            tl = c.op("tensor", lambda e, b=b: e.transpose(
                m.psA[:, b * 128:(b + 1) * 128], stage[:, b * 128:(b + 1) * 128],
                m.ident[:]), waits=tw, mark=(b == 7))
        prev_tl = tl
        prev_ev = c.op("scalar", lambda e, t=t: e.activation(
            m.xlnT[:].rearrange("p (kc tk) -> p kc tk", tk=T)[:, :, t * 128:(t + 1) * 128],
            m.psA[:, 0:1024].rearrange("p (kc w) -> p kc w", w=128),
            AF.Copy, bias=0.0, scale=1.0),
            waits=[(c.sm["tensor"], tl)], mark=True)
    wbuf = (m.f_h[:, 0:1024], m.f_h[:, 1024:2048])
    evb = m.f_h[:, 2048:4096]
    prev_q = 0
    prev_ev2 = 0
    for ob in range(24):
        wb = wbuf[ob % 2]
        wld = c.dma("sync", wb.rearrange("p (k w) -> p k w", w=128),
                    d["w_qkv"][:, ob * 128:(ob + 1) * 128].rearrange("(k p) w -> p k w", p=128),
                    waits=([(c.sm["scalar"], prev_ev)] if ob == 0 else [])
                    + ([(c.sm["tensor"], c.wv("tensor"))] if ob >= 2 else []))
        mml = 0
        for tk in range(4):
            for kc in range(8):
                tw = []
                if tk == 0 and kc == 0:
                    tw = [(c.qd["sync"], wld)]
                    if prev_ev2:
                        tw.append((c.sm["scalar"], prev_ev2))
                mml = c.op("tensor", lambda e, tk=tk, kc=kc, wb=wb: e.matmul(
                    m.psA[:, tk * 512:(tk + 1) * 512],
                    wb[:, kc * 128:(kc + 1) * 128],
                    m.xlnT[:, kc * 2048 + tk * 512:kc * 2048 + (tk + 1) * 512],
                    start=(kc == 0), stop=(kc == 7)),
                    waits=tw, mark=(tk == 3 and kc == 7))
        ev2 = c.op("scalar", lambda e, ob=ob: e.activation(
            evb, m.psA[:], AF.Copy, bias=0.0, scale=0.125 if ob < 8 else 1.0),
            waits=[(c.sm["tensor"], mml)] + ([(c.qd["scalar"], prev_q)] if prev_q else []),
            mark=True)
        prev_ev2 = ev2
        prev_q = c.dma("scalar", d["s_qkvb"][ob * 128:(ob + 1) * 128, :], evb,
                       waits=[(c.sm["scalar"], ev2)])
    c.op("sync", lambda e: e.nop(), waits=[(c.qd["scalar"], c.qd["scalar"].n)])


def _diag_add(c, m, buf, r, val):
    sub = buf[:, r * 256 + r * 128:r * 256 + (r + 1) * 128]
    c.op("vector_s", lambda e, sub=sub, val=val: e.scalar_tensor_tensor(
        sub, m.ident[:], float(val), sub, AT.mult, AT.add), mark=True)


def _mp_prod(c, m, lhsTbuf, rhsbuf, outbuf, neg_into=None, diag=None, scale=None):
    ev = 0
    for ib in range(2):
        mm = 0
        for kb in range(2):
            mm = c.op("tensor", lambda e, ib=ib, kb=kb: e.matmul(
                m.psB[:, ib * 512:ib * 512 + 256],
                lhsTbuf[:, (kb * 2 + ib) * 128:(kb * 2 + ib + 1) * 128],
                rhsbuf[:, kb * 256:(kb + 1) * 256],
                start=(kb == 0), stop=(kb == 1)),
                waits=[(c.sm["scalar"], c.wv("scalar")),
                       (c.sm["vector"], c.wv("vector"))] if (ib == 0 and kb == 0) else (),
                mark=(kb == 1))
        ev = c.op("scalar", lambda e, ib=ib, sc=scale: e.activation(
            outbuf[:, ib * 256:(ib + 1) * 256], m.psB[:, ib * 512:ib * 512 + 256],
            AF.Copy, bias=0.0, scale=1.0 if sc is None else sc),
            waits=[(c.sm["tensor"], mm)], mark=True)
    if neg_into is not None:
        c.op("vector_s", lambda e: e.tensor_scalar(
            neg_into[:, 0:512], outbuf[:, 0:512], -1.0, None, AT.mult),
            waits=[(c.sm["scalar"], ev)])
        for r in range(2):
            _diag_add(c, m, neg_into, r, diag)
    return ev


def _transpose_blocks(c, m, src, dst, wpre):
    for ib in range(2):
        for kb in range(2):
            tt = c.op("tensor", lambda e, ib=ib, kb=kb: e.transpose(
                m.psB[:, 1792:1920], src[:, ib * 256 + kb * 128:ib * 256 + (kb + 1) * 128],
                m.ident[:]),
                waits=list(wpre) if (ib == 0 and kb == 0) else [(c.sm["scalar"], c.wv("scalar"))],
                mark=True)
            c.op("scalar", lambda e, ib=ib, kb=kb: e.activation(
                dst[:, (kb * 2 + ib) * 128:(kb * 2 + ib + 1) * 128],
                m.psB[:, 1792:1920], AF.Copy, bias=0.0, scale=1.0),
                waits=[(c.sm["tensor"], tt)], mark=True)


def emit_attn_b_premax(c, m, ncores=8):
    d = m.d
    bar0 = [(c.sm[e], c.wv(e)) for e in ("vector", "scalar", "gpsimd", "tensor")]
    bar0.append((c.qd["scalar"], c.qd["scalar"].n))
    c.op("vector_s", lambda e: e.memset(m.cc[:, 23:25], 0.0), waits=bar0)
    c.op("vector_s", lambda e: e.memset(m.msc[:, 256:257], 1.0), mark=True)
    for h in range(16):
        wq = c.dma("sync", m.bq[:], d["s_qkvb"][h * 64:(h + 1) * 64, :],
                   waits=[(c.sm[e], c.wv(e)) for e in ("vector", "scalar", "tensor")])
        wk = c.dma("sync", m.bk[:], d["s_qkvb"][1024 + h * 64:1024 + (h + 1) * 64, :])
        plm = 0
        for src_, dst, wdma in ((m.bq, m.bql, wq), (m.bk, m.bkl, wk)):
            v3 = src_[:].rearrange("p (mm w) -> p mm w", w=8)
            d3 = dst[:].rearrange("p (mm w) -> p mm w", w=1)
            c.op("vector_s", lambda e, v3=v3, d3=d3: e.tensor_tensor(
                d3, v3[:, :, 0:1], v3[:, :, 1:2], AT.add),
                waits=[(c.qd["sync"], wdma)])
            for i in range(2, 8):
                c.op("vector_s", lambda e, v3=v3, d3=d3, i=i: e.tensor_tensor(
                    d3, d3, v3[:, :, i:i + 1], AT.add))
            plm = c.op("vector_s", lambda e, dst=dst: e.tensor_scalar(
                dst[:], dst[:], 0.125, None, AT.mult), mark=True)
        for lmb in range(2):
            mm = c.op("tensor", lambda e, lmb=lmb: e.matmul(
                m.psA[:, 0:256], m.bql[:, lmb * 128:(lmb + 1) * 128], m.bkl[:],
                start=True, stop=True),
                waits=[(c.sm["vector"], plm) if lmb == 0 else
                       (c.sm["vector"], c.wv("vector")),
                       (c.sm["scalar"], c.wv("scalar"))], mark=True)
            _softmax_psum(c, m, m.psA[:, 0:256], m.b_a2[:, lmb * 256:(lmb + 1) * 256],
                          256, 8, [(c.sm["tensor"], mm)],
                          [(c.sm["scalar"], c.wv("scalar"))])
        c.op("vector_s", lambda e: e.reduce_sum(
            m.cc[:, 16:17], m.b_a2[:, 0:256], axis=mybir.AxisListType.X))
        c.op("vector_s", lambda e: e.reduce_sum(
            m.cc[:, 17:18], m.b_a2[:, 256:512], axis=mybir.AxisListType.X))
        c.op("vector_s", lambda e: e.tensor_tensor(
            m.cc[:, 16:17], m.cc[:, 16:17], m.cc[:, 17:18], AT.max))
        c.op("vector_s", lambda e: e.tensor_tensor(
            m.cc[:, 23:24], m.cc[:, 23:24], m.cc[:, 16:17], AT.max), mark=True)
        mm = c.op("tensor", lambda e: e.matmul(
            m.psB[0:1, 1792:2048], m.msc[:, 256:257], m.b_a2[:, 0:256],
            start=True, stop=False),
            waits=[(c.sm["vector"], c.wv("vector"))], mark=True)
        mm = c.op("tensor", lambda e: e.matmul(
            m.psB[0:1, 1792:2048], m.msc[:, 256:257], m.b_a2[:, 256:512],
            start=False, stop=True), mark=True)
        csum = c.op("scalar", lambda e: e.activation(
            m.msc[0:1, 768:1024], m.psB[0:1, 1792:2048], AF.Copy, bias=0.0, scale=1.0),
            waits=[(c.sm["tensor"], mm)], mark=True)
        c.op("vector_s", lambda e: e.reduce_max(
            m.cc[0:1, 16:17], m.msc[0:1, 768:1024], axis=mybir.AxisListType.X),
            waits=[(c.sm["scalar"], csum)])
        c.op("vector_s", lambda e: e.tensor_tensor(
            m.cc[0:1, 24:25], m.cc[0:1, 24:25], m.cc[0:1, 16:17], AT.max),
            mark=True)
    c.op("vector_s", lambda e: e.memset(m.msc[:, 0:128], 0.0))
    cpm = c.op("vector_s", lambda e: e.tensor_scalar(
        m.msc[:, 0:1], m.cc[:, 23:24], 1.0, None, AT.mult), mark=True)
    tt = c.op("tensor", lambda e: e.transpose(
        m.psB[:, 1792:1920], m.msc[:, 0:128], m.ident[:]),
        waits=[(c.sm["vector"], cpm)], mark=True)
    cev = c.op("scalar", lambda e: e.activation(
        m.msc[:, 128:256], m.psB[:, 1792:1920], AF.Copy, bias=0.0, scale=1.0),
        waits=[(c.sm["tensor"], tt)], mark=True)
    c.op("vector_s", lambda e: e.reduce_max(
        m.cc[0:1, 23:24], m.msc[0:1, 128:256], axis=mybir.AxisListType.X),
        waits=[(c.sm["scalar"], cev)])
    c.op("vector_s", lambda e: e.tensor_scalar(
        m.rtmp[0:1, 240:241], m.cc[0:1, 23:24], 1.0, None, AT.mult))
    vmk = c.op("vector_s", lambda e: e.tensor_scalar(
        m.rtmp[0:1, 241:242], m.cc[0:1, 24:25], 1.0, None, AT.mult), mark=True)
    if ncores > 1:
        gin = c.dma("gpsimd", d["s_gmax"], m.rtmp[0:1, 240:242],
                    waits=[(c.sm["vector"], vmk)])
        ccm = c.op("gpsimd", lambda e: e.collective_compute(
            "AllReduce", AT.max, replica_groups=[list(range(8))],
            ins=[d["s_gmax"]], outs=[d["s_gmax2"]]),
            waits=[(c.qd["gpsimd"], gin)], mark=True, nohz=True)
        gl = c.dma("sync", m.rtmp[0:1, 248:250], d["s_gmax2"],
                   waits=[(c.sm["gpsimd"], ccm)])
        gm = c.op("vector_s", lambda e: e.engine_nop(),
                  waits=[(c.qd["sync"], gl)], mark=True)
    else:
        gm = c.op("vector_s", lambda e: e.tensor_scalar(
            m.rtmp[0:1, 248:250], m.rtmp[0:1, 240:242], 1.0, None, AT.mult),
            waits=[(c.sm["vector"], vmk)], mark=True)
    c.op("vector_s", lambda e: e.tensor_tensor(
        m.cc[0:1, 20:21], m.rtmp[0:1, 248:249], m.rtmp[0:1, 249:250], AT.mult))
    c.op("vector_s", lambda e: e.reciprocal(m.cc[0:1, 21:22], m.cc[0:1, 20:21]))
    for _ in range(2):
        c.op("vector_s", lambda e: e.tensor_tensor(
            m.cc[0:1, 22:23], m.cc[0:1, 20:21], m.cc[0:1, 21:22], AT.mult))
        c.op("vector_s", lambda e: e.tensor_scalar(
            m.cc[0:1, 22:23], m.cc[0:1, 22:23], -1.0, 2.0, AT.mult, AT.add))
        c.op("vector_s", lambda e: e.tensor_tensor(
            m.cc[0:1, 21:22], m.cc[0:1, 21:22], m.cc[0:1, 22:23], AT.mult))
    c.op("vector_s", lambda e: e.memset(m.msc[0:1, 512:640], 1.0), mark=True)
    mm = c.op("tensor", lambda e: e.matmul(
        m.psB[:, 1792:1793], m.msc[0:1, 512:640], m.cc[0:1, 21:22],
        start=True, stop=True),
        waits=[(c.sm["vector"], c.wv("vector"))], mark=True)
    c.op("vector_s", lambda e: e.tensor_scalar(
        m.cc[:, 107:108], m.psB[:, 1792:1793], 1.0, None, AT.mult),
        waits=[(c.sm["tensor"], mm)], mark=True)


def emit_attn_b_head(c, m, h):
    d = m.d
    CB = 8
    bar0 = [(c.sm[e], c.wv(e)) for e in ("vector", "scalar", "gpsimd", "tensor")]
    bar0.append((c.qd["scalar"], c.qd["scalar"].n))
    wq = c.dma("sync", m.bq[:], d["s_qkvb"][h * 64:(h + 1) * 64, :], waits=bar0)
    wk = c.dma("sync", m.bk[:], d["s_qkvb"][1024 + h * 64:1024 + (h + 1) * 64, :])
    wv_ = c.dma("sync", m.bv[:], d["s_qkvb"][2048 + h * 64:2048 + (h + 1) * 64, :])
    wc = c.dma("sync", m.wcv[:, 0:13], d["w_conv"][h * 64:(h + 1) * 64, :])
    plm = 0
    for src_, dst, wdma in ((m.bq, m.bql, wq), (m.bk, m.bkl, wk)):
        v3 = src_[:].rearrange("p (mm w) -> p mm w", w=8)
        d3 = dst[:].rearrange("p (mm w) -> p mm w", w=1)
        c.op("vector_s", lambda e, v3=v3, d3=d3: e.tensor_tensor(
            d3, v3[:, :, 0:1], v3[:, :, 1:2], AT.add),
            waits=[(c.qd["sync"], wdma)])
        for i in range(2, 8):
            c.op("vector_s", lambda e, v3=v3, d3=d3, i=i: e.tensor_tensor(
                d3, d3, v3[:, :, i:i + 1], AT.add))
        plm = c.op("vector_s", lambda e, dst=dst: e.tensor_scalar(
            dst[:], dst[:], 0.125, None, AT.mult), mark=True)
    prev_sc = 0
    for t in range(NT):
        mm = c.op("tensor", lambda e, t=t: e.matmul(
            m.psA[:, 0:256], m.bq[:, t * 128:(t + 1) * 128], m.bkl[:],
            start=True, stop=True),
            waits=[(c.sm["vector"], plm), (c.sm["scalar"], c.wv("scalar"))] if t == 0
            else [(c.sm["scalar"], c.wv("scalar")), (c.sm["vector"], c.wv("vector"))],
            mark=True)
        a1t = m.f_h[:, t * 256:(t + 1) * 256]
        vm = _softmax_psum(c, m, m.psA[:, 0:256], a1t, 256, CB,
                           [(c.sm["tensor"], mm)],
                           [(c.sm["scalar"], prev_sc)] if prev_sc else [])
        for lmb in range(2):
            tt = c.op("tensor", lambda e, t=t, lmb=lmb, a1t=a1t: e.transpose(
                m.psB[:, 1792:1920], a1t[:, lmb * 128:(lmb + 1) * 128], m.ident[:]),
                waits=[(c.sm["vector"], vm)] if lmb == 0 else [(c.sm["scalar"], c.wv("scalar"))],
                mark=True)
            prev_sc = c.op("scalar", lambda e, t=t, lmb=lmb: e.activation(
                m.a1T[:, lmb * 2048 + t * 128:lmb * 2048 + (t + 1) * 128],
                m.psB[:, 1792:1920], AF.Copy, bias=0.0, scale=1.0),
                waits=[(c.sm["tensor"], tt)], mark=True)
    for lmb in range(2):
        mm = c.op("tensor", lambda e, lmb=lmb: e.matmul(
            m.psA[:, 0:256], m.bql[:, lmb * 128:(lmb + 1) * 128], m.bkl[:],
            start=True, stop=True),
            waits=[(c.sm["scalar"], c.wv("scalar")),
                   (c.sm["vector"], c.wv("vector"))], mark=True)
        _softmax_psum(c, m, m.psA[:, 0:256], m.b_a2[:, lmb * 256:(lmb + 1) * 256],
                      256, CB, [(c.sm["tensor"], mm)],
                      [(c.sm["scalar"], c.wv("scalar"))])
    for lmb in range(2):
        mm = 0
        for tk in range(4):
            mm = c.op("tensor", lambda e, lmb=lmb, tk=tk: e.matmul(
                m.psA[:, tk * 512:(tk + 1) * 512],
                m.bql[:, lmb * 128:(lmb + 1) * 128],
                m.bk[:, tk * 512:(tk + 1) * 512], start=True, stop=True),
                waits=[(c.sm["vector"], c.wv("vector")),
                       (c.sm["scalar"], c.wv("scalar"))] if tk == 0 else (),
                mark=(tk == 3))
        vm = _softmax_psum(c, m, m.psA[:], m.bs[:, 0:T], T, CB,
                           [(c.sm["tensor"], mm)],
                           [(c.sm["scalar"], c.wv("scalar"))])
        for t in range(NT):
            tt = c.op("tensor", lambda e, t=t: e.transpose(
                m.psB[:, 1792:1920], m.bs[:, t * 128:(t + 1) * 128], m.ident[:]),
                waits=[(c.sm["vector"], vm)] if t == 0 else [(c.sm["scalar"], c.wv("scalar"))],
                mark=True)
            c.op("scalar", lambda e, t=t, lmb=lmb: e.activation(
                m.a3T[:, t * 256 + lmb * 128:t * 256 + (lmb + 1) * 128],
                m.psB[:, 1792:1920], AF.Copy, bias=0.0, scale=1.0),
                waits=[(c.sm["tensor"], tt)], mark=True)
    for t in range(NT):
        tt = c.op("tensor", lambda e, t=t: e.transpose(
            m.psB[:, 1792:1856], m.bv[:, t * 128:(t + 1) * 128], m.ident[0:64, 0:64]),
            waits=[(c.qd["sync"], wv_), (c.sm["scalar"], c.wv("scalar"))] if t == 0
            else [(c.sm["scalar"], c.wv("scalar"))],
            mark=True)
        c.op("scalar", lambda e, t=t: e.activation(
            m.bs[:, t * 64:(t + 1) * 64], m.psB[:, 1792:1856],
            AF.Copy, bias=0.0, scale=1.0),
            waits=[(c.sm["tensor"], tt)], mark=True)
    for lmb in range(2):
        mm = 0
        for t in range(NT):
            mm = c.op("tensor", lambda e, t=t, lmb=lmb: e.matmul(
                m.psA[:, 0:64],
                m.a3T[:, t * 256 + lmb * 128:t * 256 + (lmb + 1) * 128],
                m.bs[:, t * 64:(t + 1) * 64],
                start=(t == 0), stop=(t == NT - 1)),
                waits=[(c.sm["scalar"], c.wv("scalar"))] if t == 0 else (),
                mark=(t == NT - 1))
        c.op("scalar", lambda e, lmb=lmb: e.activation(
            m.bY[:, lmb * 64:(lmb + 1) * 64], m.psA[:, 0:64],
            AF.Copy, bias=0.0, scale=1.0),
            waits=[(c.sm["tensor"], mm)], mark=True)
    s0m = c.op("vector_s", lambda e: e.engine_nop(), mark=True)
    _transpose_blocks(c, m, m.b_a2, m.b_aT, [(c.sm["vector"], s0m)])
    zin = c.op("vector", lambda e: e.tensor_scalar(
        m.b_z[:], m.b_aT[:], m.cc[:, 107:108], None, AT.mult),
        waits=[(c.sm["scalar"], c.wv("scalar")), (c.sm["vector"], s0m)],
        mark=True, fresh=True)
    for it in range(6):
        _mp_prod(c, m, m.b_aT, m.b_z, m.b_az)
        c.op("vector_s", lambda e: e.tensor_scalar(
            m.b_t[:], m.b_az[:], -1.0, None, AT.mult),
            waits=[(c.sm["scalar"], c.wv("scalar"))])
        for r in range(2):
            _diag_add(c, m, m.b_t, r, 7.0)
        _transpose_blocks(c, m, m.b_az, m.b_azT, [(c.sm["scalar"], c.wv("scalar"))])
        _mp_prod(c, m, m.b_azT, m.b_t, m.b_t, neg_into=m.b_t, diag=15.0)
        _mp_prod(c, m, m.b_azT, m.b_t, m.b_t, neg_into=m.b_t, diag=13.0)
        _transpose_blocks(c, m, m.b_z, m.b_zT, [(c.sm["vector"], c.wv("vector"))])
        _mp_prod(c, m, m.b_zT, m.b_t, m.b_z, scale=0.25)
    _transpose_blocks(c, m, m.b_z, m.b_zT, [(c.sm["scalar"], c.wv("scalar"))])
    for ib in range(2):
        mm = 0
        for kb in range(2):
            mm = c.op("tensor", lambda e, ib=ib, kb=kb: e.matmul(
                m.psA[:, 64:128],
                m.b_zT[:, (kb * 2 + ib) * 128:(kb * 2 + ib + 1) * 128],
                m.bY[:, kb * 64:(kb + 1) * 64],
                start=(kb == 0), stop=(kb == 1)),
                waits=[(c.sm["scalar"], c.wv("scalar"))] if kb == 0 else (),
                mark=(kb == 1))
        c.op("scalar", lambda e, ib=ib: e.activation(
            m.bZ2[:, ib * 64:(ib + 1) * 64], m.psA[:, 64:128],
            AF.Copy, bias=0.0, scale=1.0),
            waits=[(c.sm["tensor"], mm)], mark=True)
    mm = 0
    for tk in range(4):
        for lmb in range(2):
            mm = c.op("tensor", lambda e, tk=tk, lmb=lmb: e.matmul(
                m.psA[0:64, tk * 512:(tk + 1) * 512],
                m.bZ2[:, lmb * 64:(lmb + 1) * 64],
                m.a1T[:, lmb * 2048 + tk * 512:lmb * 2048 + (tk + 1) * 512],
                start=(lmb == 0), stop=(lmb == 1)),
                waits=[(c.sm["scalar"], c.wv("scalar"))] if (tk == 0 and lmb == 0) else (),
                mark=(tk == 3 and lmb == 1))
    bo = m.bs[0:64, 0:T]
    ev = c.op("scalar", lambda e: e.activation(
        bo, m.psA[0:64, 0:T], AF.Copy, bias=0.0, scale=1.0),
        waits=[(c.sm["tensor"], mm)], mark=True)
    cw = c.op("vector", lambda e: e.tensor_scalar(
        m.psB[0:64, 0:T], bo, m.wcv[:, 6:7], None, AT.mult),
        waits=[(c.sm["scalar"], ev), (c.qd["sync"], wc)], mark=True, fresh=True)
    for tap in range(13):
        if tap == 6:
            continue
        dd = tap - 6
        lo = max(0, -dd)
        hi = T - max(0, dd)
        cw = c.op("vector", lambda e, tap=tap, dd=dd, lo=lo, hi=hi: e.scalar_tensor_tensor(
            m.psB[0:64, lo:hi], bo[:, lo + dd:hi + dd], m.wcv[:, tap:tap + 1],
            m.psB[0:64, lo:hi], AT.mult, AT.add),
            mark=True, fresh=True, waits=[(c.sm["vector"], cw)])
    vo = c.op("vector", lambda e: e.tensor_tensor(
        bo, bo, m.psB[0:64, 0:T], AT.add),
        waits=[(c.sm["vector"], cw)], mark=True)
    return c.dma("scalar", d["s_outb"][h * 64:(h + 1) * 64, :], bo,
                 waits=[(c.sm["vector"], vo)])


def emit_attn_b_fc(c, m):
    d = m.d
    wl = c.dma("sync", m.W64[:, 0:8192].rearrange("p (kc w) -> p kc w", w=1024),
               d["w_fc"].rearrange("(kc p) w -> p kc w", p=128),
               waits=_bar(c))
    bl = c.dma("sync", m.bfc[:], d["b_fc"])
    prev_vr = 0
    for t in range(NT):
        ol = c.dma("sync", m.bs[:, 0:1024].rearrange("p (kc w) -> p kc w", w=128),
                   d["s_outb"].rearrange("(kc p) tk -> p kc tk", p=128)[:, :, t * 128:(t + 1) * 128],
                   waits=[(c.qd["sync"], bl)] if t == 0
                   else [(c.sm["tensor"], c.wv("tensor"))])
        mml = 0
        for half in range(2):
            for kc in range(8):
                tw = []
                if half == 0 and kc == 0:
                    tw = [(c.qd["sync"], ol)]
                    if prev_vr:
                        tw.append((c.sm["scalar"], c.wv("scalar")))
                mml = c.op("tensor", lambda e, half=half, kc=kc: e.matmul(
                    m.psA[:, half * 512:(half + 1) * 512],
                    m.bs[:, kc * 128:(kc + 1) * 128],
                    m.W64[:, kc * 1024 + half * 512:kc * 1024 + (half + 1) * 512],
                    start=(kc == 0), stop=(kc == 7)),
                    waits=tw, mark=(half == 1 and kc == 7))
        ev = c.op("scalar", lambda e: e.activation(
            m.f_h[:, 0:1024], m.psA[:, 0:1024], AF.Copy, bias=0.0, scale=1.0),
            waits=[(c.sm["tensor"], mml)] + ([(c.sm["vector"], prev_vr)] if prev_vr else []),
            mark=True)
        c.op("vector", lambda e: e.tensor_tensor(
            m.f_h[:, 0:1024], m.f_h[:, 0:1024], m.bfc[:], AT.add),
            waits=[(c.sm["scalar"], ev)])
        prev_vr = c.op("vector", lambda e, t=t: e.tensor_tensor(
            m.X[:, t * 1024:(t + 1) * 1024], m.X[:, t * 1024:(t + 1) * 1024],
            m.f_h[:, 0:1024], AT.add), mark=True)


def emit_attn_b(c, m, ncores=8):
    emit_attn_b_ln_qkv(c, m)
    emit_attn_b_premax(c, m, ncores=ncores)
    for h in range(16):
        emit_attn_b_head(c, m, h)
    c.op("sync", lambda e: e.nop(), waits=[(c.qd["scalar"], c.qd["scalar"].n)])
    emit_attn_b_fc(c, m)



def emit_setup(c, m):
    d = m.d
    w = 0
    for dst, src in [(m.ident[:], d["c_ident"]), (m.cosT[:], d["c_cos"]),
                     (m.sinT[:], d["c_sin"]), (m.wr8[:], d["c_wr8"]),
                     (m.wi8[:], d["c_wi8"]), (m.zst[:], d["c_zst"]),
                     (m.iota16[:], d["c_iota16"]), (m.tri[:], d["c_tri"])]:
        w = c.dma("sync", dst, src)
    wm = 0
    for b in range(32):
        ps = m.psA if b < 16 else m.psB
        bb = b % 16
        wm = c.op("tensor", lambda e, ps=ps, bb=bb: e.transpose(
            ps[:, bb * 128:(bb + 1) * 128], m.ident[:], m.ident[:]),
            waits=[(c.qd["sync"], w)] if b == 0 else (),
            mark=(b == 31))
    c.op("scalar", lambda e: e.activation(
        m.A2[:, 0:2048], m.psA[:], AF.Copy, bias=0.0, scale=1.0),
        waits=[(c.sm["tensor"], wm)])
    c.op("scalar", lambda e: e.activation(
        m.A2[:, 0:2048], m.psB[:], AF.Copy, bias=0.0, scale=1.0), mark=True)
    c.op("vector_s", lambda e: e.memset(m.pic[:], 0.0), waits=[(c.qd["sync"], w)])
    c.op("vector_s", lambda e: e.memset(m.cc[:], 0.0))
    c.op("vector_s", lambda e: e.memset(m.pic[:, 2:3], 1.0))
    c.op("vector_s", lambda e: e.tensor_scalar(m.pic[:, 0:1], m.pic[:, 2:3], C52, C52, AT.add, AT.subtract))
    c.op("vector_s", lambda e: e.tensor_scalar(m.pic[:, 1:2], m.pic[:, 0:1], -1.0, None, AT.mult))
    c.op("vector_s", lambda e: e.memset(m.pic[:, 3:4], -4.0), mark=True)


def build(stage="full", ncores=8):
    c = Ctx(stage)
    m = Mem(c)
    emit_setup(c, m)
    d = m.d
    if stage in ("wq", "ffn_only", "attn_b_only") or stage.startswith("headb"):
        wx = 0
        for t in range(NT):
            wx = c.dma("sync", m.X[:, t * 1024:(t + 1) * 1024],
                       d["x"][t * 128:(t + 1) * 128, :],
                       waits=_bar(c, dma=False) if t == 0 else ())
        c.op("vector_s", lambda e: e.engine_nop(),
             waits=[(c.qd["sync"], wx)], mark=True)
        if stage.startswith("headb"):
            hh = int(stage[5:])
            emit_attn_b_ln_qkv(c, m)
            for h2 in range(hh + 1):
                emit_attn_b_head(c, m, h2)
            w = _bar(c)
            c.dma("sync", d["dbg"][:, 0:512], m.b_a2[:], waits=w)
            c.dma("sync", d["dbg"][:, 512:1024], m.b_z[:])
            c.dma("sync", d["dbg"][:, 1024:1152], m.bY[:])
            c.dma("sync", d["dbg"][:, 1152:1280], m.bZ2[:])
            c.dma("sync", d["dbg"][0:64, 1280:1536], m.bql[:])
            c.dma("sync", d["dbg"][0:64, 1536:1792], m.bkl[:])
            c.dma("sync", d["dbg"][:, 2048:6144], m.a1T[:])
            fin = c.dma("sync", d["dbg"][:, 6144:7168], m.f_h[:, 0:1024])
            c.op("sync", lambda e: e.nop(), waits=[(c.qd["sync"], fin)])
            return c, m
        if stage == "attn_b_only":
            emit_attn_b(c, m, ncores=ncores)
            prev = 0
            for t in range(NT):
                prev = c.dma("scalar", d["out"][t * 128:(t + 1) * 128, :],
                             m.X[:, t * 1024:(t + 1) * 1024],
                             waits=_bar(c, dma=False) if t == 0 else ())
            c.op("sync", lambda e: e.nop(), waits=[(c.qd["scalar"], prev)])
            return c, m
        emit_wquant(c, m)
        if stage == "wq":
            fin = c.dma("sync", d["dbg"][:, 4096:4204], m.cc[:],
                        waits=_bar(c))
            c.op("sync", lambda e: e.nop(), waits=[(c.qd["sync"], fin)])
        else:
            emit_ffn(c, m, final=True)
        return c, m
    emit_attn_a(c, m)
    if stage not in ("full", "ffn1"):
        if stage != "attn_a":
            w = [(c.sm[e], c.wv(e)) for e in ("vector", "scalar", "gpsimd", "tensor")]
            c.dma("sync", m.d["dbg"][:, 0:2048], m.st[0][:], waits=w)
            c.dma("sync", m.d["dbg"][:, 2048:4096], m.st[1][:])
            c.dma("sync", m.d["dbg"][:, 4096:4204], m.cc[:])
            fin = c.dma("sync", m.d["dbg"][:, 6144:8192], m.Eb[:])
            c.op("sync", lambda e: e.nop(), waits=[(c.qd["sync"], fin)])
        return c, m
    emit_wquant(c, m)
    emit_ffn(c, m, final=(stage == "ffn1"))
    if stage == "ffn1":
        return c, m
    emit_attn_b(c, m, ncores=ncores)
    if stage == "attn_b":
        prev = 0
        for t in range(NT):
            prev = c.dma("scalar", d["out"][t * 128:(t + 1) * 128, :],
                         m.X[:, t * 1024:(t + 1) * 1024],
                         waits=_bar(c, dma=False) if t == 0 else ())
        c.op("sync", lambda e: e.nop(), waits=[(c.qd["scalar"], prev)])
        return c, m
    emit_ffn(c, m, final=True)
    return c, m


def kernel(**inputs):
    consts = host_constants()
    wts = host_weights(inputs)
    c, m = build("full")
    nc = c.materialize()
    in_maps = []
    for b in range(8):
        im = {"x": np.ascontiguousarray(inputs["L0_attn_a_x"][b])}
        im.update(wts)
        im.update(consts)
        in_maps.append(im)
    res = run_bass_kernel_spmd(nc, in_maps, core_ids=list(range(8)))
    return np.stack([r["out"] for r in res.results]).astype(np.float32)
